# version 04
# baseline (speedup 1.0000x reference)
import contextlib

import numpy as np
import ml_dtypes

import concourse.bass as bass
import concourse.mybir as mybir
from concourse.bass_utils import run_bass_kernel_spmd

F32 = mybir.dt.float32
BF16 = mybir.dt.bfloat16
U8 = mybir.dt.uint8
AF = mybir.ActivationFunctionType
ALU = mybir.AluOpType
AX = mybir.AxisListType
bfdt = ml_dtypes.bfloat16

T = 2048
D = 2048
NH = 64
HP = 64
NS = 128
NG = 8
INTER = 4096
PROJ = 10304
CS = 128
NCH = T // CS
NE = 32
LAT = 1024
SHI = 4096
EPS = 1e-5

GATE0 = 0
HBC0 = 4096
DT0 = 10240

_PHASES = 7
_NCH_LIM = None
_P4_LVL = 99
_P4_TREV = 3


class SemC:
    __slots__ = ("h", "n", "last_step")

    def __init__(self, h):
        self.h = h
        self.n = 0
        self.last_step = None


class Step:
    __slots__ = ("engine", "emit", "deps", "needs_event", "sem", "val",
                 "is_dma", "tag")

    def __init__(self, engine, emit, deps, is_dma, tag):
        self.engine = engine
        self.emit = emit
        assert all(d is not None for d in deps), (tag, 'None dep')
        self.deps = list(deps)
        self.needs_event = False
        self.sem = None
        self.val = None
        self.is_dma = is_dma
        self.tag = tag


class Sched:
    ENGINES = ("sync", "gpsimd", "vector", "scalar", "tensor")

    def __init__(self, nc, esems, dsems, done):
        self.nc = nc
        self.esems = esems
        self.dsems = dsems
        self.done = done
        self.steps = {e: [] for e in self.ENGINES}

    def step(self, engine, emit, deps=(), tag=""):
        st = Step(engine, emit, deps, False, tag)
        self.steps[engine].append(st)
        return st

    def dma(self, engine, emit, deps=(), stream=0, tag=""):
        st = Step(engine, emit, deps, True, tag)
        st.sem = self.dsems[stream]
        if st.sem.last_step is not None:
            st.deps.append(st.sem.last_step)
        st.sem.last_step = st
        self.steps[engine].append(st)
        return st

    def emit(self, blk):
        for e in self.ENGINES:
            for st in self.steps[e]:
                for d in st.deps:
                    d.needs_event = True
        for e in self.ENGINES:
            for st in self.steps[e]:
                if st.is_dma:
                    st.val = st.sem.n + 16
                    st.sem.n = st.val
                elif st.needs_event:
                    st.sem = self.esems[e]
                    st.val = st.sem.n + 1
                    st.sem.n = st.val
        self._simulate()

        def body_for(engine):
            steps = self.steps[engine]

            def body(eng):
                waited = {}
                for st in steps:
                    for d in st.deps:
                        assert d.sem is not None, (st.tag, d.tag)
                        key = id(d.sem)
                        if waited.get(key, -1) < d.val:
                            eng.wait_ge(d.sem.h, d.val)
                            waited[key] = d.val
                    inst = st.emit(eng)
                    if st.is_dma:
                        inst.then_inc(st.sem.h, 16)
                    elif st.needs_event:
                        inst.then_inc(st.sem.h, 1)
            return body

        for engine in self.ENGINES:
            if self.steps[engine]:
                getattr(blk, engine)(body_for(engine))
        for e in self.ENGINES:
            for st in self.steps[e]:
                self.done.add(id(st))

    def _simulate(self):
        local_done = set(self.done)
        ptr = {e: 0 for e in self.ENGINES}
        progressed = True
        while progressed:
            progressed = False
            for e in self.ENGINES:
                while ptr[e] < len(self.steps[e]):
                    st = self.steps[e][ptr[e]]
                    if any(id(d) not in local_done for d in st.deps):
                        break
                    local_done.add(id(st))
                    ptr[e] += 1
                    progressed = True
        stuck = [(e, self.steps[e][ptr[e]].tag)
                 for e in self.ENGINES if ptr[e] < len(self.steps[e])]
        if stuck:
            raise RuntimeError(f"schedule deadlock at: {stuck}")


def _build(dbg=frozenset(), phases=None):
    if phases is None:
        phases = _PHASES
    nc = bass.Bass("TRN2", target_bir_lowering=False,
                   detect_race_conditions=False)
    ctx = contextlib.ExitStack()

    def dram_in(name, shape, dt=F32):
        return nc.dram_tensor(name, shape, dt, kind="ExternalInput").ap()

    def scratch(name, shape, dt):
        kind = "ExternalOutput" if name in dbg else "Internal"
        return nc.dram_tensor(name, shape, dt, kind=kind).ap()

    x_in = dram_in("x", [T, D])
    winT = dram_in("winT", [D, PROJ])
    woutT = dram_in("woutT", [INTER, D])
    gw_pk = dram_in("gw_pk", [128, 512])
    fc1T = dram_in("fc1T", [D, LAT])
    fc2T = dram_in("fc2T", [LAT, D])
    w1T = dram_in("w1T", [NE, LAT, LAT])
    w2T = dram_in("w2T", [NE, LAT, LAT])
    supT = dram_in("supT", [D, SHI])
    sdnT = dram_in("sdnT", [SHI, D])
    convw_p = dram_in("convw_p", [128, 192])
    convb_p = dram_in("convb_p", [128, 48])
    dtb_in = dram_in("dtb", [NH, 1])
    alog_in = dram_in("alog", [NH, 1])
    d_row_in = dram_in("d_row", [1, NH])
    nw_row_in = dram_in("nw_row", [1, INTER])
    gb_row_in = dram_in("gb_row", [1, NE])
    identb_in = dram_in("identb_h", [128, 128], BF16)
    identf_in = dram_in("identf_h", [128, 128])
    maskU_in = dram_in("maskU_h", [128, 128])

    out_d = nc.dram_tensor("out", [T, D], F32, kind="ExternalOutput").ap()

    xT_s = scratch("xT_s", [T, INTER], BF16)
    BT_s = scratch("BT_s", [T, NG * NS], BF16)
    Bf_s = scratch("Bf_s", [NG * NS, T], BF16)
    C_s = scratch("C_s", [NG * NS, T], BF16)
    gate_s = scratch("gate_s", [T, INTER], F32)
    u_s = scratch("u_s", [SHI, T], BF16)
    onorm_s = scratch("onorm_s", [T, INTER], BF16)
    cwhi_s = scratch("cwhi_s", [NE, T], BF16)
    cwlo_s = scratch("cwlo_s", [NE, T], BF16)
    acs_d = scratch("acs_d", [2, NH * CS], BF16)
    xlat_s = scratch("xlat_s", [LAT, T], BF16)

    dbg_t = {}
    for nm, shape, dt in [("xn_d", [128, 16 * T], BF16),
                          ("dt_d", [NH, T], F32),
                          ("xlat_d", [128, 8 * T], BF16),
                          ("cw_d", [NE, T], F32),
                          ("ylat_d", [128, 8 * T], F32),
                          ("ych_d", [NCH * 128, INTER], F32)]:
        if nm in dbg:
            dbg_t[nm] = nc.dram_tensor(nm, shape, dt, kind="ExternalOutput").ap()

    identb = nc.alloc_sbuf_tensor("identb", [128, 128], BF16)
    identf = nc.alloc_sbuf_tensor("identf", [128, 128], F32)
    maskU = nc.alloc_sbuf_tensor("maskU", [128, 128], F32)
    ones2 = nc.alloc_sbuf_tensor("ones2", [2, 128], BF16)
    ones1f = nc.alloc_sbuf_tensor("ones1f", [1, 128], F32)
    ones64 = nc.alloc_sbuf_tensor("ones64", [NH, CS], F32)
    dtbc = nc.alloc_sbuf_tensor("dtbc", [NH, 1], F32)
    expA = nc.alloc_sbuf_tensor("expA", [NH, 1], F32)
    alog_t = nc.alloc_sbuf_tensor("alog_t", [NH, 1], F32)
    d_rep = nc.alloc_sbuf_tensor("d_rep", [128, NH], F32)
    nw_bc = nc.alloc_sbuf_tensor("nw_bc", [128, INTER], BF16)
    gb_bc = nc.alloc_sbuf_tensor("gb_bc", [128, NE], F32)
    epsc = nc.alloc_sbuf_tensor("epsc", [128, 1], F32)
    x_lat = nc.alloc_sbuf_tensor("x_lat", [128, 8 * T], BF16)
    arena = nc.alloc_sbuf_tensor("arena", [128, 65536], U8)

    def arn(off_bytes, n, dt):
        step = 2 if dt == BF16 else 4
        return arena[:, off_bytes:off_bytes + n * step].bitcast(dt)

    def xn_t(k):
        return arn(k * T * 2, T, BF16)

    def xn_all():
        return arn(0, 16 * T, BF16)

    psA = nc.alloc_psum_tensor("psA", [128, 2048], F32)
    psB = nc.alloc_psum_tensor("psB", [128, 2048], F32)

    esems = {e: SemC(ctx.enter_context(nc.semaphore("se_" + e)))
             for e in Sched.ENGINES}
    dsems = [SemC(ctx.enter_context(nc.semaphore(f"sd{i}"))) for i in range(10)]
    fsems = {e: SemC(ctx.enter_context(nc.semaphore("sf_" + e)))
             for e in ("vector", "scalar", "gpsimd")}
    DONE = set()

    def mkfence(name, eng):
        fs = fsems[name]

        def F(inst):
            fs.n += 1
            inst.then_inc(fs.h, 1)
            eng.wait_ge(fs.h, fs.n)
            return inst
        return F

    def sched():
        return Sched(nc, esems, dsems, DONE)

    p0_stack = contextlib.ExitStack()
    nwrow = p0_stack.enter_context(nc.sbuf_tensor("nwrow", [1, INTER], F32))

    sc0 = sched()
    ld = []
    for dst, src in [(identb[:], identb_in), (identf[:], identf_in),
                     (maskU[:], maskU_in), (dtbc[:], dtb_in),
                     (alog_t[:], alog_in), (d_rep[0:1, :], d_row_in),
                     (gb_bc[0:1, :], gb_row_in), (nwrow[:], nw_row_in)]:
        ld.append(sc0.dma("sync", lambda e, d=dst, s=src: e.dma_start(d, s),
                          stream=0, tag="c0"))
    sc0.step("vector", lambda v: v.memset(ones2[:], 1.0))
    st_o1 = sc0.step("vector", lambda v: v.memset(ones1f[:], 1.0), tag="ones1f")
    sc0.step("vector", lambda v: v.memset(ones64[:], 1.0))
    sc0.step("vector", lambda v: v.memset(epsc[:], EPS))
    st_ea = sc0.step("scalar", lambda s: s.activation(expA[:], alog_t[:], AF.Exp),
                     deps=[ld[6]], tag="expA")
    def _bc1(t):
        t.matmul(psA[:, 0:64], ones1f[:], d_rep[0:1, :], start=True, stop=True)
        t.matmul(psA[:, 64:96], ones1f[:], gb_bc[0:1, :], start=True, stop=True)
        for n in range(4):
            i = t.matmul(psB[:, n * 512:(n + 1) * 512], ones1f[:],
                         nwrow[:, n * 512:(n + 1) * 512], start=True, stop=True)
        return i

    st_bc = sc0.step("tensor", _bc1, deps=ld[7:10] + [ld[0], st_o1], tag="bcasts")
    st_ev1 = sc0.step("vector", lambda v: (
        v.tensor_copy(d_rep[:], psA[:, 0:64]),
        v.tensor_copy(gb_bc[:], psA[:, 64:96]),
        v.tensor_copy(nw_bc[:, 0:2048], psB[:]))[-1],
                      deps=[st_bc], tag="bcast_evac1")
    st_bc2 = sc0.step("tensor", lambda t: [
        t.matmul(psB[:, n * 512:(n + 1) * 512], ones1f[:],
                 nwrow[:, 2048 + n * 512:2048 + (n + 1) * 512],
                 start=True, stop=True) for n in range(4)][-1],
                      deps=[st_ev1], tag="bcasts2")
    sc0.step("vector", lambda v: v.tensor_copy(nw_bc[:, 2048:4096], psB[:]),
             deps=[st_bc2], tag="bcast_evac2")
    with nc.Block() as blk:
        sc0.emit(blk)
    p0_stack.close()

    with contextlib.ExitStack() as p1:
        xld = p1.enter_context(nc.sbuf_tensor("xld", [128, 2 * D], F32))
        xtm = p1.enter_context(nc.sbuf_tensor("xtm", [128, 2 * D], BF16))
        stat = p1.enter_context(nc.sbuf_tensor("stat", [128, 16], F32))
        junk = p1.enter_context(nc.sbuf_tensor("junk", [128, D], F32))

        def stc(pp, i):
            return stat[:, 8 * pp + i:8 * pp + i + 1]

        def trreg(pp):
            return (psA if pp == 0 else psB)[:, 0:1024].bitcast(BF16)

        sc = sched()
        st_load = [None] * 16
        st_a3 = [None] * 16
        st_tr = [None] * 16
        st_ev = [None] * 16
        for m in range(16):
            pp = m % 2
            st_load[m] = sc.dma(
                "sync",
                lambda e, m=m, pp=pp: e.dma_start(
                    xld[:, pp * D:(pp + 1) * D], x_in[m * 128:(m + 1) * 128, :]),
                deps=[st_a3[m - 2]] if m >= 2 else [],
                stream=pp, tag=f"xld{m}")
            xv = xld[:, pp * D:(pp + 1) * D]
            dep_free = [st_a3[m - 2]] if m >= 2 else []
            def v_stat(v, xv=xv, pp=pp):
                F = mkfence("vector", v)
                v.tensor_scalar(junk[:], xv, 1.0, 0.0, ALU.mult, ALU.add,
                                accum_out=stc(pp, 1))
                F(v.scalar_tensor_tensor(junk[:], xv, 1.0, xv, ALU.mult, ALU.mult,
                                         accum_out=stc(pp, 0)))
                F(v.tensor_scalar(stc(pp, 2), stc(pp, 1), 1.0 / D, None, ALU.mult))
                return F(v.tensor_scalar(stc(pp, 3), stc(pp, 0), 1.0 / D, None,
                                         ALU.mult))

            sc.step("vector", v_stat, deps=[st_load[m]] + dep_free,
                    tag=f"v_stat{m}")
            st_v5 = sc.step("vector", lambda v, pp=pp: v.scalar_tensor_tensor(
                stc(pp, 4), stc(pp, 2), stc(pp, 2), stc(pp, 3), ALU.mult,
                ALU.subtract), tag=f"v5_{m}")
            def a2f(s, pp=pp):
                F = mkfence("scalar", s)
                F(s.activation(stc(pp, 5), stc(pp, 4), AF.Ln, bias=epsc[:, 0:1],
                               scale=-1.0))
                return s.activation(stc(pp, 6), stc(pp, 5), AF.Exp, scale=-0.5)

            st_a2 = sc.step("scalar", a2f, deps=[st_v5], tag=f"a2_{m}")
            st_v6 = sc.step("vector", lambda v, pp=pp: v.tensor_scalar(
                stc(pp, 7), stc(pp, 2), stc(pp, 6), -1.0, ALU.mult, ALU.mult),
                            deps=[st_a2], tag=f"v6_{m}")
            st_a3[m] = sc.step("scalar", lambda s, pp=pp: s.activation(
                xtm[:, pp * D:(pp + 1) * D], xld[:, pp * D:(pp + 1) * D],
                AF.Identity, bias=stc(pp, 7), scale=stc(pp, 6)),
                               deps=[st_v6], tag=f"a3_{m}")
            st_tr[m] = sc.step("tensor", lambda t, m=m, pp=pp: [
                t.transpose(trreg(pp)[:, j * 128:(j + 1) * 128],
                            xtm[:, pp * D + j * 128:pp * D + (j + 1) * 128],
                            identb[:]) for j in range(16)][-1],
                deps=[st_a3[m]] + ([st_ev[m - 2]] if m >= 2 else []),
                tag=f"tr{m}")
            st_ev[m] = sc.step("vector", lambda v, m=m, pp=pp: v.tensor_copy(
                xn_all().rearrange("p (j t) -> p j t", t=T)[
                    :, :, m * 128:(m + 1) * 128],
                trreg(pp).rearrange("p (j q) -> p j q", q=128)),
                deps=[st_tr[m]], tag=f"ev{m}")
        with nc.Block() as blk:
            sc.emit(blk)

    if "xn_d" in dbg_t:
        sc = sched()
        sc.dma("sync", lambda e: e.dma_start(dbg_t["xn_d"], xn_all()),
               stream=0, tag="dump_xn")
        with nc.Block() as blk:
            sc.emit(blk)

    p24 = ctx.enter_context(contextlib.ExitStack())
    dt_raw = p24.enter_context(nc.sbuf_tensor("dt_raw", [NH, T], F32))

    with contextlib.ExitStack() as p2:
        wdt = p2.enter_context(nc.sbuf_tensor("wdt", [128, 16 * 64], BF16))
        convw = p2.enter_context(nc.sbuf_tensor("convw", [128, 192], F32))
        convb = p2.enter_context(nc.sbuf_tensor("convb", [128, 48], F32))
        wsl = p2.enter_context(nc.sbuf_tensor("wsl", [128, 2 * 2048], BF16))
        cvin = p2.enter_context(nc.sbuf_tensor("cvin", [128, 2 * 2051], F32))
        tA = p2.enter_context(nc.sbuf_tensor("tA", [128, 2048], F32))
        tB = p2.enter_context(nc.sbuf_tensor("tB", [128, 2048], F32))
        tC = p2.enter_context(nc.sbuf_tensor("tC", [128, 2048], F32))
        tD = p2.enter_context(nc.sbuf_tensor("tD", [128, 2048], F32))

        NSLAB = 48

        def kindof(s):
            return "x" if s < 32 else ("b" if s < 40 else "c")

        sc = sched()
        st_wl = [None] * NSLAB
        st_mm = [None] * NSLAB
        st_cvev = [None] * NSLAB
        st_tap = [None] * NSLAB
        st_w = [None] * NSLAB
        st_silu = [None] * NSLAB
        st_tp = [None] * NSLAB
        st_tev = [None] * NSLAB
        st_fl = [None] * NSLAB
        st_fc = [None] * NSLAB

        st_pad = sc.step("vector", lambda v: (
            v.memset(cvin[:, 0:3], 0.0),
            v.memset(cvin[:, 2051:2054], 0.0))[-1], tag="cvpad")
        st_wdt = sc.dma("gpsimd", lambda g: g.dma_start(
            wdt[:], winT[:, DT0:DT0 + 64].rearrange("(kt p) m -> p kt m", p=128)),
            stream=2, tag="wdt")
        st_cw1 = sc.dma("sync", lambda e: e.dma_start(convw[:], convw_p),
                        stream=8, tag="convw")
        st_cw2 = sc.dma("sync", lambda e: e.dma_start(convb[:], convb_p),
                        stream=8, tag="convb")

        def wload(g, s):
            pp = s % 2
            return g.dma_start(
                wsl[:, pp * 2048:(pp + 1) * 2048],
                winT[:, HBC0 + s * 128:HBC0 + (s + 1) * 128].rearrange(
                    "(kt p) m -> p kt m", p=128))

        st_dtmm = sc.step("tensor", lambda t: [
            t.matmul(psB[0:64, n * 512:(n + 1) * 512], wdt[:, k * 64:(k + 1) * 64],
                     xn_t(k)[:, n * 512:(n + 1) * 512],
                     start=(k == 0), stop=(k == 15))
            for k in range(16) for n in range(4)][-1],
                          deps=[st_wdt], tag="dtmm")
        st_dtev = sc.step("vector", lambda v: v.tensor_copy(dt_raw[:], psB[0:64, :]),
                          deps=[st_dtmm], tag="dtev")

        for s in range(NSLAB):
            pp = s % 2
            kd = kindof(s)
            st_wl[s] = sc.dma("gpsimd", lambda g, s=s: wload(g, s),
                              deps=[st_mm[s - 2]] if s >= 2 else [],
                              stream=2 + pp, tag=f"wl{s}")
            free = []
            if s == 1:
                free.append(st_dtev)
            if s >= 2:
                free.append(st_cvev[s - 2])
                if kindof(s - 2) != "c":
                    free.append(st_tev[s - 2])
            ps = psA if pp == 0 else psB
            st_mm[s] = sc.step("tensor", lambda t, s=s, pp=pp, ps=ps: [
                t.matmul(ps[:, n * 512:(n + 1) * 512],
                         wsl[:, pp * 2048 + k * 128:pp * 2048 + (k + 1) * 128],
                         xn_t(k)[:, n * 512:(n + 1) * 512],
                         start=(k == 0), stop=(k == 15))
                for k in range(16) for n in range(4)][-1],
                deps=[st_wl[s]] + free, tag=f"mm{s}")
            cv = cvin[:, pp * 2051:(pp + 1) * 2051]
            st_cvev[s] = sc.step("vector", lambda v, cv=cv, ps=ps: v.tensor_copy(
                cv[:, 3:2051], ps[:]),
                deps=[st_mm[s]] + ([st_silu[s - 1]] if s >= 1 else []),
                tag=f"cvev{s}")
            st_tap[s] = sc.step("gpsimd", lambda g, cv=cv, s=s: (
                g.tensor_tensor(tC[:], cv[:, 3:2051],
                                convw[:, 4 * s + 3:4 * s + 4].broadcast_to(
                                    [128, 2048]), ALU.mult),
                g.tensor_tensor(tD[:], cv[:, 2:2050],
                                convw[:, 4 * s + 2:4 * s + 3].broadcast_to(
                                    [128, 2048]), ALU.mult),
                g.tensor_add(tC[:], tC[:], tD[:]))[-1],
                deps=[st_cvev[s], st_cw1] + ([st_w[s - 1]] if s >= 1 else []),
                tag=f"tap{s}")
            st_w[s] = sc.step("vector", lambda v, cv=cv, s=s: (
                v.tensor_scalar(tA[:], cv[:, 1:2049], convw[:, 4 * s + 1:4 * s + 2],
                                convb[:, s:s + 1], ALU.mult, ALU.add),
                v.scalar_tensor_tensor(tB[:], cv[:, 0:2048],
                                       convw[:, 4 * s:4 * s + 1], tA[:],
                                       ALU.mult, ALU.add),
                v.tensor_add(tA[:], tB[:], tC[:]))[-1],
                deps=[st_tap[s], st_cw1, st_cw2] +
                ([st_silu[s - 1]] if s >= 1 else []),
                tag=f"w{s}")
            sxfree = []
            if s >= 2:
                k2 = kindof(s - 2)
                if k2 in ("b", "c"):
                    sxfree.append(st_fc[s - 2])
                if k2 != "c":
                    sxfree.append(st_tp[s - 2])
            st_silu[s] = sc.step("scalar", lambda sce, s=s, pp=pp: sce.activation(
                x_lat[:, pp * 2048:(pp + 1) * 2048], tA[:], AF.Silu),
                deps=[st_w[s]] + sxfree, tag=f"silu{s}")
            if kd in ("b", "c"):
                dst = (Bf_s[(s - 32) * 128:(s - 31) * 128, :] if kd == "b"
                       else C_s[(s - 40) * 128:(s - 39) * 128, :])
                st_fc[s] = sc.dma("sync", lambda e, dst=dst, pp=pp: e.dma_start(
                    dst, x_lat[:, pp * 2048:(pp + 1) * 2048]),
                    deps=[st_silu[s]], stream=4 + pp, tag=f"fc{s}")
            if kd != "c":
                ps = psA if pp == 0 else psB
                st_tp[s] = sc.step("tensor", lambda t, pp=pp, ps=ps: [
                    t.transpose(ps[:, 0:1024].bitcast(BF16)[:, j * 128:(j + 1) * 128],
                                x_lat[:, pp * 2048 + j * 128:pp * 2048 + (j + 1) * 128],
                                identb[:]) for j in range(16)][-1],
                    deps=[st_silu[s], st_cvev[s]], tag=f"tp{s}")
                hp = (s // 2) % 2
                dst = x_lat[:, 4096 + hp * 4096:4096 + (hp + 1) * 4096].rearrange(
                    "p (j q) -> p j q", q=256)[:, :, pp * 128:pp * 128 + 128]
                stage_free = [st_fl[2 * (s // 2 - 2) + 1]] if s // 2 >= 2 else []
                st_tev[s] = sc.step("scalar", lambda sce, dst=dst, pp=pp, ps=ps:
                                    sce.activation(dst, ps[:, 0:1024].bitcast(
                                        BF16).rearrange("p (j q) -> p j q", q=128),
                                        AF.Copy),
                                    deps=[st_tp[s]] + stage_free, tag=f"tev{s}")
                if s % 2 == 1:
                    hp = (s // 2) % 2
                    src = x_lat[:, 4096 + hp * 4096:4096 + (hp + 1) * 4096].rearrange(
                        "p (j q) -> p j q", q=256)
                    if kd == "x":
                        dstd = xT_s[:, (s - 1) * 128:(s + 1) * 128].rearrange(
                            "(j p) q -> p j q", p=128)
                    else:
                        dstd = BT_s[:, (s - 33) * 128:(s - 31) * 128].rearrange(
                            "(j p) q -> p j q", p=128)
                    st_fl[s] = sc.dma("sync", lambda e, dstd=dstd, src=src:
                                      e.dma_start(dstd, src),
                                      deps=[st_tev[s], st_tev[s - 1]],
                                      stream=6 + hp, tag=f"fl{s}")
        with nc.Block() as blk:
            sc.emit(blk)

    if "dt_d" in dbg_t:
        sc = sched()
        sc.dma("sync", lambda e: e.dma_start(dbg_t["dt_d"], dt_raw[:]), stream=0,
               tag="dump_dt")
        with nc.Block() as blk:
            sc.emit(blk)

    with contextlib.ExitStack() as p2b:
        wg = p2b.enter_context(nc.sbuf_tensor("wg", [128, 2 * 16 * 512], BF16))
        gst = p2b.enter_context(nc.sbuf_tensor("gst", [128, 2 * 512], F32))

        sc = sched()
        st_wgl = [None] * 8
        st_gmm = {}
        st_gev = {}
        st_gdm = {}
        for ng in range(8):
            pp = ng % 2
            deps = [st_gmm[ng * 16 - 17]] if ng >= 2 else []
            st_wgl[ng] = sc.dma("gpsimd", lambda g, ng=ng, pp=pp: g.dma_start(
                wg[:, pp * 8192:(pp + 1) * 8192],
                winT[:, ng * 512:(ng + 1) * 512].rearrange(
                    "(kt p) m -> p kt m", p=128)),
                deps=deps, stream=2 + pp, tag=f"wgl{ng}")
            for m in range(16):
                gi = ng * 16 + m
                sl = gi % 2
                psv = psA[:, sl * 512:(sl + 1) * 512]
                prev_ev = st_gev.get(gi - 2)
                st_gmm[gi] = sc.step("tensor", lambda t, m=m, pp=pp, psv=psv: [
                    t.matmul(psv, xn_t(k)[:, m * 128:(m + 1) * 128],
                             wg[:, pp * 8192 + k * 512:pp * 8192 + (k + 1) * 512],
                             start=(k == 0), stop=(k == 15)) for k in range(16)][-1],
                    deps=[st_wgl[ng]] + ([prev_ev] if prev_ev else []),
                    tag=f"gmm{gi}")
                prev_dm = st_gdm.get(gi - 2)
                st_gev[gi] = sc.step("scalar", lambda s, sl=sl, psv=psv:
                                     s.activation(gst[:, sl * 512:(sl + 1) * 512],
                                                  psv, AF.Copy),
                                     deps=[st_gmm[gi]] +
                                     ([prev_dm] if prev_dm else []),
                                     tag=f"gev{gi}")
                st_gdm[gi] = sc.dma("sync", lambda e, ng=ng, m=m, sl=sl:
                                    e.dma_start(
                                        gate_s[m * 128:(m + 1) * 128,
                                               ng * 512:(ng + 1) * 512],
                                        gst[:, sl * 512:(sl + 1) * 512]),
                                    deps=[st_gev[gi]], stream=4 + sl,
                                    tag=f"gdm{gi}")
        with nc.Block() as blk:
            sc.emit(blk)

    if phases <= 2:
        _finish_stub(nc, sched, out_d, arn, dsems)
        ctx.close()
        return nc

    with contextlib.ExitStack() as p3:
        gwb = p3.enter_context(nc.sbuf_tensor("gwb", [128, 512], BF16))
        rw = p3.enter_context(nc.sbuf_tensor("rw", [128, 2 * 144], F32))
        cwT = p3.enter_context(nc.sbuf_tensor("cwT", [NE, T], F32))
        cwbf = p3.enter_context(nc.sbuf_tensor("cwbf", [NE, 2 * T], BF16))
        cwlf = p3.enter_context(nc.sbuf_tensor("cwlf", [NE, T], F32))
        wstr = p3.enter_context(nc.sbuf_tensor("wstr", [128, 2 * 2048], BF16))
        tr32 = p3.enter_context(nc.sbuf_tensor("tr32", [128, 2 * 2048], F32))
        usq = p3.enter_context(nc.sbuf_tensor("usq", [128, 2 * 2048], BF16))
        spacer = p3.enter_context(nc.sbuf_tensor("spacer", [128, 32], F32))

        sc = sched()
        st_gw = sc.dma("gpsimd", lambda g: g.dma_start(gwb[:], gw_pk), stream=2,
                       tag="gwb")
        st_rt = [None] * 16
        st_cwev = [None] * 16

        def psl_v(sl):
            return psB[:, 1024 + sl * 512:1024 + sl * 512 + 32]

        def pst_v(sl):
            return psB[0:32, 1024 + sl * 512 + 32:1024 + sl * 512 + 160]

        for m in range(16):
            pp = m % 2
            b = pp * 144
            sl = m % 2
            dep_ps = [st_cwev[m - 2]] if m >= 2 else []
            st_lg = sc.step("tensor", lambda t, m=m, sl=sl: [
                t.matmul(psl_v(sl), xn_t(k)[:, m * 128:(m + 1) * 128],
                         gwb[:, k * 32:(k + 1) * 32],
                         start=(k == 0), stop=(k == 15)) for k in range(16)][-1],
                deps=[st_gw] + dep_ps, tag=f"lg{m}")
            st_sig = sc.step("scalar", lambda s, b=b, sl=sl: s.activation(
                rw[:, b:b + 32], psl_v(sl), AF.Sigmoid),
                deps=[st_lg], tag=f"sig{m}")

            def routing(v, b=b):
                w = rw
                F = mkfence("vector", v)
                F(v.tensor_add(w[:, b + 32:b + 64], w[:, b:b + 32], gb_bc[:]))
                for g in range(4):
                    F(v.max(out=w[:, b + 104 + 8 * g:b + 112 + 8 * g],
                            in_=w[:, b + 32 + 8 * g:b + 40 + 8 * g]))
                F(v.memset(w[:, b + 72:b + 76], -1e30))
                for g in range(4):
                    F(v.tensor_add(w[:, b + 64 + g:b + 65 + g],
                                   w[:, b + 104 + 8 * g:b + 105 + 8 * g],
                                   w[:, b + 105 + 8 * g:b + 106 + 8 * g]))
                F(v.tensor_copy(w[:, b + 68:b + 72], w[:, b + 64:b + 68]))
                F(v.max(out=w[:, b + 84:b + 92], in_=w[:, b + 68:b + 76]))
                F(v.tensor_scalar(w[:, b + 92:b + 96], w[:, b + 64:b + 68],
                                  w[:, b + 85:b + 86], None, ALU.is_ge))
                F(v.tensor_scalar(w[:, b + 96:b + 100], w[:, b + 92:b + 96],
                                  1e30, -1e30, ALU.mult, ALU.add))
                for g in range(4):
                    F(v.scalar_tensor_tensor(
                        w[:, b + 32 + 8 * g:b + 40 + 8 * g],
                        w[:, b + 96 + g:b + 97 + g].broadcast_to([128, 8]),
                        1.0, w[:, b + 32 + 8 * g:b + 40 + 8 * g],
                        ALU.mult, ALU.add))
                F(v.max(out=w[:, b + 76:b + 84], in_=w[:, b + 32:b + 64]))
                F(v.tensor_scalar(w[:, b + 32:b + 64], w[:, b + 32:b + 64],
                                  w[:, b + 83:b + 84], None, ALU.is_ge))
                F(v.scalar_tensor_tensor(w[:, b + 104:b + 136], w[:, b + 32:b + 64],
                                         1.0, w[:, b:b + 32], ALU.mult, ALU.mult,
                                         accum_out=w[:, b + 100:b + 101]))
                F(v.reciprocal(w[:, b + 101:b + 102], w[:, b + 100:b + 101]))
                return v.tensor_scalar(w[:, b + 32:b + 64], w[:, b + 104:b + 136],
                                       w[:, b + 101:b + 102], 2.5, ALU.mult,
                                       ALU.mult)

            st_rt[m] = sc.step("vector", routing, deps=[st_sig], tag=f"rt{m}")
            st_ct = sc.step("tensor", lambda t, b=b, sl=sl: t.transpose(
                pst_v(sl), rw[:, b + 32:b + 64], identf[:]),
                deps=[st_rt[m]], tag=f"ct{m}")
            st_cwev[m] = sc.step("vector", lambda v, m=m, sl=sl: v.tensor_copy(
                cwT[:, m * 128:(m + 1) * 128], pst_v(sl)),
                deps=[st_ct], tag=f"cwev{m}")
        st_hl = sc.step("vector", lambda v: (
            v.tensor_copy(cwbf[:, 0:T], cwT[:]),
            v.tensor_copy(cwlf[:], cwbf[:, 0:T]),
            v.tensor_sub(cwlf[:], cwT[:], cwlf[:]),
            v.tensor_copy(cwbf[:, T:2 * T], cwlf[:]))[-1],
            deps=[st_cwev[15]], tag="cwhl")
        sc.dma("sync", lambda e: e.dma_start(cwhi_s, cwbf[:, 0:T]),
               deps=[st_hl], stream=6, tag="cwd1")
        sc.dma("sync", lambda e: e.dma_start(cwlo_s, cwbf[:, T:2 * T]),
               deps=[st_hl], stream=7, tag="cwd2")
        if "cw_d" in dbg_t:
            sc.dma("sync", lambda e: e.dma_start(dbg_t["cw_d"], cwT[:]),
                   deps=[st_hl], stream=6, tag="dump_cw")

        st_f1l = [None] * 8
        st_f1m = [None] * 8
        st_f1e = [None] * 8
        for mj in range(8):
            pp = mj % 2
            st_f1l[mj] = sc.dma("gpsimd", lambda g, mj=mj, pp=pp: g.dma_start(
                wstr[:, pp * 2048:(pp + 1) * 2048],
                fc1T[:, mj * 128:(mj + 1) * 128].rearrange(
                    "(kt p) m -> p kt m", p=128)),
                deps=[st_f1m[mj - 2]] if mj >= 2 else [], stream=2 + pp,
                tag=f"f1l{mj}")
            st_f1m[mj] = sc.step("tensor", lambda t, mj=mj, pp=pp: [
                t.matmul(psA[:, n * 512:(n + 1) * 512],
                         wstr[:, pp * 2048 + k * 128:pp * 2048 + (k + 1) * 128],
                         xn_t(k)[:, n * 512:(n + 1) * 512],
                         start=(k == 0), stop=(k == 15))
                for k in range(16) for n in range(4)][-1],
                deps=[st_f1l[mj]] + ([st_f1e[mj - 1]] if mj >= 1 else []),
                tag=f"f1m{mj}")
            st_f1e[mj] = sc.step("scalar", lambda s, mj=mj: s.activation(
                x_lat[:, mj * T:(mj + 1) * T], psA[:], AF.Copy),
                deps=[st_f1m[mj]], tag=f"f1e{mj}")
            sc.dma("sync", lambda e, mj=mj: e.dma_start(
                xlat_s[mj * 128:(mj + 1) * 128, :].rearrange(
                    "(a p) q -> p a q", p=128).squeeze(1),
                x_lat[:, mj * T:(mj + 1) * T]),
                deps=[st_f1e[mj]], stream=8, tag=f"f1d{mj}")
        with nc.Block() as blk:
            sc.emit(blk)

        if "xlat_d" in dbg_t:
            sc = sched()
            sc.dma("sync", lambda e: e.dma_start(dbg_t["xlat_d"], x_lat[:]),
                   stream=0, tag="dump_xlat")
            with nc.Block() as blk:
                sc.emit(blk)

        sc = sched()
        st_sul = [None] * 32
        st_sum = [None] * 32
        st_sue = [None] * 32
        st_sud = [None] * 32
        for mj in range(32):
            pp = mj % 2
            ps = psA if pp == 0 else psB
            st_sul[mj] = sc.dma("gpsimd", lambda g, mj=mj, pp=pp: g.dma_start(
                wstr[:, pp * 2048:(pp + 1) * 2048],
                supT[:, mj * 128:(mj + 1) * 128].rearrange(
                    "(kt p) m -> p kt m", p=128)),
                deps=[st_sum[mj - 2]] if mj >= 2 else [], stream=2 + pp,
                tag=f"sul{mj}")
            st_sum[mj] = sc.step("tensor", lambda t, mj=mj, pp=pp, ps=ps: [
                t.matmul(ps[:, n * 512:(n + 1) * 512],
                         wstr[:, pp * 2048 + k * 128:pp * 2048 + (k + 1) * 128],
                         xn_t(k)[:, n * 512:(n + 1) * 512],
                         start=(k == 0), stop=(k == 15))
                for k in range(16) for n in range(4)][-1],
                deps=[st_sul[mj]] + ([st_sue[mj - 2]] if mj >= 2 else []),
                tag=f"sum{mj}")
            st_sue[mj] = sc.step("scalar", lambda s, mj=mj, pp=pp, ps=ps: (
                s.activation(tr32[:, pp * 2048:(pp + 1) * 2048], ps[:], AF.Relu),
                s.activation(usq[:, pp * 2048:(pp + 1) * 2048],
                             tr32[:, pp * 2048:(pp + 1) * 2048], AF.Square))[-1],
                deps=[st_sum[mj]] + ([st_sud[mj - 2]] if mj >= 2 else []),
                tag=f"sue{mj}")
            st_sud[mj] = sc.dma("sync", lambda e, mj=mj, pp=pp: e.dma_start(
                u_s[mj * 128:(mj + 1) * 128, :],
                usq[:, pp * 2048:(pp + 1) * 2048]),
                deps=[st_sue[mj]], stream=4 + pp, tag=f"sud{mj}")
        with nc.Block() as blk:
            sc.emit(blk)

    if phases <= 3:
        _finish_stub(nc, sched, out_d, arn, dsems)
        ctx.close()
        return nc

    with contextlib.ExitStack() as p4:
        xTc2 = p4.enter_context(nc.sbuf_tensor("xTc2", [128, 2 * 4096], BF16))
        BTc2 = p4.enter_context(nc.sbuf_tensor("BTc2", [128, 2 * 1024], BF16))
        Bfc2 = p4.enter_context(nc.sbuf_tensor("Bfc2", [128, 2 * 1024], BF16))
        Cc2 = p4.enter_context(nc.sbuf_tensor("Cc2", [128, 2 * 1024], BF16))
        gatec2 = p4.enter_context(nc.sbuf_tensor("gatec2", [128, 2 * 4096], F32))
        dtw = p4.enter_context(nc.sbuf_tensor("dtw", [128, 10 * CS], F32))
        hilo = p4.enter_context(nc.sbuf_tensor("hilo", [NH, 2 * CS], BF16))
        tmw = p4.enter_context(nc.sbuf_tensor("tmw", [128, 5 * 64], F32))
        cdrep = p4.enter_context(nc.sbuf_tensor("cdrep", [128, 64], F32))
        sct = x_lat[:, 12288:13312].bitcast(F32)
        EB = p4.enter_context(nc.sbuf_tensor("EB", [128, 2 * 1024], BF16))
        SB2 = p4.enter_context(nc.sbuf_tensor("SB2", [128, 2 * 1024], BF16))
        SMb = p4.enter_context(nc.sbuf_tensor("SMb", [128, 2 * 1024], BF16))
        CSD = p4.enter_context(nc.sbuf_tensor("CSD", [128, 2 * 1024], BF16))
        carry_f = x_lat[:, 0:8192].bitcast(F32)
        carry_bf = x_lat[:, 8192:12288]
        vrw = p4.enter_context(nc.sbuf_tensor("vrw", [128, 24], F32))
        tn512 = x_lat[:, 13312:14336].bitcast(F32)
        sqj = x_lat[:, 14336:15360].bitcast(F32)

        xdtT = arn(0, 4096, BF16)
        xddT = arn(8192, 4096, BF16)
        t0full = arn(16384, 4096, F32)
        Ych = arn(32768, 4096, F32)
        onstage = arn(16384, 4096, BF16)
        acsfl = arena[0:2, 49152:49152 + NH * CS * 2].bitcast(BF16)

        def dw(i):
            return dtw[0:NH, i * CS:(i + 1) * CS]

        def dwf(i):
            return dtw[:, i * CS:(i + 1) * CS]

        def tw(i):
            return tmw[:, i * 64:(i + 1) * 64]

        def ldsl(buf, par, w):
            return buf[:, par * w:(par + 1) * w]

        sc = sched()

        def mb(lvl, fn):
            return fn if _P4_LVL >= lvl else (lambda e: e.nop())

        st_init_cf = sc.step("vector", lambda v: v.memset(carry_f[:], 0.0),
                             tag="cf0")
        st_dtw0 = sc.step("vector", lambda v: v.memset(dtw[64:128, :], 0.0),
                          tag="dtw0")
        st_init_cb = sc.step("gpsimd", lambda g: g.memset(carry_bf[:], 0.0),
                             tag="cb0")

        st_ld = {}
        st_px = [None] * NCH
        st_ych = [None] * NCH
        st_gm = [None] * NCH
        st_onrm = [None] * NCH
        st_dmon = [None] * NCH
        st_trev = [None] * NCH
        st_cdexp = [None] * NCH
        st_fl3 = [None] * NCH
        st_sc_l = {}
        st_sct_l = {}
        st_esgb = {}
        st_sm = {}
        st_csd = {}
        st_y = {}
        st_stt = {}
        st_t0 = {}
        st_carry = {}
        st_silu = [None] * NCH

        nch_lim = _NCH_LIM if _NCH_LIM is not None else NCH
        for c in range(nch_lim):
            par = c % 2
            cs0, cs1 = c * CS, (c + 1) * CS
            dep_x = [st_ych[c - 2]] if c >= 2 else []
            dep_bt = [st_stt[(c - 2, 7)]] if c >= 2 else []
            dep_bf = [st_sc_l[(c - 2, 7)]] if c >= 2 else []
            dep_c = [st_y[(c - 2, 7)], st_csd[(c - 2, 7)]] if c >= 2 else []
            dep_g = [st_gm[c - 2]] if c >= 2 else []
            l_x = sc.dma("sync", lambda e, c=c, par=par: e.dma_start(
                ldsl(xTc2, par, 4096), xT_s[c * CS:(c + 1) * CS, :]),
                deps=dep_x, stream=par, tag=f"lx{c}")
            l_bt = sc.dma("sync", lambda e, c=c, par=par: e.dma_start(
                ldsl(BTc2, par, 1024), BT_s[c * CS:(c + 1) * CS, :]),
                deps=dep_bt, stream=par, tag=f"lbt{c}")
            l_bf = sc.dma("sync", lambda e, c=c, par=par: e.dma_start(
                ldsl(Bfc2, par, 1024),
                Bf_s[:, c * CS:(c + 1) * CS].rearrange("(gt p) s -> p gt s", p=128)),
                deps=dep_bf, stream=par, tag=f"lbf{c}")
            l_c = sc.dma("sync", lambda e, c=c, par=par: e.dma_start(
                ldsl(Cc2, par, 1024),
                C_s[:, c * CS:(c + 1) * CS].rearrange("(gt p) s -> p gt s", p=128)),
                deps=dep_c, stream=par, tag=f"lc{c}")
            l_g = sc.dma("sync", lambda e, c=c, par=par: e.dma_start(
                ldsl(gatec2, par, 4096), gate_s[c * CS:(c + 1) * CS, :]),
                deps=dep_g, stream=par, tag=f"lg{c}")

            def dt1(v, cs0=cs0, cs1=cs1):
                return v.tensor_scalar(dw(0), dt_raw[:, cs0:cs1], dtbc[:, 0:1],
                                       None, ALU.add)
            st_dt1 = sc.step("vector", mb(1, dt1), tag=f"dt1_{c}")

            def act1(s):
                F = mkfence("scalar", s)
                F(s.activation(dw(1), dw(0), AF.Abs))
                F(s.activation(dw(2), dw(1), AF.Exp, scale=-1.0))
                return s.activation(dw(3), dw(2), AF.Ln, bias=1.0)
            st_act1 = sc.step("scalar", mb(1, act1), deps=[st_dt1], tag=f"act1_{c}")

            def dt2(v):
                F = mkfence("vector", v)
                F(v.tensor_scalar(dw(4), dw(0), 0.0, None, ALU.max))
                F(v.tensor_add(dw(5), dw(4), dw(3)))
                F(v.tensor_scalar(dw(6), dw(5), expA[:, 0:1], -1.0, ALU.mult,
                                  ALU.mult))
                F(v.tensor_tensor_scan(dw(7), ones64[:], dw(6), 0.0, ALU.mult,
                                       ALU.add))
                F(v.tensor_copy(hilo[:, 0:CS], dw(7)))
                F(v.tensor_copy(dw(8), hilo[:, 0:CS]))
                F(v.tensor_sub(dw(8), dw(7), dw(8)))
                F(v.tensor_copy(hilo[:, CS:2 * CS], dw(8)))
                return v.tensor_scalar(dw(8), dw(7), dw(7)[:, CS - 1:CS], None,
                                       ALU.subtract)
            st_dt2 = sc.step("vector", mb(1, dt2), deps=[st_act1], tag=f"dt2_{c}")

            def act2(s):
                return s.activation(dw(9), dw(8), AF.Exp, scale=-1.0)
            st_act2 = sc.step("scalar", mb(1, act2), deps=[st_dt2], tag=f"act2_{c}")

            g1 = sc.dma("gpsimd", lambda g: g.dma_start(
                acs_d[0:1, :].rearrange("a (h l) -> a h l", h=NH).squeeze(0),
                hilo[:, 0:CS]), deps=[st_dt2], stream=2, tag=f"fh{c}")
            g2 = sc.dma("gpsimd", lambda g: g.dma_start(
                acs_d[1:2, :].rearrange("a (h l) -> a h l", h=NH).squeeze(0),
                hilo[:, CS:2 * CS]), deps=[st_dt2], stream=2, tag=f"fl{c}")
            dep_fl = [st_sc_l[(c - 1, 7)]] if c >= 1 else []
            st_fl3[c] = sc.dma("gpsimd", lambda g: g.dma_start(acsfl[:], acs_d),
                               deps=[g1, g2] + ([st_esgb[(c - 1, 7)],
                                                 st_cdexp[c - 1]] if c >= 1 else []),
                               stream=2, tag=f"fa{c}")

            dep_b4 = [st_sct_l[(c - 1, 7)], st_cdexp[c - 1]] if c >= 1 else []
            def tr1(t, c=c):
                t.transpose(psB[:, 0:128], dwf(7), identf[:])
                t.transpose(psB[:, 128:256], dwf(5), identf[:])
                t.transpose(psB[:, 256:384], dwf(9), identf[:])
                return t.matmul(psB[:, 384:448], ones2[:],
                                acsfl[0:2, CS - 1::CS], start=True, stop=True)
            st_tr1 = sc.step("tensor", mb(3, tr1),
                             deps=[st_dt2, st_act2, st_fl3[c]] + dep_b4,
                             tag=f"tr1_{c}")

            def trev(v):
                if _P4_TREV == 1:
                    v.tensor_copy(tw(0), psB[:, 0:64])
                    v.tensor_copy(tw(2), psB[:, 128:192])
                    v.tensor_copy(tw(3), psB[:, 256:320])
                    v.tensor_scalar(tw(1), tw(0), -1.0, None, ALU.mult)
                    return v.tensor_tensor(tw(4), tw(2), tw(3), ALU.mult)
                F = mkfence("vector", v)
                F(v.tensor_copy(tw(0), psB[:, 0:64]))
                F(v.tensor_scalar(tw(1), tw(0), -1.0, None, ALU.mult))
                F(v.tensor_copy(tw(2), psB[:, 128:192]))
                F(v.tensor_copy(tw(3), psB[:, 256:320]))
                return v.tensor_tensor(tw(4), tw(2), tw(3), ALU.mult)
            st_trev[c] = sc.step("vector", trev, deps=[st_tr1], tag=f"trev{c}")

            st_cdexp[c] = sc.step("scalar", lambda s: s.activation(
                cdrep[:], psB[:, 384:448], AF.Exp),
                deps=[st_tr1, st_trev[c]], tag=f"cde{c}")

            def px(g, par=par):
                xr = ldsl(xTc2, par, 4096).rearrange("p (h q) -> p h q", q=64)
                g.tensor_tensor(xdtT.rearrange("p (h q) -> p h q", q=64), xr,
                                tw(2).unsqueeze(2).broadcast_to([128, 64, 64]),
                                ALU.mult)
                return g.tensor_tensor(
                    xddT.rearrange("p (h q) -> p h q", q=64), xr,
                    tw(4).unsqueeze(2).broadcast_to([128, 64, 64]), ALU.mult)
            st_px[c] = sc.step("gpsimd", mb(4, px),
                               deps=[st_trev[c], l_x] +
                               ([st_y[(c - 1, 7)], st_stt[(c - 1, 7)]]
                                if c >= 1 else []),
                               tag=f"px{c}")

            for g in range(8):
                G = c * 8 + g
                parG = G % 2
                h0 = 8 * g

                def pe_seg(t, parG=parG, h0=h0):
                    for hp in range(8):
                        i = t.matmul(
                            psA[:, parG * 1024 + hp * 128:parG * 1024 + (hp + 1) * 128],
                            ones2[:],
                            acsfl[0:2, (h0 + hp) * CS:(h0 + hp + 1) * CS],
                            start=True, stop=True)
                    return i
                dep_seg = [st_fl3[c]] + ([st_esgb[divmod(G - 2, 8)]]
                                         if G >= 2 else [])
                st_seg = sc.step("tensor", mb(5, pe_seg), deps=dep_seg, tag=f"seg{G}")

                dep_sc = [l_bf, l_c] + ([st_sct_l[divmod(G - 1, 8)]]
                                        if G >= 1 else [])
                if g == 0:
                    dep_sc += [st_trev[c], st_cdexp[c]]
                st_sc_l[(c, g)] = sc.step("tensor", lambda t, par=par, g=g: t.matmul(
                    psB[:, 0:128], ldsl(Bfc2, par, 1024)[:, g * 128:(g + 1) * 128],
                    ldsl(Cc2, par, 1024)[:, g * 128:(g + 1) * 128],
                    start=True, stop=True),
                    deps=dep_sc, tag=f"sc{G}")

                def v_sct(v, parG=parG):
                    F = mkfence("vector", v)
                    F(v.tensor_tensor(
                        sct[:, parG * 128:(parG + 1) * 128], psB[:, 0:128],
                        maskU[:], ALU.mult))
                    return v.engine_nop()
                st_sct_l[(c, g)] = sc.step("vector", mb(5, v_sct),
                                           deps=[st_sc_l[(c, g)]], tag=f"sctm{G}")

                def a_esgb(s, parG=parG, h0=h0):
                    for hp in range(8):
                        pseg = psA[:, parG * 1024 + hp * 128:parG * 1024 + (hp + 1) * 128]
                        s.activation(EB[:, parG * 1024 + hp * 128:parG * 1024 + (hp + 1) * 128],
                                     pseg, AF.Exp, bias=tw(1)[:, h0 + hp - h0 + 0:0] if False
                                     else tw(1)[:, hp + h0 - h0:hp + 1] if False
                                     else tmw[:, 64 + h0 + hp:64 + h0 + hp + 1])
                        i = s.activation(SB2[:, parG * 1024 + hp * 128:parG * 1024 + (hp + 1) * 128],
                                         pseg, AF.Exp)
                    return i
                st_esgb[(c, g)] = sc.step(
                    "scalar", mb(5, a_esgb),
                    deps=[st_seg, st_trev[c]] +
                    ([st_sm[divmod(G - 2, 8)], st_csd[divmod(G - 2, 8)]]
                     if G >= 2 else []),
                    tag=f"esgb{G}")

                def v_sm(v, parG=parG):
                    for hp in range(8):
                        i = v.scalar_tensor_tensor(
                            SMb[:, parG * 1024 + hp * 128:parG * 1024 + (hp + 1) * 128],
                            EB[:, parG * 1024 + hp * 128:parG * 1024 + (hp + 1) * 128],
                            1.0, sct[:, parG * 128:(parG + 1) * 128].unsqueeze(1)
                            .broadcast_to([128, 1, 128]).squeeze(1) if False
                            else sct[:, parG * 128:(parG + 1) * 128],
                            ALU.min, ALU.mult)
                    return i
                st_sm[(c, g)] = sc.step("vector", mb(6, v_sm),
                                        deps=[st_esgb[(c, g)], st_sct_l[(c, g)]],
                                        tag=f"sm{G}")

                def g_csd(g_, par=par, parG=parG, g=g):
                    for hp in range(8):
                        i = g_.tensor_tensor(
                            CSD[:, parG * 1024 + hp * 128:parG * 1024 + (hp + 1) * 128],
                            ldsl(Cc2, par, 1024)[:, g * 128:(g + 1) * 128],
                            SB2[:, parG * 1024 + hp * 128:parG * 1024 + (hp + 1) * 128],
                            ALU.mult)
                    return i
                st_csd[(c, g)] = sc.step("gpsimd", mb(6, g_csd),
                                         deps=[st_esgb[(c, g)], l_c],
                                         tag=f"csd{G}")

                def pe_y(t, parG=parG, h0=h0):
                    for hp in range(8):
                        ps = psB[:, 512 + parG * 1024 + hp * 64:
                                 512 + parG * 1024 + (hp + 1) * 64]
                        t.matmul(ps,
                                 SMb[:, parG * 1024 + hp * 128:parG * 1024 + (hp + 1) * 128],
                                 xdtT[:, (h0 + hp) * 64:(h0 + hp + 1) * 64],
                                 start=True, stop=False)
                        i = t.matmul(ps,
                                     CSD[:, parG * 1024 + hp * 128:parG * 1024 + (hp + 1) * 128],
                                     carry_bf[:, (h0 + hp) * 64:(h0 + hp + 1) * 64],
                                     start=False, stop=True)
                    return i
                dep_y = [st_sm[(c, g)], st_csd[(c, g)], st_px[c]]
                dep_y += [st_init_cb] if c == 0 else [st_cb]
                dep_y += [st_t0[divmod(G - 2, 8)]] if G >= 2 else []
                st_y[(c, g)] = sc.step("tensor", mb(7, pe_y), deps=dep_y, tag=f"y{G}")

                def pe_st(t, par=par, g=g, h0=h0):
                    for hp in range(8):
                        i = t.matmul(
                            psB[:, 1024 + hp * 64:1024 + (hp + 1) * 64],
                            ldsl(BTc2, par, 1024)[:, g * 128:(g + 1) * 128],
                            xddT[:, (h0 + hp) * 64:(h0 + hp + 1) * 64],
                            start=True, stop=True)
                    return i
                dep_st = [st_px[c], l_bt] + ([st_carry[divmod(G - 1, 8)]]
                                             if G >= 1 else [])
                st_stt[(c, g)] = sc.step("tensor", mb(7, pe_st), deps=dep_st,
                                         tag=f"st{G}")

                st_t0[(c, g)] = sc.step("scalar", mb(8, lambda s, parG=parG, g=g:
                                        s.activation(
                    t0full[:, g * 512:(g + 1) * 512],
                    psB[:, 512 + parG * 1024:512 + parG * 1024 + 512], AF.Copy)),
                    deps=[st_y[(c, g)]] +
                    ([st_ych[c - 1]] if c >= 1 and g == 0 else []) +
                    ([st_dmon[c - 1]] if c >= 1 and g <= 3 else []),
                    tag=f"t0_{G}")

                def v_carry(v, h0=h0):
                    for hp in range(8):
                        i = v.scalar_tensor_tensor(
                            carry_f[:, (h0 + hp) * 64:(h0 + hp + 1) * 64],
                            carry_f[:, (h0 + hp) * 64:(h0 + hp + 1) * 64],
                            cdrep[:, h0 + hp:h0 + hp + 1],
                            psB[:, 1024 + hp * 64:1024 + (hp + 1) * 64],
                            ALU.mult, ALU.add)
                    return i
                st_carry[(c, g)] = sc.step(
                    "vector", mb(8, v_carry),
                    deps=[st_stt[(c, g)], st_cdexp[c]] +
                    ([st_init_cf] if c == 0 and g == 0 else []),
                    tag=f"carry{G}")

            st_cb = sc.step("gpsimd", mb(8, lambda g: g.tensor_copy(
                carry_bf[:], carry_f[:])),
                deps=[st_carry[(c, 7)]], tag=f"cb{c}")
            st_silu[c] = sc.step("scalar", mb(9, lambda s, par=par: s.activation(
                ldsl(gatec2, par, 4096), ldsl(gatec2, par, 4096), AF.Silu)),
                deps=[l_g], tag=f"silu{c}")

            def g_ych(g, par=par):
                xr = ldsl(xTc2, par, 4096).rearrange("p (h q) -> p h q", q=64)
                g.tensor_tensor(Ych.rearrange("p (h q) -> p h q", q=64), xr,
                                d_rep[:].unsqueeze(2).broadcast_to([128, 64, 64]),
                                ALU.mult)
                return g.tensor_add(Ych[:], Ych[:], t0full[:])
            st_ych[c] = sc.step("gpsimd", mb(9, g_ych),
                                deps=[st_t0[(c, 7)], l_x] +
                                ([st_onrm[c - 1]] if c >= 1 else []),
                                tag=f"ych{c}")
            st_gm[c] = sc.step("gpsimd", mb(9, lambda g, par=par: g.tensor_tensor(
                Ych[:], Ych[:], ldsl(gatec2, par, 4096), ALU.mult)),
                deps=[st_ych[c], st_silu[c]], tag=f"gm{c}")

            def v_var(v):
                for ng in range(8):
                    i = v.scalar_tensor_tensor(
                        sqj[:], Ych[:, ng * 512:(ng + 1) * 512], 1.0,
                        Ych[:, ng * 512:(ng + 1) * 512], ALU.mult, ALU.mult,
                        accum_out=vrw[:, ng:ng + 1])
                return i
            st_var = sc.step("vector", mb(9, v_var), deps=[st_gm[c]], tag=f"var{c}")

            def a_rstd(s):
                F = mkfence("scalar", s)
                F(s.activation(vrw[:, 8:16], vrw[:, 0:8], AF.Ln,
                               bias=epsc[:, 0:1], scale=1.0 / 512))
                return s.activation(vrw[:, 8:16], vrw[:, 8:16], AF.Exp,
                                    scale=-0.5)
            st_rstd = sc.step("scalar", mb(9, a_rstd), deps=[st_var], tag=f"rstd{c}")

            def g_onrm(g):
                for ng in range(8):
                    g.tensor_tensor(tn512[:], Ych[:, ng * 512:(ng + 1) * 512],
                                    vrw[:, 8 + ng:9 + ng].broadcast_to([128, 512]),
                                    ALU.mult)
                    i = g.tensor_tensor(onstage[:, ng * 512:(ng + 1) * 512],
                                        tn512[:], nw_bc[:, ng * 512:(ng + 1) * 512],
                                        ALU.mult)
                return i
            st_onrm[c] = sc.step("gpsimd", mb(9, g_onrm),
                                 deps=[st_rstd, st_gm[c]] +
                                 ([st_dmon[c - 1]] if c >= 1 else []),
                                 tag=f"onrm{c}")
            st_dmon[c] = sc.dma("sync", lambda e, c=c: e.dma_start(
                onorm_s[c * CS:(c + 1) * CS, :], onstage[:]),
                deps=[st_onrm[c]], stream=4 + par, tag=f"dmon{c}")
            if "ych_d" in dbg_t:
                sc.dma("sync", lambda e, c=c: e.dma_start(
                    dbg_t["ych_d"][c * CS:(c + 1) * CS, :], Ych[:]),
                    deps=[st_gm[c], st_var], stream=6, tag=f"dych{c}")

        with nc.Block() as blk:
            sc.emit(blk)

    if phases <= 4:
        _finish_stub(nc, sched, out_d, arn, dsems)
        ctx.close()
        return nc

    with contextlib.ExitStack() as p6:
        w1b = p6.enter_context(nc.sbuf_tensor("w1b", [128, 2 * 8192], BF16))
        w2b = p6.enter_context(nc.sbuf_tensor("w2b", [128, 8192], BF16))
        hbuf = p6.enter_context(nc.sbuf_tensor("hbuf", [128, 8 * 2048], BF16))
        cwrow = p6.enter_context(nc.sbuf_tensor("cwrow", [2, 2048], BF16))
        cwb_bf = p6.enter_context(nc.sbuf_tensor("cwb_bf", [128, 2048], BF16))
        trl = p6.enter_context(nc.sbuf_tensor("trl", [128, 1024], F32))

        def ylat(l):
            return arn(l * T * 4, T, F32)

        sc = sched()
        st_xlr = sc.dma("sync", lambda e: e.dma_start(
            x_lat[:].rearrange("p (t q) -> p t q", q=T),
            xlat_s.rearrange("(t p) q -> p t q", p=128)), stream=0, tag="xlr")
        st_y0 = sc.step("vector", lambda v: v.memset(arn(0, 8 * T, F32), 0.0),
                        tag="y0")

        st_w1 = [None] * NE
        st_w2 = [None] * NE
        st_cwr = [None] * NE
        st_cwb = [None] * NE
        st_cwbe = [None] * NE
        st_hm = {}
        st_sq = {}
        st_hx = {}
        st_ym = {}
        st_ya = {}
        for e_ in range(NE):
            par = e_ % 2
            st_w1[e_] = sc.dma("gpsimd", lambda g, e_=e_, par=par: g.dma_start(
                w1b[:, par * 8192:(par + 1) * 8192],
                w1T[e_].rearrange("(kt p) i -> p kt i", p=128)),
                deps=[st_hm[(e_ - 2, 15)]] if e_ >= 2 else [],
                stream=2 + par, tag=f"w1_{e_}")
            st_w2[e_] = sc.dma("gpsimd", lambda g, e_=e_: g.dma_start(
                w2b[:], w2T[e_].rearrange("(kt p) i -> p kt i", p=128)),
                deps=[st_ym[(e_ - 1, 15)]] if e_ >= 1 else [],
                stream=4, tag=f"w2_{e_}")
            st_cwra = sc.dma("sync", lambda e, e_=e_: e.dma_start(
                cwrow[0:1, :], cwhi_s[e_:e_ + 1, :]),
                deps=[st_cwb[e_ - 1]] if e_ >= 1 else [], stream=6,
                tag=f"cwra{e_}")
            st_cwr[e_] = sc.dma("sync", lambda e, e_=e_: e.dma_start(
                cwrow[1:2, :], cwlo_s[e_:e_ + 1, :]),
                deps=[st_cwra], stream=6, tag=f"cwr{e_}")
            dep_psA = [st_sq[(e_ - 1, 15)]] if e_ >= 1 else []
            st_cwb[e_] = sc.step("tensor", lambda t, e_=e_: [
                t.matmul(psA[:, n * 512:(n + 1) * 512], ones2[:],
                         cwrow[0:2, n * 512:(n + 1) * 512], start=True, stop=True)
                for n in range(4)][-1],
                deps=[st_cwr[e_]] + dep_psA, tag=f"cwb{e_}")
            st_cwbe[e_] = sc.step("scalar", lambda s: s.activation(
                cwb_bf[:], psA[:], AF.Copy),
                deps=[st_cwb[e_]] + ([st_hx[(e_ - 1, 15)]] if e_ >= 1 else []),
                tag=f"cwbe{e_}")
            for i in range(8):
                for nh in range(2):
                    gi2 = 2 * i + nh
                    sl = gi2 % 2

                    def hmm(t, e_=e_, par=par, i=i, nh=nh, sl=sl):
                        for k in range(8):
                            for nn in range(2):
                                col = nh * 1024 + nn * 512
                                inst = t.matmul(
                                    psA[:, sl * 1024 + nn * 512:sl * 1024 + nn * 512 + 512],
                                    w1b[:, par * 8192 + k * 1024 + i * 128:
                                        par * 8192 + k * 1024 + (i + 1) * 128],
                                    x_lat[:, k * T + col:k * T + col + 512],
                                    start=(k == 0), stop=(k == 7))
                        return inst
                    dep_h = [st_w1[e_], st_xlr, st_cwbe[e_]]
                    if gi2 >= 2:
                        dep_h.append(st_sq[(e_, gi2 - 2)])
                    elif e_ >= 1:
                        dep_h.append(st_sq[(e_ - 1, 15 - (1 - gi2))])
                    st_hm[(e_, gi2)] = sc.step("tensor", hmm, deps=dep_h,
                                               tag=f"hm{e_}_{gi2}")

                    def sqf(s, i=i, nh=nh, sl=sl):
                        s.activation(trl[:], psA[:, sl * 1024:(sl + 1) * 1024],
                                     AF.Relu)
                        return s.activation(
                            hbuf[:, i * 2048 + nh * 1024:i * 2048 + (nh + 1) * 1024],
                            trl[:], AF.Square)
                    st_sq[(e_, gi2)] = sc.step("scalar", sqf,
                                               deps=[st_hm[(e_, gi2)]],
                                               tag=f"sq{e_}_{gi2}")
                    st_hx[(e_, gi2)] = sc.step(
                        "vector", lambda v, i=i, nh=nh: v.tensor_tensor(
                            hbuf[:, i * 2048 + nh * 1024:i * 2048 + (nh + 1) * 1024],
                            hbuf[:, i * 2048 + nh * 1024:i * 2048 + (nh + 1) * 1024],
                            cwb_bf[:, nh * 1024:(nh + 1) * 1024], ALU.mult),
                        deps=[st_sq[(e_, gi2)], st_cwbe[e_]], tag=f"hx{e_}_{gi2}")
            for l in range(8):
                for nh in range(2):
                    gi2 = 2 * l + nh
                    sl = gi2 % 2

                    def ymm(t, l=l, nh=nh, sl=sl):
                        for k in range(8):
                            for nn in range(2):
                                col = nh * 1024 + nn * 512
                                inst = t.matmul(
                                    psB[:, sl * 1024 + nn * 512:sl * 1024 + nn * 512 + 512],
                                    w2b[:, k * 1024 + l * 128:k * 1024 + (l + 1) * 128],
                                    hbuf[:, k * 2048 + col:k * 2048 + col + 512],
                                    start=(k == 0), stop=(k == 7))
                        return inst
                    dep_y = [st_w2[e_], st_hx[(e_, 15)]]
                    if gi2 >= 2:
                        dep_y.append(st_ya[(e_, gi2 - 2)])
                    elif e_ >= 1:
                        dep_y.append(st_ya[(e_ - 1, 15 - (1 - gi2))])
                    st_ym[(e_, gi2)] = sc.step("tensor", ymm, deps=dep_y,
                                               tag=f"ym{e_}_{gi2}")
                    st_ya[(e_, gi2)] = sc.step(
                        "vector", lambda v, l=l, nh=nh, sl=sl:
                        v.scalar_tensor_tensor(
                            ylat(l)[:, nh * 1024:(nh + 1) * 1024],
                            psA[:, 0:0] if False else
                            psB[:, sl * 1024:(sl + 1) * 1024], 1.0,
                            ylat(l)[:, nh * 1024:(nh + 1) * 1024],
                            ALU.mult, ALU.add),
                        deps=[st_ym[(e_, gi2)], st_y0], tag=f"ya{e_}_{gi2}")
        with nc.Block() as blk:
            sc.emit(blk)

    if "ylat_d" in dbg_t:
        sc = sched()
        sc.dma("sync", lambda e: e.dma_start(dbg_t["ylat_d"], arn(0, 8 * T, F32)),
               stream=0, tag="dump_ylat")
        with nc.Block() as blk:
            sc.emit(blk)

    if phases <= 6:
        _finish_stub(nc, sched, out_d, arn, dsems)
        ctx.close()
        return nc

    with contextlib.ExitStack() as p7:
        um = p7.enter_context(nc.sbuf_tensor("um", [128, 4 * 4096], BF16))
        onm = p7.enter_context(nc.sbuf_tensor("onm", [128, 4 * 4096], BF16))
        fc2sl = p7.enter_context(nc.sbuf_tensor("fc2sl", [128, 2 * 4096], BF16))
        xre = p7.enter_context(nc.sbuf_tensor("xre", [128, 2 * 512], F32))
        outst = p7.enter_context(nc.sbuf_tensor("outst", [128, 2 * 512], F32))
        tsil = p7.enter_context(nc.sbuf_tensor("tsil", [128, 2 * 512], F32))

        sc = sched()
        def ycast(v):
            for l in range(8):
                inst = v.tensor_copy(x_lat[:, l * T:(l + 1) * T], arn(l * T * 4, T, F32))
            return inst
        st_yc = sc.step("vector", ycast, tag="ycast")
        ylb = x_lat

        sdnb = arn(0, 16384, BF16)
        woutb = arn(32768, 16384, BF16)

        st_g = {}
        st_s7 = {}
        st_o2 = {}
        st_uml = {}
        st_onl = {}
        st_tr = {}
        st_trev7 = {}
        st_wl7 = {}
        for nq in range(4):
            parn = nq % 2
            wdep = ([st_g[(nq - 1, 15)], st_s7[(nq - 1, 15)]] if nq >= 1
                    else [st_yc])
            wl_a = sc.dma("gpsimd", lambda g, nq=nq, parn=parn: g.dma_start(
                fc2sl[:, parn * 4096:(parn + 1) * 4096],
                fc2T[:, nq * 512:(nq + 1) * 512].rearrange(
                    "(kt p) q -> p kt q", p=128)),
                deps=wdep, stream=2, tag=f"wl7a_{nq}")
            wl_b = sc.dma("gpsimd", lambda g, nq=nq: g.dma_start(
                sdnb, sdnT[:, nq * 512:(nq + 1) * 512].rearrange(
                    "(kt p) q -> p kt q", p=128)),
                deps=[wl_a], stream=2, tag=f"wl7b_{nq}")
            st_wl7[nq] = sc.dma("gpsimd", lambda g, nq=nq: g.dma_start(
                woutb, woutT[:, nq * 512:(nq + 1) * 512].rearrange(
                    "(kt p) q -> p kt q", p=128)),
                deps=[wl_b], stream=2, tag=f"wl7c_{nq}")
            for m in range(16):
                mi = m % 4
                parm = m % 2
                pu = (nq, m - 4) if m >= 4 else ((nq - 1, m + 12) if nq >= 1 else None)
                st_uml[(nq, m)] = sc.dma("sync", lambda e, m=m, mi=mi: e.dma_start(
                    um[:, mi * 4096:(mi + 1) * 4096],
                    u_s[:, m * 128:(m + 1) * 128].rearrange(
                        "(kt p) q -> p kt q", p=128)),
                    deps=[st_g[pu]] if pu else [], stream=0, tag=f"um{nq}_{m}")
                st_onl[(nq, m)] = sc.dma("sync", lambda e, m=m, mi=mi: e.dma_start(
                    onm[:, mi * 4096:(mi + 1) * 4096],
                    onorm_s[m * 128:(m + 1) * 128, :]),
                    deps=[st_s7[pu]] if pu else [], stream=1, tag=f"on{nq}_{m}")

                def tr7(t, mi=mi):
                    for j in range(32):
                        inst = t.transpose(
                            psA[:, j * 64:(j + 1) * 64].bitcast(BF16),
                            onm[:, mi * 4096 + j * 128:mi * 4096 + (j + 1) * 128],
                            identb[:])
                    return inst
                dep_tr = [st_onl[(nq, m)]]
                pt = (nq, m - 1) if m >= 1 else ((nq - 1, 15) if nq >= 1 else None)
                if pt:
                    dep_tr.append(st_trev7[pt])
                st_tr[(nq, m)] = sc.step("tensor", tr7, deps=dep_tr,
                                         tag=f"tr7_{nq}_{m}")
                st_trev7[(nq, m)] = sc.step("scalar", lambda s, mi=mi: s.activation(
                    onm[:, mi * 4096:(mi + 1) * 4096],
                    psA[:, 0:2048].bitcast(BF16), AF.Copy),
                    deps=[st_tr[(nq, m)]], tag=f"trev7_{nq}_{m}")

                def gmm(t, m=m, mi=mi, parm=parm, parn=parn):
                    for k in range(8):
                        t.matmul(psB[:, parm * 512:(parm + 1) * 512],
                                 ylb[:, k * T + m * 128:k * T + (m + 1) * 128],
                                 fc2sl[:, parn * 4096 + k * 512:
                                       parn * 4096 + (k + 1) * 512],
                                 start=(k == 0), stop=False)
                    for k in range(32):
                        inst = t.matmul(
                            psB[:, parm * 512:(parm + 1) * 512],
                            um[:, mi * 4096 + k * 128:mi * 4096 + (k + 1) * 128],
                            sdnb[:, k * 512:(k + 1) * 512],
                            start=False, stop=(k == 31))
                    return inst

                def smm(t, mi=mi, parm=parm):
                    for k in range(32):
                        inst = t.matmul(
                            psB[:, 1024 + parm * 512:1024 + (parm + 1) * 512],
                            onm[:, mi * 4096 + k * 128:mi * 4096 + (k + 1) * 128],
                            woutb[:, k * 512:(k + 1) * 512],
                            start=(k == 0), stop=(k == 31))
                    return inst
                pg = (nq, m - 2) if m >= 2 else ((nq - 1, m + 14) if nq >= 1 else None)
                depg = [st_wl7[nq], st_yc, st_uml[(nq, m)], st_trev7[(nq, m)]]
                if pg and pg in st_o2:
                    depg.append(st_o2[pg])
                st_g[(nq, m)] = sc.step("tensor", gmm, deps=depg, tag=f"g{nq}_{m}")
                st_s7[(nq, m)] = sc.step("tensor", smm, deps=[st_g[(nq, m)]],
                                         tag=f"s{nq}_{m}")
                st_xl = sc.dma("sync", lambda e, m=m, nq=nq, parm=parm: e.dma_start(
                    xre[:, parm * 512:(parm + 1) * 512],
                    x_in[m * 128:(m + 1) * 128, nq * 512:(nq + 1) * 512]),
                    deps=[st_o2[pg]] if pg and pg in st_o2 else [],
                    stream=4 + parm, tag=f"xl{nq}_{m}")
                st_sil = sc.step("scalar", lambda s, parm=parm: s.activation(
                    tsil[:, parm * 512:(parm + 1) * 512],
                    psB[:, parm * 512:(parm + 1) * 512], AF.Silu),
                    deps=[st_g[(nq, m)]], tag=f"sil{nq}_{m}")

                def o2f(v, parm=parm):
                    v.scalar_tensor_tensor(
                        outst[:, parm * 512:(parm + 1) * 512],
                        tsil[:, parm * 512:(parm + 1) * 512], 1.0,
                        psB[:, 1024 + parm * 512:1024 + (parm + 1) * 512],
                        ALU.mult, ALU.mult)
                    return v.scalar_tensor_tensor(
                        outst[:, parm * 512:(parm + 1) * 512],
                        outst[:, parm * 512:(parm + 1) * 512], 1.0,
                        xre[:, parm * 512:(parm + 1) * 512],
                        ALU.mult, ALU.add)
                st_o2[(nq, m)] = sc.step("vector", o2f,
                                         deps=[st_sil, st_s7[(nq, m)], st_xl],
                                         tag=f"o2_{nq}_{m}")
                sc.dma("sync", lambda e, m=m, nq=nq, parm=parm: e.dma_start(
                    out_d[m * 128:(m + 1) * 128, nq * 512:(nq + 1) * 512],
                    outst[:, parm * 512:(parm + 1) * 512]),
                    deps=[st_o2[(nq, m)]], stream=6 + parm, tag=f"od{nq}_{m}")
        with nc.Block() as blk:
            sc.emit(blk)

    _drain_all(nc, dsems)
    ctx.close()
    return nc




def _finish_stub(nc, sched, out_d, arn, dsems):
    sc = sched()
    sc.dma("sync", lambda e: e.dma_start(out_d[0:128, :], arn(0, 2048, F32)),
           stream=0, tag="stub_out")
    with nc.Block() as blk:
        sc.emit(blk)
    _drain_all(nc, dsems)


def _drain_all(nc, dsems):
    with nc.Block() as blk:
        @blk.sync
        def _(e):
            for ds in dsems:
                if ds.n:
                    e.wait_ge(ds.h, ds.n)


def _host_pack(inputs):
    f32 = np.float32
    g = lambda k: np.asarray(inputs[k], f32)
    conv_w = g("L0_attn_conv_w")
    return {
        "winT": np.ascontiguousarray(g("L0_attn_W_in").T),
        "woutT": np.ascontiguousarray(g("L0_attn_W_out").T),
        "gw_pk": np.ascontiguousarray(
            g("L0_ffn_gate_w").T.reshape(16, 128, 32).transpose(1, 0, 2).reshape(128, 512)),
        "fc1T": np.ascontiguousarray(g("L0_ffn_fc1_w").T),
        "fc2T": np.ascontiguousarray(g("L0_ffn_fc2_w").T),
        "w1T": np.ascontiguousarray(g("L0_ffn_w1").transpose(0, 2, 1)),
        "w2T": np.ascontiguousarray(g("L0_ffn_w2").transpose(0, 2, 1)),
        "supT": np.ascontiguousarray(g("L0_ffn_shared_up_w").T),
        "sdnT": np.ascontiguousarray(g("L0_ffn_shared_down_w").T),
        "convw_p": np.ascontiguousarray(
            conv_w.reshape(48, 128, 4).transpose(1, 0, 2).reshape(128, 192)),
        "convb_p": np.ascontiguousarray(g("L0_attn_conv_b").reshape(48, 128).T),
        "dtb": np.ascontiguousarray(g("L0_attn_dt_bias")[:, None]),
        "alog": np.ascontiguousarray(g("L0_attn_A_log")[:, None]),
        "d_row": np.ascontiguousarray(g("L0_attn_D")[None, :]),
        "nw_row": np.ascontiguousarray(g("L0_attn_norm_weight")[None, :]),
        "gb_row": np.ascontiguousarray(g("L0_ffn_gate_bias")[None, :]),
        "identb_h": np.eye(128, dtype=bfdt),
        "identf_h": np.eye(128, dtype=f32),
        "maskU_h": np.triu(np.ones((128, 128), f32)),
    }


def kernel(**inputs):
    shared = _host_pack(inputs)
    x = np.asarray(inputs["L0_attn_hidden_states"], np.float32)
    nc = _build()
    in_maps = []
    for c in range(8):
        m = dict(shared)
        m["x"] = np.ascontiguousarray(x[c])
        in_maps.append(m)
    res = run_bass_kernel_spmd(nc, in_maps, core_ids=list(range(8)))
    return np.stack([np.asarray(res.results[c]["out"]) for c in range(8)], axis=0)


# version 05
# speedup vs baseline: 1.2052x; 1.2052x over previous
import contextlib

import numpy as np
import ml_dtypes

import concourse.bass as bass
import concourse.mybir as mybir
from concourse.bass_utils import run_bass_kernel_spmd

F32 = mybir.dt.float32
BF16 = mybir.dt.bfloat16
U8 = mybir.dt.uint8
AF = mybir.ActivationFunctionType
ALU = mybir.AluOpType
AX = mybir.AxisListType
bfdt = ml_dtypes.bfloat16

T = 2048
D = 2048
NH = 64
HP = 64
NS = 128
NG = 8
INTER = 4096
PROJ = 10304
CS = 128
NCH = T // CS
NE = 32
LAT = 1024
SHI = 4096
EPS = 1e-5

GATE0 = 0
HBC0 = 4096
DT0 = 10240

_PHASES = 7
_NCH_LIM = None
_P4_LVL = 99
_P4_TREV = 3


class SemC:
    __slots__ = ("h", "n", "last_step")

    def __init__(self, h):
        self.h = h
        self.n = 0
        self.last_step = None


class Step:
    __slots__ = ("engine", "emit", "deps", "needs_event", "sem", "val",
                 "is_dma", "tag")

    def __init__(self, engine, emit, deps, is_dma, tag):
        self.engine = engine
        self.emit = emit
        assert all(d is not None for d in deps), (tag, 'None dep')
        self.deps = list(deps)
        self.needs_event = False
        self.sem = None
        self.val = None
        self.is_dma = is_dma
        self.tag = tag


class Sched:
    ENGINES = ("sync", "gpsimd", "vector", "scalar", "tensor")

    def __init__(self, nc, esems, dsems, done):
        self.nc = nc
        self.esems = esems
        self.dsems = dsems
        self.done = done
        self.steps = {e: [] for e in self.ENGINES}

    def step(self, engine, emit, deps=(), tag=""):
        st = Step(engine, emit, deps, False, tag)
        self.steps[engine].append(st)
        return st

    def dma(self, engine, emit, deps=(), stream=0, tag=""):
        st = Step(engine, emit, deps, True, tag)
        st.sem = self.dsems[stream]
        if st.sem.last_step is not None:
            st.deps.append(st.sem.last_step)
        st.sem.last_step = st
        self.steps[engine].append(st)
        return st

    def emit(self, blk):
        for e in self.ENGINES:
            for st in self.steps[e]:
                for d in st.deps:
                    d.needs_event = True
        for e in self.ENGINES:
            for st in self.steps[e]:
                if st.is_dma:
                    st.val = st.sem.n + 16
                    st.sem.n = st.val
                elif st.needs_event:
                    st.sem = self.esems[e]
                    st.val = st.sem.n + 1
                    st.sem.n = st.val
        self._simulate()

        def body_for(engine):
            steps = self.steps[engine]

            def body(eng):
                waited = {}
                for st in steps:
                    for d in st.deps:
                        assert d.sem is not None, (st.tag, d.tag)
                        key = id(d.sem)
                        if waited.get(key, -1) < d.val:
                            eng.wait_ge(d.sem.h, d.val)
                            waited[key] = d.val
                    inst = st.emit(eng)
                    if st.is_dma:
                        inst.then_inc(st.sem.h, 16)
                    elif st.needs_event:
                        inst.then_inc(st.sem.h, 1)
            return body

        for engine in self.ENGINES:
            if self.steps[engine]:
                getattr(blk, engine)(body_for(engine))
        for e in self.ENGINES:
            for st in self.steps[e]:
                self.done.add(id(st))

    def _simulate(self):
        local_done = set(self.done)
        ptr = {e: 0 for e in self.ENGINES}
        progressed = True
        while progressed:
            progressed = False
            for e in self.ENGINES:
                while ptr[e] < len(self.steps[e]):
                    st = self.steps[e][ptr[e]]
                    if any(id(d) not in local_done for d in st.deps):
                        break
                    local_done.add(id(st))
                    ptr[e] += 1
                    progressed = True
        stuck = [(e, self.steps[e][ptr[e]].tag)
                 for e in self.ENGINES if ptr[e] < len(self.steps[e])]
        if stuck:
            raise RuntimeError(f"schedule deadlock at: {stuck}")


def _build(dbg=frozenset(), phases=None):
    if phases is None:
        phases = _PHASES
    nc = bass.Bass("TRN2", target_bir_lowering=False,
                   detect_race_conditions=False)
    ctx = contextlib.ExitStack()

    def dram_in(name, shape, dt=F32):
        return nc.dram_tensor(name, shape, dt, kind="ExternalInput").ap()

    def scratch(name, shape, dt):
        kind = "ExternalOutput" if name in dbg else "Internal"
        return nc.dram_tensor(name, shape, dt, kind=kind).ap()

    x_in = dram_in("x", [T, D])
    winT = dram_in("winT", [D, PROJ])
    woutT = dram_in("woutT", [INTER, D])
    gw_pk = dram_in("gw_pk", [128, 512])
    fc1T = dram_in("fc1T", [D, LAT])
    fc2T = dram_in("fc2T", [LAT, D])
    w1T = dram_in("w1T", [NE, LAT, LAT])
    w2T = dram_in("w2T", [NE, LAT, LAT])
    supT = dram_in("supT", [D, SHI])
    sdnT = dram_in("sdnT", [SHI, D])
    convw_p = dram_in("convw_p", [128, 192])
    convb_p = dram_in("convb_p", [128, 48])
    dtb_in = dram_in("dtb", [NH, 1])
    alog_in = dram_in("alog", [NH, 1])
    d_row_in = dram_in("d_row", [1, NH])
    nw_row_in = dram_in("nw_row", [1, INTER])
    gb_row_in = dram_in("gb_row", [1, NE])
    identb_in = dram_in("identb_h", [128, 128], BF16)
    identf_in = dram_in("identf_h", [128, 128])
    maskU_in = dram_in("maskU_h", [128, 128])

    out_d = nc.dram_tensor("out", [T, D], F32, kind="ExternalOutput").ap()

    xT_s = scratch("xT_s", [T, INTER], BF16)
    BT_s = scratch("BT_s", [T, NG * NS], BF16)
    Bf_s = scratch("Bf_s", [NG * NS, T], BF16)
    C_s = scratch("C_s", [NG * NS, T], BF16)
    gate_s = scratch("gate_s", [T, INTER], F32)
    u_s = scratch("u_s", [SHI, T], BF16)
    onorm_s = scratch("onorm_s", [T, INTER], BF16)
    cwhi_s = scratch("cwhi_s", [NE, T], BF16)
    cwlo_s = scratch("cwlo_s", [NE, T], BF16)
    acs_d = scratch("acs_d", [2, 2 * NH * CS], BF16)
    xlat_s = scratch("xlat_s", [LAT, T], BF16)

    dbg_t = {}
    for nm, shape, dt in [("xn_d", [128, 16 * T], BF16),
                          ("dt_d", [NH, T], F32),
                          ("xlat_d", [128, 8 * T], BF16),
                          ("cw_d", [NE, T], F32),
                          ("ylat_d", [128, 8 * T], F32),
                          ("ych_d", [NCH * 128, INTER], F32)]:
        if nm in dbg:
            dbg_t[nm] = nc.dram_tensor(nm, shape, dt, kind="ExternalOutput").ap()

    identb = nc.alloc_sbuf_tensor("identb", [128, 128], BF16)
    identf = nc.alloc_sbuf_tensor("identf", [128, 128], F32)
    maskU = nc.alloc_sbuf_tensor("maskU", [128, 128], F32)
    ones2 = nc.alloc_sbuf_tensor("ones2", [2, 128], BF16)
    ones1f = nc.alloc_sbuf_tensor("ones1f", [1, 128], F32)
    ones64 = nc.alloc_sbuf_tensor("ones64", [NH, CS], F32)
    dtbc = nc.alloc_sbuf_tensor("dtbc", [NH, 1], F32)
    expA = nc.alloc_sbuf_tensor("expA", [NH, 1], F32)
    alog_t = nc.alloc_sbuf_tensor("alog_t", [NH, 1], F32)
    d_rep = nc.alloc_sbuf_tensor("d_rep", [128, NH], F32)
    nw_bc = nc.alloc_sbuf_tensor("nw_bc", [128, INTER], BF16)
    gb_bc = nc.alloc_sbuf_tensor("gb_bc", [128, NE], F32)
    epsc = nc.alloc_sbuf_tensor("epsc", [128, 1], F32)
    x_lat = nc.alloc_sbuf_tensor("x_lat", [128, 8 * T], BF16)
    arena = nc.alloc_sbuf_tensor("arena", [128, 65536], U8)

    def arn(off_bytes, n, dt):
        step = 2 if dt == BF16 else 4
        return arena[:, off_bytes:off_bytes + n * step].bitcast(dt)

    def xn_t(k):
        return arn(k * T * 2, T, BF16)

    def xn_all():
        return arn(0, 16 * T, BF16)

    psA = nc.alloc_psum_tensor("psA", [128, 2048], F32)
    psB = nc.alloc_psum_tensor("psB", [128, 2048], F32)

    esems = {e: SemC(ctx.enter_context(nc.semaphore("se_" + e)))
             for e in Sched.ENGINES}
    dsems = [SemC(ctx.enter_context(nc.semaphore(f"sd{i}"))) for i in range(10)]
    fsems = {e: SemC(ctx.enter_context(nc.semaphore("sf_" + e)))
             for e in ("vector", "scalar", "gpsimd")}
    DONE = set()

    def mkfence(name, eng):
        fs = fsems[name]

        def F(inst):
            fs.n += 1
            inst.then_inc(fs.h, 1)
            eng.wait_ge(fs.h, fs.n)
            return inst
        return F

    def sched():
        return Sched(nc, esems, dsems, DONE)

    p0_stack = contextlib.ExitStack()
    nwrow = p0_stack.enter_context(nc.sbuf_tensor("nwrow", [1, INTER], F32))

    sc0 = sched()
    ld = []
    for dst, src in [(identb[:], identb_in), (identf[:], identf_in),
                     (maskU[:], maskU_in), (dtbc[:], dtb_in),
                     (alog_t[:], alog_in), (d_rep[0:1, :], d_row_in),
                     (gb_bc[0:1, :], gb_row_in), (nwrow[:], nw_row_in)]:
        ld.append(sc0.dma("sync", lambda e, d=dst, s=src: e.dma_start(d, s),
                          stream=0, tag="c0"))
    sc0.step("vector", lambda v: v.memset(ones2[:], 1.0))
    st_o1 = sc0.step("vector", lambda v: v.memset(ones1f[:], 1.0), tag="ones1f")
    sc0.step("vector", lambda v: v.memset(ones64[:], 1.0))
    sc0.step("vector", lambda v: v.memset(epsc[:], EPS))
    st_ea = sc0.step("scalar", lambda s: s.activation(expA[:], alog_t[:], AF.Exp),
                     deps=[ld[6]], tag="expA")
    def _bc1(t):
        t.matmul(psA[:, 0:64], ones1f[:], d_rep[0:1, :], start=True, stop=True)
        t.matmul(psA[:, 64:96], ones1f[:], gb_bc[0:1, :], start=True, stop=True)
        for n in range(4):
            i = t.matmul(psB[:, n * 512:(n + 1) * 512], ones1f[:],
                         nwrow[:, n * 512:(n + 1) * 512], start=True, stop=True)
        return i

    st_bc = sc0.step("tensor", _bc1, deps=ld[7:10] + [ld[0], st_o1], tag="bcasts")
    st_ev1 = sc0.step("vector", lambda v: (
        v.tensor_copy(d_rep[:], psA[:, 0:64]),
        v.tensor_copy(gb_bc[:], psA[:, 64:96]),
        v.tensor_copy(nw_bc[:, 0:2048], psB[:]))[-1],
                      deps=[st_bc], tag="bcast_evac1")
    st_bc2 = sc0.step("tensor", lambda t: [
        t.matmul(psB[:, n * 512:(n + 1) * 512], ones1f[:],
                 nwrow[:, 2048 + n * 512:2048 + (n + 1) * 512],
                 start=True, stop=True) for n in range(4)][-1],
                      deps=[st_ev1], tag="bcasts2")
    sc0.step("vector", lambda v: v.tensor_copy(nw_bc[:, 2048:4096], psB[:]),
             deps=[st_bc2], tag="bcast_evac2")
    with nc.Block() as blk:
        sc0.emit(blk)
    p0_stack.close()

    with contextlib.ExitStack() as p1:
        xld = p1.enter_context(nc.sbuf_tensor("xld", [128, 2 * D], F32))
        xtm = p1.enter_context(nc.sbuf_tensor("xtm", [128, 2 * D], BF16))
        stat = p1.enter_context(nc.sbuf_tensor("stat", [128, 16], F32))
        junk = p1.enter_context(nc.sbuf_tensor("junk", [128, D], F32))

        def stc(pp, i):
            return stat[:, 8 * pp + i:8 * pp + i + 1]

        def trreg(pp):
            return (psA if pp == 0 else psB)[:, 0:1024].bitcast(BF16)

        sc = sched()
        st_load = [None] * 16
        st_a3 = [None] * 16
        st_tr = [None] * 16
        st_ev = [None] * 16
        for m in range(16):
            pp = m % 2
            st_load[m] = sc.dma(
                "sync",
                lambda e, m=m, pp=pp: e.dma_start(
                    xld[:, pp * D:(pp + 1) * D], x_in[m * 128:(m + 1) * 128, :]),
                deps=[st_a3[m - 2]] if m >= 2 else [],
                stream=pp, tag=f"xld{m}")
            xv = xld[:, pp * D:(pp + 1) * D]
            dep_free = [st_a3[m - 2]] if m >= 2 else []
            def v_stat(v, xv=xv, pp=pp):
                F = mkfence("vector", v)
                v.tensor_scalar(junk[:], xv, 1.0, 0.0, ALU.mult, ALU.add,
                                accum_out=stc(pp, 1))
                F(v.scalar_tensor_tensor(junk[:], xv, 1.0, xv, ALU.mult, ALU.mult,
                                         accum_out=stc(pp, 0)))
                F(v.tensor_scalar(stc(pp, 2), stc(pp, 1), 1.0 / D, None, ALU.mult))
                return F(v.tensor_scalar(stc(pp, 3), stc(pp, 0), 1.0 / D, None,
                                         ALU.mult))

            sc.step("vector", v_stat, deps=[st_load[m]] + dep_free,
                    tag=f"v_stat{m}")
            st_v5 = sc.step("vector", lambda v, pp=pp: v.scalar_tensor_tensor(
                stc(pp, 4), stc(pp, 2), stc(pp, 2), stc(pp, 3), ALU.mult,
                ALU.subtract), tag=f"v5_{m}")
            def a2f(s, pp=pp):
                F = mkfence("scalar", s)
                F(s.activation(stc(pp, 5), stc(pp, 4), AF.Ln, bias=epsc[:, 0:1],
                               scale=-1.0))
                return s.activation(stc(pp, 6), stc(pp, 5), AF.Exp, scale=-0.5)

            st_a2 = sc.step("scalar", a2f, deps=[st_v5], tag=f"a2_{m}")
            st_v6 = sc.step("vector", lambda v, pp=pp: v.tensor_scalar(
                stc(pp, 7), stc(pp, 2), stc(pp, 6), -1.0, ALU.mult, ALU.mult),
                            deps=[st_a2], tag=f"v6_{m}")
            st_a3[m] = sc.step("scalar", lambda s, pp=pp: s.activation(
                xtm[:, pp * D:(pp + 1) * D], xld[:, pp * D:(pp + 1) * D],
                AF.Identity, bias=stc(pp, 7), scale=stc(pp, 6)),
                               deps=[st_v6], tag=f"a3_{m}")
            st_tr[m] = sc.step("tensor", lambda t, m=m, pp=pp: [
                t.transpose(trreg(pp)[:, j * 128:(j + 1) * 128],
                            xtm[:, pp * D + j * 128:pp * D + (j + 1) * 128],
                            identb[:]) for j in range(16)][-1],
                deps=[st_a3[m]] + ([st_ev[m - 2]] if m >= 2 else []),
                tag=f"tr{m}")
            st_ev[m] = sc.step("vector", lambda v, m=m, pp=pp: v.tensor_copy(
                xn_all().rearrange("p (j t) -> p j t", t=T)[
                    :, :, m * 128:(m + 1) * 128],
                trreg(pp).rearrange("p (j q) -> p j q", q=128)),
                deps=[st_tr[m]], tag=f"ev{m}")
        with nc.Block() as blk:
            sc.emit(blk)

    if "xn_d" in dbg_t:
        sc = sched()
        sc.dma("sync", lambda e: e.dma_start(dbg_t["xn_d"], xn_all()),
               stream=0, tag="dump_xn")
        with nc.Block() as blk:
            sc.emit(blk)

    p24 = ctx.enter_context(contextlib.ExitStack())
    dt_raw = p24.enter_context(nc.sbuf_tensor("dt_raw", [NH, T], F32))

    with contextlib.ExitStack() as p2:
        wdt = p2.enter_context(nc.sbuf_tensor("wdt", [128, 16 * 64], BF16))
        convw = p2.enter_context(nc.sbuf_tensor("convw", [128, 192], F32))
        convb = p2.enter_context(nc.sbuf_tensor("convb", [128, 48], F32))
        wsl = p2.enter_context(nc.sbuf_tensor("wsl", [128, 2 * 2048], BF16))
        cvin = p2.enter_context(nc.sbuf_tensor("cvin", [128, 2 * 2051], F32))
        tA = p2.enter_context(nc.sbuf_tensor("tA", [128, 2048], F32))
        tB = p2.enter_context(nc.sbuf_tensor("tB", [128, 2048], F32))
        tC = p2.enter_context(nc.sbuf_tensor("tC", [128, 2048], F32))
        tD = p2.enter_context(nc.sbuf_tensor("tD", [128, 2048], F32))

        NSLAB = 48

        def kindof(s):
            return "x" if s < 32 else ("b" if s < 40 else "c")

        sc = sched()
        st_wl = [None] * NSLAB
        st_mm = [None] * NSLAB
        st_cvev = [None] * NSLAB
        st_tap = [None] * NSLAB
        st_w = [None] * NSLAB
        st_silu = [None] * NSLAB
        st_tp = [None] * NSLAB
        st_tev = [None] * NSLAB
        st_fl = [None] * NSLAB
        st_fc = [None] * NSLAB

        st_pad = sc.step("vector", lambda v: (
            v.memset(cvin[:, 0:3], 0.0),
            v.memset(cvin[:, 2051:2054], 0.0))[-1], tag="cvpad")
        st_wdt = sc.dma("gpsimd", lambda g: g.dma_start(
            wdt[:], winT[:, DT0:DT0 + 64].rearrange("(kt p) m -> p kt m", p=128)),
            stream=2, tag="wdt")
        st_cw1 = sc.dma("sync", lambda e: e.dma_start(convw[:], convw_p),
                        stream=8, tag="convw")
        st_cw2 = sc.dma("sync", lambda e: e.dma_start(convb[:], convb_p),
                        stream=8, tag="convb")

        def wload(g, s):
            pp = s % 2
            return g.dma_start(
                wsl[:, pp * 2048:(pp + 1) * 2048],
                winT[:, HBC0 + s * 128:HBC0 + (s + 1) * 128].rearrange(
                    "(kt p) m -> p kt m", p=128))

        st_dtmm = sc.step("tensor", lambda t: [
            t.matmul(psB[0:64, n * 512:(n + 1) * 512], wdt[:, k * 64:(k + 1) * 64],
                     xn_t(k)[:, n * 512:(n + 1) * 512],
                     start=(k == 0), stop=(k == 15))
            for k in range(16) for n in range(4)][-1],
                          deps=[st_wdt], tag="dtmm")
        st_dtev = sc.step("vector", lambda v: v.tensor_copy(dt_raw[:], psB[0:64, :]),
                          deps=[st_dtmm], tag="dtev")

        for s in range(NSLAB):
            pp = s % 2
            kd = kindof(s)
            st_wl[s] = sc.dma("gpsimd", lambda g, s=s: wload(g, s),
                              deps=[st_mm[s - 2]] if s >= 2 else [],
                              stream=2 + pp, tag=f"wl{s}")
            free = []
            if s == 1:
                free.append(st_dtev)
            if s >= 2:
                free.append(st_cvev[s - 2])
                if kindof(s - 2) != "c":
                    free.append(st_tev[s - 2])
            ps = psA if pp == 0 else psB
            st_mm[s] = sc.step("tensor", lambda t, s=s, pp=pp, ps=ps: [
                t.matmul(ps[:, n * 512:(n + 1) * 512],
                         wsl[:, pp * 2048 + k * 128:pp * 2048 + (k + 1) * 128],
                         xn_t(k)[:, n * 512:(n + 1) * 512],
                         start=(k == 0), stop=(k == 15))
                for k in range(16) for n in range(4)][-1],
                deps=[st_wl[s]] + free, tag=f"mm{s}")
            cv = cvin[:, pp * 2051:(pp + 1) * 2051]
            st_cvev[s] = sc.step("vector", lambda v, cv=cv, ps=ps: v.tensor_copy(
                cv[:, 3:2051], ps[:]),
                deps=[st_mm[s]] + ([st_silu[s - 1]] if s >= 1 else []),
                tag=f"cvev{s}")
            st_tap[s] = sc.step("gpsimd", lambda g, cv=cv, s=s: (
                g.tensor_tensor(tC[:], cv[:, 3:2051],
                                convw[:, 4 * s + 3:4 * s + 4].broadcast_to(
                                    [128, 2048]), ALU.mult),
                g.tensor_tensor(tD[:], cv[:, 2:2050],
                                convw[:, 4 * s + 2:4 * s + 3].broadcast_to(
                                    [128, 2048]), ALU.mult),
                g.tensor_add(tC[:], tC[:], tD[:]))[-1],
                deps=[st_cvev[s], st_cw1] + ([st_w[s - 1]] if s >= 1 else []),
                tag=f"tap{s}")
            st_w[s] = sc.step("vector", lambda v, cv=cv, s=s: (
                v.tensor_scalar(tA[:], cv[:, 1:2049], convw[:, 4 * s + 1:4 * s + 2],
                                convb[:, s:s + 1], ALU.mult, ALU.add),
                v.scalar_tensor_tensor(tB[:], cv[:, 0:2048],
                                       convw[:, 4 * s:4 * s + 1], tA[:],
                                       ALU.mult, ALU.add),
                v.tensor_add(tA[:], tB[:], tC[:]))[-1],
                deps=[st_tap[s], st_cw1, st_cw2] +
                ([st_silu[s - 1]] if s >= 1 else []),
                tag=f"w{s}")
            sxfree = []
            if s >= 2:
                k2 = kindof(s - 2)
                if k2 in ("b", "c"):
                    sxfree.append(st_fc[s - 2])
                if k2 != "c":
                    sxfree.append(st_tp[s - 2])
            st_silu[s] = sc.step("scalar", lambda sce, s=s, pp=pp: sce.activation(
                x_lat[:, pp * 2048:(pp + 1) * 2048], tA[:], AF.Silu),
                deps=[st_w[s]] + sxfree, tag=f"silu{s}")
            if kd in ("b", "c"):
                dst = (Bf_s[(s - 32) * 128:(s - 31) * 128, :] if kd == "b"
                       else C_s[(s - 40) * 128:(s - 39) * 128, :])
                st_fc[s] = sc.dma("sync", lambda e, dst=dst, pp=pp: e.dma_start(
                    dst, x_lat[:, pp * 2048:(pp + 1) * 2048]),
                    deps=[st_silu[s]], stream=4 + pp, tag=f"fc{s}")
            if kd != "c":
                ps = psA if pp == 0 else psB
                st_tp[s] = sc.step("tensor", lambda t, pp=pp, ps=ps: [
                    t.transpose(ps[:, 0:1024].bitcast(BF16)[:, j * 128:(j + 1) * 128],
                                x_lat[:, pp * 2048 + j * 128:pp * 2048 + (j + 1) * 128],
                                identb[:]) for j in range(16)][-1],
                    deps=[st_silu[s], st_cvev[s]], tag=f"tp{s}")
                hp = (s // 2) % 2
                dst = x_lat[:, 4096 + hp * 4096:4096 + (hp + 1) * 4096].rearrange(
                    "p (j q) -> p j q", q=256)[:, :, pp * 128:pp * 128 + 128]
                stage_free = [st_fl[2 * (s // 2 - 2) + 1]] if s // 2 >= 2 else []
                st_tev[s] = sc.step("scalar", lambda sce, dst=dst, pp=pp, ps=ps:
                                    sce.activation(dst, ps[:, 0:1024].bitcast(
                                        BF16).rearrange("p (j q) -> p j q", q=128),
                                        AF.Copy),
                                    deps=[st_tp[s]] + stage_free, tag=f"tev{s}")
                if s % 2 == 1:
                    hp = (s // 2) % 2
                    src = x_lat[:, 4096 + hp * 4096:4096 + (hp + 1) * 4096].rearrange(
                        "p (j q) -> p j q", q=256)
                    if kd == "x":
                        dstd = xT_s[:, (s - 1) * 128:(s + 1) * 128].rearrange(
                            "(j p) q -> p j q", p=128)
                    else:
                        dstd = BT_s[:, (s - 33) * 128:(s - 31) * 128].rearrange(
                            "(j p) q -> p j q", p=128)
                    st_fl[s] = sc.dma("sync", lambda e, dstd=dstd, src=src:
                                      e.dma_start(dstd, src),
                                      deps=[st_tev[s], st_tev[s - 1]],
                                      stream=6 + hp, tag=f"fl{s}")
        with nc.Block() as blk:
            sc.emit(blk)

    if "dt_d" in dbg_t:
        sc = sched()
        sc.dma("sync", lambda e: e.dma_start(dbg_t["dt_d"], dt_raw[:]), stream=0,
               tag="dump_dt")
        with nc.Block() as blk:
            sc.emit(blk)

    with contextlib.ExitStack() as p2b:
        wg = p2b.enter_context(nc.sbuf_tensor("wg", [128, 2 * 16 * 512], BF16))
        gst = p2b.enter_context(nc.sbuf_tensor("gst", [128, 2 * 512], F32))

        sc = sched()
        st_wgl = [None] * 8
        st_gmm = {}
        st_gev = {}
        st_gdm = {}
        for ng in range(8):
            pp = ng % 2
            deps = [st_gmm[ng * 16 - 17]] if ng >= 2 else []
            st_wgl[ng] = sc.dma("gpsimd", lambda g, ng=ng, pp=pp: g.dma_start(
                wg[:, pp * 8192:(pp + 1) * 8192],
                winT[:, ng * 512:(ng + 1) * 512].rearrange(
                    "(kt p) m -> p kt m", p=128)),
                deps=deps, stream=2 + pp, tag=f"wgl{ng}")
            for m in range(16):
                gi = ng * 16 + m
                sl = gi % 2
                psv = psA[:, sl * 512:(sl + 1) * 512]
                prev_ev = st_gev.get(gi - 2)
                st_gmm[gi] = sc.step("tensor", lambda t, m=m, pp=pp, psv=psv: [
                    t.matmul(psv, xn_t(k)[:, m * 128:(m + 1) * 128],
                             wg[:, pp * 8192 + k * 512:pp * 8192 + (k + 1) * 512],
                             start=(k == 0), stop=(k == 15)) for k in range(16)][-1],
                    deps=[st_wgl[ng]] + ([prev_ev] if prev_ev else []),
                    tag=f"gmm{gi}")
                prev_dm = st_gdm.get(gi - 2)
                st_gev[gi] = sc.step("scalar", lambda s, sl=sl, psv=psv:
                                     s.activation(gst[:, sl * 512:(sl + 1) * 512],
                                                  psv, AF.Copy),
                                     deps=[st_gmm[gi]] +
                                     ([prev_dm] if prev_dm else []),
                                     tag=f"gev{gi}")
                st_gdm[gi] = sc.dma("sync", lambda e, ng=ng, m=m, sl=sl:
                                    e.dma_start(
                                        gate_s[m * 128:(m + 1) * 128,
                                               ng * 512:(ng + 1) * 512],
                                        gst[:, sl * 512:(sl + 1) * 512]),
                                    deps=[st_gev[gi]], stream=4 + sl,
                                    tag=f"gdm{gi}")
        with nc.Block() as blk:
            sc.emit(blk)

    if phases <= 2:
        _finish_stub(nc, sched, out_d, arn, dsems)
        ctx.close()
        return nc

    with contextlib.ExitStack() as p3:
        gwb = p3.enter_context(nc.sbuf_tensor("gwb", [128, 512], BF16))
        rw = p3.enter_context(nc.sbuf_tensor("rw", [128, 2 * 144], F32))
        cwT = p3.enter_context(nc.sbuf_tensor("cwT", [NE, T], F32))
        cwbf = p3.enter_context(nc.sbuf_tensor("cwbf", [NE, 2 * T], BF16))
        cwlf = p3.enter_context(nc.sbuf_tensor("cwlf", [NE, T], F32))
        wstr = p3.enter_context(nc.sbuf_tensor("wstr", [128, 2 * 2048], BF16))
        tr32 = p3.enter_context(nc.sbuf_tensor("tr32", [128, 2 * 2048], F32))
        usq = p3.enter_context(nc.sbuf_tensor("usq", [128, 2 * 2048], BF16))
        spacer = p3.enter_context(nc.sbuf_tensor("spacer", [128, 32], F32))

        sc = sched()
        st_gw = sc.dma("gpsimd", lambda g: g.dma_start(gwb[:], gw_pk), stream=2,
                       tag="gwb")
        st_rt = [None] * 16
        st_cwev = [None] * 16

        def psl_v(sl):
            return psB[:, 1024 + sl * 512:1024 + sl * 512 + 32]

        def pst_v(sl):
            return psB[0:32, 1024 + sl * 512 + 32:1024 + sl * 512 + 160]

        for m in range(16):
            pp = m % 2
            b = pp * 144
            sl = m % 2
            dep_ps = [st_cwev[m - 2]] if m >= 2 else []
            st_lg = sc.step("tensor", lambda t, m=m, sl=sl: [
                t.matmul(psl_v(sl), xn_t(k)[:, m * 128:(m + 1) * 128],
                         gwb[:, k * 32:(k + 1) * 32],
                         start=(k == 0), stop=(k == 15)) for k in range(16)][-1],
                deps=[st_gw] + dep_ps, tag=f"lg{m}")
            st_sig = sc.step("scalar", lambda s, b=b, sl=sl: s.activation(
                rw[:, b:b + 32], psl_v(sl), AF.Sigmoid),
                deps=[st_lg], tag=f"sig{m}")

            def routing(v, b=b):
                w = rw
                F = mkfence("vector", v)
                F(v.tensor_add(w[:, b + 32:b + 64], w[:, b:b + 32], gb_bc[:]))
                for g in range(4):
                    F(v.max(out=w[:, b + 104 + 8 * g:b + 112 + 8 * g],
                            in_=w[:, b + 32 + 8 * g:b + 40 + 8 * g]))
                F(v.memset(w[:, b + 72:b + 76], -1e30))
                for g in range(4):
                    F(v.tensor_add(w[:, b + 64 + g:b + 65 + g],
                                   w[:, b + 104 + 8 * g:b + 105 + 8 * g],
                                   w[:, b + 105 + 8 * g:b + 106 + 8 * g]))
                F(v.tensor_copy(w[:, b + 68:b + 72], w[:, b + 64:b + 68]))
                F(v.max(out=w[:, b + 84:b + 92], in_=w[:, b + 68:b + 76]))
                F(v.tensor_scalar(w[:, b + 92:b + 96], w[:, b + 64:b + 68],
                                  w[:, b + 85:b + 86], None, ALU.is_ge))
                F(v.tensor_scalar(w[:, b + 96:b + 100], w[:, b + 92:b + 96],
                                  1e30, -1e30, ALU.mult, ALU.add))
                for g in range(4):
                    F(v.scalar_tensor_tensor(
                        w[:, b + 32 + 8 * g:b + 40 + 8 * g],
                        w[:, b + 96 + g:b + 97 + g].broadcast_to([128, 8]),
                        1.0, w[:, b + 32 + 8 * g:b + 40 + 8 * g],
                        ALU.mult, ALU.add))
                F(v.max(out=w[:, b + 76:b + 84], in_=w[:, b + 32:b + 64]))
                F(v.tensor_scalar(w[:, b + 32:b + 64], w[:, b + 32:b + 64],
                                  w[:, b + 83:b + 84], None, ALU.is_ge))
                F(v.scalar_tensor_tensor(w[:, b + 104:b + 136], w[:, b + 32:b + 64],
                                         1.0, w[:, b:b + 32], ALU.mult, ALU.mult,
                                         accum_out=w[:, b + 100:b + 101]))
                F(v.reciprocal(w[:, b + 101:b + 102], w[:, b + 100:b + 101]))
                return v.tensor_scalar(w[:, b + 32:b + 64], w[:, b + 104:b + 136],
                                       w[:, b + 101:b + 102], 2.5, ALU.mult,
                                       ALU.mult)

            st_rt[m] = sc.step("vector", routing, deps=[st_sig], tag=f"rt{m}")
            st_ct = sc.step("tensor", lambda t, b=b, sl=sl: t.transpose(
                pst_v(sl), rw[:, b + 32:b + 64], identf[:]),
                deps=[st_rt[m]], tag=f"ct{m}")
            st_cwev[m] = sc.step("vector", lambda v, m=m, sl=sl: v.tensor_copy(
                cwT[:, m * 128:(m + 1) * 128], pst_v(sl)),
                deps=[st_ct], tag=f"cwev{m}")
        st_hl = sc.step("vector", lambda v: (
            v.tensor_copy(cwbf[:, 0:T], cwT[:]),
            v.tensor_copy(cwlf[:], cwbf[:, 0:T]),
            v.tensor_sub(cwlf[:], cwT[:], cwlf[:]),
            v.tensor_copy(cwbf[:, T:2 * T], cwlf[:]))[-1],
            deps=[st_cwev[15]], tag="cwhl")
        sc.dma("sync", lambda e: e.dma_start(cwhi_s, cwbf[:, 0:T]),
               deps=[st_hl], stream=6, tag="cwd1")
        sc.dma("sync", lambda e: e.dma_start(cwlo_s, cwbf[:, T:2 * T]),
               deps=[st_hl], stream=7, tag="cwd2")
        if "cw_d" in dbg_t:
            sc.dma("sync", lambda e: e.dma_start(dbg_t["cw_d"], cwT[:]),
                   deps=[st_hl], stream=6, tag="dump_cw")

        st_f1l = [None] * 8
        st_f1m = [None] * 8
        st_f1e = [None] * 8
        for mj in range(8):
            pp = mj % 2
            st_f1l[mj] = sc.dma("gpsimd", lambda g, mj=mj, pp=pp: g.dma_start(
                wstr[:, pp * 2048:(pp + 1) * 2048],
                fc1T[:, mj * 128:(mj + 1) * 128].rearrange(
                    "(kt p) m -> p kt m", p=128)),
                deps=[st_f1m[mj - 2]] if mj >= 2 else [], stream=2 + pp,
                tag=f"f1l{mj}")
            st_f1m[mj] = sc.step("tensor", lambda t, mj=mj, pp=pp: [
                t.matmul(psA[:, n * 512:(n + 1) * 512],
                         wstr[:, pp * 2048 + k * 128:pp * 2048 + (k + 1) * 128],
                         xn_t(k)[:, n * 512:(n + 1) * 512],
                         start=(k == 0), stop=(k == 15))
                for k in range(16) for n in range(4)][-1],
                deps=[st_f1l[mj]] + ([st_f1e[mj - 1]] if mj >= 1 else []),
                tag=f"f1m{mj}")
            st_f1e[mj] = sc.step("scalar", lambda s, mj=mj: s.activation(
                x_lat[:, mj * T:(mj + 1) * T], psA[:], AF.Copy),
                deps=[st_f1m[mj]], tag=f"f1e{mj}")
            sc.dma("sync", lambda e, mj=mj: e.dma_start(
                xlat_s[mj * 128:(mj + 1) * 128, :].rearrange(
                    "(a p) q -> p a q", p=128).squeeze(1),
                x_lat[:, mj * T:(mj + 1) * T]),
                deps=[st_f1e[mj]], stream=8, tag=f"f1d{mj}")
        with nc.Block() as blk:
            sc.emit(blk)

        if "xlat_d" in dbg_t:
            sc = sched()
            sc.dma("sync", lambda e: e.dma_start(dbg_t["xlat_d"], x_lat[:]),
                   stream=0, tag="dump_xlat")
            with nc.Block() as blk:
                sc.emit(blk)

        sc = sched()
        st_sul = [None] * 32
        st_sum = [None] * 32
        st_sue = [None] * 32
        st_sud = [None] * 32
        for mj in range(32):
            pp = mj % 2
            ps = psA if pp == 0 else psB
            st_sul[mj] = sc.dma("gpsimd", lambda g, mj=mj, pp=pp: g.dma_start(
                wstr[:, pp * 2048:(pp + 1) * 2048],
                supT[:, mj * 128:(mj + 1) * 128].rearrange(
                    "(kt p) m -> p kt m", p=128)),
                deps=[st_sum[mj - 2]] if mj >= 2 else [], stream=2 + pp,
                tag=f"sul{mj}")
            st_sum[mj] = sc.step("tensor", lambda t, mj=mj, pp=pp, ps=ps: [
                t.matmul(ps[:, n * 512:(n + 1) * 512],
                         wstr[:, pp * 2048 + k * 128:pp * 2048 + (k + 1) * 128],
                         xn_t(k)[:, n * 512:(n + 1) * 512],
                         start=(k == 0), stop=(k == 15))
                for k in range(16) for n in range(4)][-1],
                deps=[st_sul[mj]] + ([st_sue[mj - 2]] if mj >= 2 else []),
                tag=f"sum{mj}")
            st_sue[mj] = sc.step("scalar", lambda s, mj=mj, pp=pp, ps=ps: (
                s.activation(tr32[:, pp * 2048:(pp + 1) * 2048], ps[:], AF.Relu),
                s.activation(usq[:, pp * 2048:(pp + 1) * 2048],
                             tr32[:, pp * 2048:(pp + 1) * 2048], AF.Square))[-1],
                deps=[st_sum[mj]] + ([st_sud[mj - 2]] if mj >= 2 else []),
                tag=f"sue{mj}")
            st_sud[mj] = sc.dma("sync", lambda e, mj=mj, pp=pp: e.dma_start(
                u_s[mj * 128:(mj + 1) * 128, :],
                usq[:, pp * 2048:(pp + 1) * 2048]),
                deps=[st_sue[mj]], stream=4 + pp, tag=f"sud{mj}")
        with nc.Block() as blk:
            sc.emit(blk)

    if phases <= 3:
        _finish_stub(nc, sched, out_d, arn, dsems)
        ctx.close()
        return nc

    with contextlib.ExitStack() as p4:
        xTc2 = p4.enter_context(nc.sbuf_tensor("xTc2", [128, 2 * 4096], BF16))
        BTc2 = p4.enter_context(nc.sbuf_tensor("BTc2", [128, 2 * 1024], BF16))
        Bfc2 = p4.enter_context(nc.sbuf_tensor("Bfc2", [128, 2 * 1024], BF16))
        Cc2 = p4.enter_context(nc.sbuf_tensor("Cc2", [128, 2 * 1024], BF16))
        gatec2 = p4.enter_context(nc.sbuf_tensor("gatec2", [128, 2 * 4096], F32))
        dtw = p4.enter_context(nc.sbuf_tensor("dtw", [128, 10 * CS], F32))
        hilo = p4.enter_context(nc.sbuf_tensor("hilo", [NH, 2 * CS], BF16))
        tmw = p4.enter_context(nc.sbuf_tensor("tmw", [128, 5 * 64], F32))
        cdrep = p4.enter_context(nc.sbuf_tensor("cdrep", [128, 64], F32))
        sct = x_lat[:, 12288:13312].bitcast(F32)
        EB = p4.enter_context(nc.sbuf_tensor("EB", [128, 2 * 1024], BF16))
        SB2 = p4.enter_context(nc.sbuf_tensor("SB2", [128, 2 * 1024], BF16))
        SMb = p4.enter_context(nc.sbuf_tensor("SMb", [128, 2 * 1024], BF16))
        CSD = p4.enter_context(nc.sbuf_tensor("CSD", [128, 2 * 1024], BF16))
        carry_f = x_lat[:, 0:8192].bitcast(F32)
        carry_bf = x_lat[:, 8192:12288]
        vrw = p4.enter_context(nc.sbuf_tensor("vrw", [128, 24], F32))
        tn512 = x_lat[:, 13312:14336].bitcast(F32)
        sqj = x_lat[:, 14336:15360].bitcast(F32)

        xdtT = arn(0, 4096, BF16)
        xddT = arn(8192, 4096, BF16)
        t0full = arn(16384, 4096, F32)
        Ych = arn(32768, 4096, F32)
        onstage = arn(16384, 4096, BF16)
        acsfl = arena[0:2, 49152:49152 + 4096].bitcast(BF16)
        acs_last = arena[0:2, 53248:53248 + 256].bitcast(BF16)

        def dw(i):
            return dtw[0:NH, i * CS:(i + 1) * CS]

        def dwf(i):
            return dtw[:, i * CS:(i + 1) * CS]

        def tw(i):
            return tmw[:, i * 64:(i + 1) * 64]

        def ldsl(buf, par, w):
            return buf[:, par * w:(par + 1) * w]

        sc = sched()

        def mb(lvl, fn):
            return fn if _P4_LVL >= lvl else (lambda e: e.nop())

        st_init_cf = sc.step("vector", lambda v: v.memset(carry_f[:], 0.0),
                             tag="cf0")
        st_dtw0 = sc.step("vector", lambda v: v.memset(dtw[64:128, :], 0.0),
                          tag="dtw0")
        st_init_cb = sc.step("gpsimd", lambda g: g.memset(carry_bf[:], 0.0),
                             tag="cb0")

        st_ld = {}
        st_px = [None] * NCH
        st_ych = [None] * NCH
        st_gm = [None] * NCH
        st_onrm = [None] * NCH
        st_dmon = [None] * NCH
        st_trev = [None] * NCH
        st_cdexp = [None] * NCH
        st_fl3 = [None] * NCH
        st_sc_l = {}
        st_seg_l = {}
        st_sct_l = {}
        st_esgb = {}
        st_sm = {}
        st_csd = {}
        st_y = {}
        st_stt = {}
        st_t0 = {}
        st_carry = {}
        st_silu = [None] * NCH

        nch_lim = _NCH_LIM if _NCH_LIM is not None else NCH
        for c in range(nch_lim):
            par = c % 2
            cs0, cs1 = c * CS, (c + 1) * CS
            dep_x = [st_ych[c - 2]] if c >= 2 else []
            dep_bt = [st_stt[(c - 2, 7)]] if c >= 2 else []
            dep_bf = [st_sc_l[(c - 2, 7)]] if c >= 2 else []
            dep_c = [st_y[(c - 2, 7)], st_csd[(c - 2, 7)]] if c >= 2 else []
            dep_g = [st_gm[c - 2]] if c >= 2 else []
            l_x = sc.dma("sync", lambda e, c=c, par=par: e.dma_start(
                ldsl(xTc2, par, 4096), xT_s[c * CS:(c + 1) * CS, :]),
                deps=dep_x, stream=par, tag=f"lx{c}")
            l_bt = sc.dma("sync", lambda e, c=c, par=par: e.dma_start(
                ldsl(BTc2, par, 1024), BT_s[c * CS:(c + 1) * CS, :]),
                deps=dep_bt, stream=par, tag=f"lbt{c}")
            l_bf = sc.dma("sync", lambda e, c=c, par=par: e.dma_start(
                ldsl(Bfc2, par, 1024),
                Bf_s[:, c * CS:(c + 1) * CS].rearrange("(gt p) s -> p gt s", p=128)),
                deps=dep_bf, stream=par, tag=f"lbf{c}")
            l_c = sc.dma("sync", lambda e, c=c, par=par: e.dma_start(
                ldsl(Cc2, par, 1024),
                C_s[:, c * CS:(c + 1) * CS].rearrange("(gt p) s -> p gt s", p=128)),
                deps=dep_c, stream=par, tag=f"lc{c}")
            l_g = sc.dma("sync", lambda e, c=c, par=par: e.dma_start(
                ldsl(gatec2, par, 4096), gate_s[c * CS:(c + 1) * CS, :]),
                deps=dep_g, stream=par, tag=f"lg{c}")

            def dt1(v, cs0=cs0, cs1=cs1):
                return v.tensor_scalar(dw(0), dt_raw[:, cs0:cs1], dtbc[:, 0:1],
                                       None, ALU.add)
            st_dt1 = sc.step("vector", mb(1, dt1), tag=f"dt1_{c}")

            def act1(s):
                F = mkfence("scalar", s)
                F(s.activation(dw(1), dw(0), AF.Abs))
                F(s.activation(dw(2), dw(1), AF.Exp, scale=-1.0))
                return s.activation(dw(3), dw(2), AF.Ln, bias=1.0)
            st_act1 = sc.step("scalar", mb(1, act1), deps=[st_dt1], tag=f"act1_{c}")

            def dt2(v):
                F = mkfence("vector", v)
                F(v.tensor_scalar(dw(4), dw(0), 0.0, None, ALU.max))
                F(v.tensor_add(dw(5), dw(4), dw(3)))
                F(v.tensor_scalar(dw(6), dw(5), expA[:, 0:1], -1.0, ALU.mult,
                                  ALU.mult))
                F(v.tensor_tensor_scan(dw(7), ones64[:], dw(6), 0.0, ALU.mult,
                                       ALU.add))
                F(v.tensor_copy(hilo[:, 0:CS], dw(7)))
                F(v.tensor_copy(dw(8), hilo[:, 0:CS]))
                F(v.tensor_sub(dw(8), dw(7), dw(8)))
                F(v.tensor_copy(hilo[:, CS:2 * CS], dw(8)))
                return v.tensor_scalar(dw(8), dw(7), dw(7)[:, CS - 1:CS], None,
                                       ALU.subtract)
            st_dt2 = sc.step("vector", mb(1, dt2), deps=[st_act1], tag=f"dt2_{c}")

            def act2(s):
                return s.activation(dw(9), dw(8), AF.Exp, scale=-1.0)
            st_act2 = sc.step("scalar", mb(1, act2), deps=[st_dt2], tag=f"act2_{c}")

            adp = acs_d[:, par * NH * CS:(par + 1) * NH * CS]
            g1 = sc.dma("gpsimd", lambda g, adp=adp: g.dma_start(
                adp[0:1, :].rearrange("a (h l) -> a h l", h=NH).squeeze(0),
                hilo[:, 0:CS]),
                deps=[st_dt2] + ([st_esgb[(c - 2, 7)], st_cdexp[c - 2]]
                                 if c >= 2 else []),
                stream=2, tag=f"fh{c}")
            g2 = sc.dma("gpsimd", lambda g, adp=adp: g.dma_start(
                adp[1:2, :].rearrange("a (h l) -> a h l", h=NH).squeeze(0),
                hilo[:, CS:2 * CS]), deps=[st_dt2], stream=2, tag=f"fl{c}")
            st_fl3[c] = sc.dma("gpsimd", lambda g, adp=adp, par=par: g.dma_start(
                acs_last[0:2, par * 64:(par + 1) * 64],
                adp[:, CS - 1::CS], allow_slow_non_contiguous=True),
                deps=[g1, g2] + ([st_cdexp[c - 1]] if c >= 1 else []),
                stream=2, tag=f"fa{c}")
            st_flg = {}

            def mkflg(g_, adp=adp, g1=g1, g2=g2, c=c):
                Gg = c * 8 + g_
                prev = st_seg_l.get(Gg - 2)
                st_flg[g_] = sc.dma(
                    "gpsimd", lambda g, adp=adp, g_=g_, parG=Gg % 2: g.dma_start(
                        acsfl[0:2, parG * 1024:(parG + 1) * 1024],
                        adp[:, g_ * 8 * CS:(g_ + 1) * 8 * CS]),
                    deps=[g1, g2] + ([prev] if prev else []),
                    stream=7, tag=f"flg{Gg}")

            mkflg(0)
            mkflg(1)

            dep_b4 = [st_sct_l[(c - 1, 7)], st_cdexp[c - 1]] if c >= 1 else []
            def tr1(t, c=c):
                t.transpose(psB[:, 0:128], dwf(7), identf[:])
                t.transpose(psB[:, 128:256], dwf(5), identf[:])
                t.transpose(psB[:, 256:384], dwf(9), identf[:])
                return t.matmul(psB[:, 384:448], ones2[:],
                                acs_last[0:2, par * 64:(par + 1) * 64],
                                start=True, stop=True)
            st_tr1 = sc.step("tensor", mb(3, tr1),
                             deps=[st_dt2, st_act2, st_fl3[c]] + dep_b4,
                             tag=f"tr1_{c}")

            def trev(v):
                if _P4_TREV == 1:
                    v.tensor_copy(tw(0), psB[:, 0:64])
                    v.tensor_copy(tw(2), psB[:, 128:192])
                    v.tensor_copy(tw(3), psB[:, 256:320])
                    v.tensor_scalar(tw(1), tw(0), -1.0, None, ALU.mult)
                    return v.tensor_tensor(tw(4), tw(2), tw(3), ALU.mult)
                F = mkfence("vector", v)
                F(v.tensor_copy(tw(0), psB[:, 0:64]))
                F(v.tensor_scalar(tw(1), tw(0), -1.0, None, ALU.mult))
                F(v.tensor_copy(tw(2), psB[:, 128:192]))
                F(v.tensor_copy(tw(3), psB[:, 256:320]))
                return v.tensor_tensor(tw(4), tw(2), tw(3), ALU.mult)
            st_trev[c] = sc.step("vector", trev, deps=[st_tr1], tag=f"trev{c}")

            st_cdexp[c] = sc.step("scalar", lambda s: s.activation(
                cdrep[:], psB[:, 384:448], AF.Exp),
                deps=[st_tr1, st_trev[c]], tag=f"cde{c}")

            def px(g, par=par):
                xr = ldsl(xTc2, par, 4096).rearrange("p (h q) -> p h q", q=64)
                g.tensor_tensor(xdtT.rearrange("p (h q) -> p h q", q=64), xr,
                                tw(2).unsqueeze(2).broadcast_to([128, 64, 64]),
                                ALU.mult)
                return g.tensor_tensor(
                    xddT.rearrange("p (h q) -> p h q", q=64), xr,
                    tw(4).unsqueeze(2).broadcast_to([128, 64, 64]), ALU.mult)
            st_px[c] = sc.step("gpsimd", mb(4, px),
                               deps=[st_trev[c], l_x] +
                               ([st_y[(c - 1, 7)], st_stt[(c - 1, 7)]]
                                if c >= 1 else []),
                               tag=f"px{c}")

            for g in range(8):
                G = c * 8 + g
                parG = G % 2
                h0 = 8 * g

                def pe_seg(t, parG=parG):
                    for hp in range(8):
                        i = t.matmul(
                            psA[:, parG * 1024 + hp * 128:parG * 1024 + (hp + 1) * 128],
                            ones2[:],
                            acsfl[0:2, parG * 1024 + hp * 128:parG * 1024 + (hp + 1) * 128],
                            start=True, stop=True)
                    return i
                dep_seg = [st_flg[g]] + ([st_esgb[divmod(G - 2, 8)]]
                                         if G >= 2 else [])
                st_seg = sc.step("tensor", mb(5, pe_seg), deps=dep_seg, tag=f"seg{G}")
                st_seg_l[G] = st_seg
                if g + 2 < 8:
                    mkflg(g + 2)

                dep_sc = [l_bf, l_c] + ([st_sct_l[divmod(G - 1, 8)]]
                                        if G >= 1 else [])
                if g == 0:
                    dep_sc += [st_trev[c], st_cdexp[c]]
                st_sc_l[(c, g)] = sc.step("tensor", lambda t, par=par, g=g: t.matmul(
                    psB[:, 0:128], ldsl(Bfc2, par, 1024)[:, g * 128:(g + 1) * 128],
                    ldsl(Cc2, par, 1024)[:, g * 128:(g + 1) * 128],
                    start=True, stop=True),
                    deps=dep_sc, tag=f"sc{G}")

                def v_sct(v, parG=parG):
                    F = mkfence("vector", v)
                    F(v.tensor_tensor(
                        sct[:, parG * 128:(parG + 1) * 128], psB[:, 0:128],
                        maskU[:], ALU.mult))
                    return v.engine_nop()
                st_sct_l[(c, g)] = sc.step("vector", mb(5, v_sct),
                                           deps=[st_sc_l[(c, g)]], tag=f"sctm{G}")

                def a_esgb(s, parG=parG, h0=h0):
                    for hp in range(8):
                        pseg = psA[:, parG * 1024 + hp * 128:parG * 1024 + (hp + 1) * 128]
                        s.activation(EB[:, parG * 1024 + hp * 128:parG * 1024 + (hp + 1) * 128],
                                     pseg, AF.Exp, bias=tw(1)[:, h0 + hp - h0 + 0:0] if False
                                     else tw(1)[:, hp + h0 - h0:hp + 1] if False
                                     else tmw[:, 64 + h0 + hp:64 + h0 + hp + 1])
                        i = s.activation(SB2[:, parG * 1024 + hp * 128:parG * 1024 + (hp + 1) * 128],
                                         pseg, AF.Exp)
                    return i
                st_esgb[(c, g)] = sc.step(
                    "scalar", mb(5, a_esgb),
                    deps=[st_seg, st_trev[c]] +
                    ([st_sm[divmod(G - 2, 8)], st_csd[divmod(G - 2, 8)]]
                     if G >= 2 else []),
                    tag=f"esgb{G}")

                def v_sm(v, parG=parG):
                    for hp in range(8):
                        i = v.scalar_tensor_tensor(
                            SMb[:, parG * 1024 + hp * 128:parG * 1024 + (hp + 1) * 128],
                            EB[:, parG * 1024 + hp * 128:parG * 1024 + (hp + 1) * 128],
                            1.0, sct[:, parG * 128:(parG + 1) * 128].unsqueeze(1)
                            .broadcast_to([128, 1, 128]).squeeze(1) if False
                            else sct[:, parG * 128:(parG + 1) * 128],
                            ALU.min, ALU.mult)
                    return i
                st_sm[(c, g)] = sc.step("vector", mb(6, v_sm),
                                        deps=[st_esgb[(c, g)], st_sct_l[(c, g)]],
                                        tag=f"sm{G}")

                def g_csd(g_, par=par, parG=parG, g=g):
                    for hp in range(8):
                        i = g_.tensor_tensor(
                            CSD[:, parG * 1024 + hp * 128:parG * 1024 + (hp + 1) * 128],
                            ldsl(Cc2, par, 1024)[:, g * 128:(g + 1) * 128],
                            SB2[:, parG * 1024 + hp * 128:parG * 1024 + (hp + 1) * 128],
                            ALU.mult)
                    return i
                st_csd[(c, g)] = sc.step("gpsimd", mb(6, g_csd),
                                         deps=[st_esgb[(c, g)], l_c],
                                         tag=f"csd{G}")

                def pe_y(t, parG=parG, h0=h0):
                    for hp in range(8):
                        ps = psB[:, 512 + parG * 1024 + hp * 64:
                                 512 + parG * 1024 + (hp + 1) * 64]
                        t.matmul(ps,
                                 SMb[:, parG * 1024 + hp * 128:parG * 1024 + (hp + 1) * 128],
                                 xdtT[:, (h0 + hp) * 64:(h0 + hp + 1) * 64],
                                 start=True, stop=False)
                        i = t.matmul(ps,
                                     CSD[:, parG * 1024 + hp * 128:parG * 1024 + (hp + 1) * 128],
                                     carry_bf[:, (h0 + hp) * 64:(h0 + hp + 1) * 64],
                                     start=False, stop=True)
                    return i
                dep_y = [st_sm[(c, g)], st_csd[(c, g)], st_px[c]]
                dep_y += [st_init_cb] if c == 0 else [st_cb]
                dep_y += [st_t0[divmod(G - 2, 8)]] if G >= 2 else []
                st_y[(c, g)] = sc.step("tensor", mb(7, pe_y), deps=dep_y, tag=f"y{G}")

                def pe_st(t, par=par, g=g, h0=h0):
                    for hp in range(8):
                        i = t.matmul(
                            psB[:, 1024 + hp * 64:1024 + (hp + 1) * 64],
                            ldsl(BTc2, par, 1024)[:, g * 128:(g + 1) * 128],
                            xddT[:, (h0 + hp) * 64:(h0 + hp + 1) * 64],
                            start=True, stop=True)
                    return i
                dep_st = [st_px[c], l_bt] + ([st_carry[divmod(G - 1, 8)]]
                                             if G >= 1 else [])
                st_stt[(c, g)] = sc.step("tensor", mb(7, pe_st), deps=dep_st,
                                         tag=f"st{G}")

                st_t0[(c, g)] = sc.step("scalar", mb(8, lambda s, parG=parG, g=g:
                                        s.activation(
                    t0full[:, g * 512:(g + 1) * 512],
                    psB[:, 512 + parG * 1024:512 + parG * 1024 + 512], AF.Copy)),
                    deps=[st_y[(c, g)]] +
                    ([st_ych[c - 1]] if c >= 1 and g == 0 else []) +
                    ([st_dmon[c - 1]] if c >= 1 and g <= 3 else []),
                    tag=f"t0_{G}")

                def v_carry(v, h0=h0):
                    for hp in range(8):
                        i = v.scalar_tensor_tensor(
                            carry_f[:, (h0 + hp) * 64:(h0 + hp + 1) * 64],
                            carry_f[:, (h0 + hp) * 64:(h0 + hp + 1) * 64],
                            cdrep[:, h0 + hp:h0 + hp + 1],
                            psB[:, 1024 + hp * 64:1024 + (hp + 1) * 64],
                            ALU.mult, ALU.add)
                    return i
                st_carry[(c, g)] = sc.step(
                    "vector", mb(8, v_carry),
                    deps=[st_stt[(c, g)], st_cdexp[c]] +
                    ([st_init_cf] if c == 0 and g == 0 else []),
                    tag=f"carry{G}")

            st_cb = sc.step("gpsimd", mb(8, lambda g: g.tensor_copy(
                carry_bf[:], carry_f[:])),
                deps=[st_carry[(c, 7)]], tag=f"cb{c}")
            st_silu[c] = sc.step("scalar", mb(9, lambda s, par=par: s.activation(
                ldsl(gatec2, par, 4096), ldsl(gatec2, par, 4096), AF.Silu)),
                deps=[l_g], tag=f"silu{c}")

            def g_ych(g, par=par):
                xr = ldsl(xTc2, par, 4096).rearrange("p (h q) -> p h q", q=64)
                g.tensor_tensor(Ych.rearrange("p (h q) -> p h q", q=64), xr,
                                d_rep[:].unsqueeze(2).broadcast_to([128, 64, 64]),
                                ALU.mult)
                return g.tensor_add(Ych[:], Ych[:], t0full[:])
            st_ych[c] = sc.step("gpsimd", mb(9, g_ych),
                                deps=[st_t0[(c, 7)], l_x] +
                                ([st_onrm[c - 1]] if c >= 1 else []),
                                tag=f"ych{c}")
            st_gm[c] = sc.step("gpsimd", mb(9, lambda g, par=par: g.tensor_tensor(
                Ych[:], Ych[:], ldsl(gatec2, par, 4096), ALU.mult)),
                deps=[st_ych[c], st_silu[c]], tag=f"gm{c}")

            def v_var(v):
                for ng in range(8):
                    i = v.scalar_tensor_tensor(
                        sqj[:], Ych[:, ng * 512:(ng + 1) * 512], 1.0,
                        Ych[:, ng * 512:(ng + 1) * 512], ALU.mult, ALU.mult,
                        accum_out=vrw[:, ng:ng + 1])
                return i
            st_var = sc.step("vector", mb(9, v_var), deps=[st_gm[c]], tag=f"var{c}")

            def a_rstd(s):
                F = mkfence("scalar", s)
                F(s.activation(vrw[:, 8:16], vrw[:, 0:8], AF.Ln,
                               bias=epsc[:, 0:1], scale=1.0 / 512))
                return s.activation(vrw[:, 8:16], vrw[:, 8:16], AF.Exp,
                                    scale=-0.5)
            st_rstd = sc.step("scalar", mb(9, a_rstd), deps=[st_var], tag=f"rstd{c}")

            def g_onrm(g):
                for ng in range(8):
                    g.tensor_tensor(tn512[:], Ych[:, ng * 512:(ng + 1) * 512],
                                    vrw[:, 8 + ng:9 + ng].broadcast_to([128, 512]),
                                    ALU.mult)
                    i = g.tensor_tensor(onstage[:, ng * 512:(ng + 1) * 512],
                                        tn512[:], nw_bc[:, ng * 512:(ng + 1) * 512],
                                        ALU.mult)
                return i
            st_onrm[c] = sc.step("gpsimd", mb(9, g_onrm),
                                 deps=[st_rstd, st_gm[c]] +
                                 ([st_dmon[c - 1]] if c >= 1 else []),
                                 tag=f"onrm{c}")
            st_dmon[c] = sc.dma("sync", lambda e, c=c: e.dma_start(
                onorm_s[c * CS:(c + 1) * CS, :], onstage[:]),
                deps=[st_onrm[c]], stream=4 + par, tag=f"dmon{c}")
            if "ych_d" in dbg_t:
                sc.dma("sync", lambda e, c=c: e.dma_start(
                    dbg_t["ych_d"][c * CS:(c + 1) * CS, :], Ych[:]),
                    deps=[st_gm[c], st_var], stream=6, tag=f"dych{c}")

        with nc.Block() as blk:
            sc.emit(blk)

    if phases <= 4:
        _finish_stub(nc, sched, out_d, arn, dsems)
        ctx.close()
        return nc

    with contextlib.ExitStack() as p6:
        w1b = p6.enter_context(nc.sbuf_tensor("w1b", [128, 2 * 8192], BF16))
        w2b = p6.enter_context(nc.sbuf_tensor("w2b", [128, 8192], BF16))
        hbuf = p6.enter_context(nc.sbuf_tensor("hbuf", [128, 8 * 2048], BF16))
        cwrow = p6.enter_context(nc.sbuf_tensor("cwrow", [2, 2048], BF16))
        cwb_bf = p6.enter_context(nc.sbuf_tensor("cwb_bf", [128, 2048], BF16))
        trl = p6.enter_context(nc.sbuf_tensor("trl", [128, 1024], F32))

        def ylat(l):
            return arn(l * T * 4, T, F32)

        sc = sched()
        st_xlr = sc.dma("sync", lambda e: e.dma_start(
            x_lat[:].rearrange("p (t q) -> p t q", q=T),
            xlat_s.rearrange("(t p) q -> p t q", p=128)), stream=0, tag="xlr")
        st_y0 = sc.step("vector", lambda v: v.memset(arn(0, 8 * T, F32), 0.0),
                        tag="y0")

        st_w1 = [None] * NE
        st_w2 = [None] * NE
        st_cwr = [None] * NE
        st_cwb = [None] * NE
        st_cwbe = [None] * NE
        st_hm = {}
        st_sq = {}
        st_hx = {}
        st_ym = {}
        st_ya = {}
        for e_ in range(NE):
            par = e_ % 2
            st_w1[e_] = sc.dma("gpsimd", lambda g, e_=e_, par=par: g.dma_start(
                w1b[:, par * 8192:(par + 1) * 8192],
                w1T[e_].rearrange("(kt p) i -> p kt i", p=128)),
                deps=[st_hm[(e_ - 2, 15)]] if e_ >= 2 else [],
                stream=2 + par, tag=f"w1_{e_}")
            st_w2[e_] = sc.dma("gpsimd", lambda g, e_=e_: g.dma_start(
                w2b[:], w2T[e_].rearrange("(kt p) i -> p kt i", p=128)),
                deps=[st_ym[(e_ - 1, 15)]] if e_ >= 1 else [],
                stream=4, tag=f"w2_{e_}")
            st_cwra = sc.dma("sync", lambda e, e_=e_: e.dma_start(
                cwrow[0:1, :], cwhi_s[e_:e_ + 1, :]),
                deps=[st_cwb[e_ - 1]] if e_ >= 1 else [], stream=6,
                tag=f"cwra{e_}")
            st_cwr[e_] = sc.dma("sync", lambda e, e_=e_: e.dma_start(
                cwrow[1:2, :], cwlo_s[e_:e_ + 1, :]),
                deps=[st_cwra], stream=6, tag=f"cwr{e_}")
            dep_psA = [st_sq[(e_ - 1, 15)]] if e_ >= 1 else []
            st_cwb[e_] = sc.step("tensor", lambda t, e_=e_: [
                t.matmul(psA[:, n * 512:(n + 1) * 512], ones2[:],
                         cwrow[0:2, n * 512:(n + 1) * 512], start=True, stop=True)
                for n in range(4)][-1],
                deps=[st_cwr[e_]] + dep_psA, tag=f"cwb{e_}")
            st_cwbe[e_] = sc.step("scalar", lambda s: s.activation(
                cwb_bf[:], psA[:], AF.Copy),
                deps=[st_cwb[e_]] + ([st_hx[(e_ - 1, 15)]] if e_ >= 1 else []),
                tag=f"cwbe{e_}")
            for i in range(8):
                for nh in range(2):
                    gi2 = 2 * i + nh
                    sl = gi2 % 2

                    def hmm(t, e_=e_, par=par, i=i, nh=nh, sl=sl):
                        for k in range(8):
                            for nn in range(2):
                                col = nh * 1024 + nn * 512
                                inst = t.matmul(
                                    psA[:, sl * 1024 + nn * 512:sl * 1024 + nn * 512 + 512],
                                    w1b[:, par * 8192 + k * 1024 + i * 128:
                                        par * 8192 + k * 1024 + (i + 1) * 128],
                                    x_lat[:, k * T + col:k * T + col + 512],
                                    start=(k == 0), stop=(k == 7))
                        return inst
                    dep_h = [st_w1[e_], st_xlr, st_cwbe[e_]]
                    if gi2 >= 2:
                        dep_h.append(st_sq[(e_, gi2 - 2)])
                    elif e_ >= 1:
                        dep_h.append(st_sq[(e_ - 1, 15 - (1 - gi2))])
                    st_hm[(e_, gi2)] = sc.step("tensor", hmm, deps=dep_h,
                                               tag=f"hm{e_}_{gi2}")

                    def sqf(s, i=i, nh=nh, sl=sl):
                        s.activation(trl[:], psA[:, sl * 1024:(sl + 1) * 1024],
                                     AF.Relu)
                        return s.activation(
                            hbuf[:, i * 2048 + nh * 1024:i * 2048 + (nh + 1) * 1024],
                            trl[:], AF.Square)
                    st_sq[(e_, gi2)] = sc.step("scalar", sqf,
                                               deps=[st_hm[(e_, gi2)]],
                                               tag=f"sq{e_}_{gi2}")
                    st_hx[(e_, gi2)] = sc.step(
                        "vector", lambda v, i=i, nh=nh: v.tensor_tensor(
                            hbuf[:, i * 2048 + nh * 1024:i * 2048 + (nh + 1) * 1024],
                            hbuf[:, i * 2048 + nh * 1024:i * 2048 + (nh + 1) * 1024],
                            cwb_bf[:, nh * 1024:(nh + 1) * 1024], ALU.mult),
                        deps=[st_sq[(e_, gi2)], st_cwbe[e_]], tag=f"hx{e_}_{gi2}")
            for l in range(8):
                for nh in range(2):
                    gi2 = 2 * l + nh
                    sl = gi2 % 2

                    def ymm(t, l=l, nh=nh, sl=sl):
                        for k in range(8):
                            for nn in range(2):
                                col = nh * 1024 + nn * 512
                                inst = t.matmul(
                                    psB[:, sl * 1024 + nn * 512:sl * 1024 + nn * 512 + 512],
                                    w2b[:, k * 1024 + l * 128:k * 1024 + (l + 1) * 128],
                                    hbuf[:, k * 2048 + col:k * 2048 + col + 512],
                                    start=(k == 0), stop=(k == 7))
                        return inst
                    dep_y = [st_w2[e_], st_hx[(e_, 15)]]
                    if gi2 >= 2:
                        dep_y.append(st_ya[(e_, gi2 - 2)])
                    elif e_ >= 1:
                        dep_y.append(st_ya[(e_ - 1, 15 - (1 - gi2))])
                    st_ym[(e_, gi2)] = sc.step("tensor", ymm, deps=dep_y,
                                               tag=f"ym{e_}_{gi2}")
                    st_ya[(e_, gi2)] = sc.step(
                        "vector", lambda v, l=l, nh=nh, sl=sl:
                        v.scalar_tensor_tensor(
                            ylat(l)[:, nh * 1024:(nh + 1) * 1024],
                            psA[:, 0:0] if False else
                            psB[:, sl * 1024:(sl + 1) * 1024], 1.0,
                            ylat(l)[:, nh * 1024:(nh + 1) * 1024],
                            ALU.mult, ALU.add),
                        deps=[st_ym[(e_, gi2)], st_y0], tag=f"ya{e_}_{gi2}")
        with nc.Block() as blk:
            sc.emit(blk)

    if "ylat_d" in dbg_t:
        sc = sched()
        sc.dma("sync", lambda e: e.dma_start(dbg_t["ylat_d"], arn(0, 8 * T, F32)),
               stream=0, tag="dump_ylat")
        with nc.Block() as blk:
            sc.emit(blk)

    if phases <= 6:
        _finish_stub(nc, sched, out_d, arn, dsems)
        ctx.close()
        return nc

    with contextlib.ExitStack() as p7:
        um = p7.enter_context(nc.sbuf_tensor("um", [128, 4 * 4096], BF16))
        onm = p7.enter_context(nc.sbuf_tensor("onm", [128, 4 * 4096], BF16))
        fc2sl = p7.enter_context(nc.sbuf_tensor("fc2sl", [128, 2 * 4096], BF16))
        xre = p7.enter_context(nc.sbuf_tensor("xre", [128, 2 * 512], F32))
        outst = p7.enter_context(nc.sbuf_tensor("outst", [128, 2 * 512], F32))
        tsil = p7.enter_context(nc.sbuf_tensor("tsil", [128, 2 * 512], F32))

        sc = sched()
        def ycast(v):
            for l in range(8):
                inst = v.tensor_copy(x_lat[:, l * T:(l + 1) * T], arn(l * T * 4, T, F32))
            return inst
        st_yc = sc.step("vector", ycast, tag="ycast")
        ylb = x_lat

        sdnb = arn(0, 16384, BF16)
        woutb = arn(32768, 16384, BF16)

        st_tr = {}
        st_trev7 = {}
        st_g = {}
        st_s7 = {}
        st_o2 = {}
        st_od = {}
        st_wl7 = {}
        st_uml = {}
        st_onl = {}
        last_wl = {}
        for mb in range(4):
            for mi in range(4):
                m = mb * 4 + mi
                parm = m % 2
                dep_u = [st_g[(mb - 1, 3, mb * 4 - 2 + mi - 2)]] if False else []
                st_uml[m] = sc.dma("sync", lambda e, m=m, mi=mi: e.dma_start(
                    um[:, mi * 4096:(mi + 1) * 4096],
                    u_s[:, m * 128:(m + 1) * 128].rearrange(
                        "(kt p) q -> p kt q", p=128)),
                    deps=[st_g[(mb - 2, 3, (mb - 2) * 4 + mi)]] if mb >= 2 else [],
                    stream=0, tag=f"um{m}")
                st_onl[m] = sc.dma("sync", lambda e, m=m, mi=mi: e.dma_start(
                    onm[:, mi * 4096:(mi + 1) * 4096],
                    onorm_s[m * 128:(m + 1) * 128, :]),
                    deps=[st_s7[(mb - 2, 3, (mb - 2) * 4 + mi)]] if mb >= 2 else [],
                    stream=1, tag=f"on{m}")
                def tr7(t, mi=mi):
                    for j in range(32):
                        inst = t.transpose(
                            psA[:, j * 64:(j + 1) * 64].bitcast(BF16),
                            onm[:, mi * 4096 + j * 128:mi * 4096 + (j + 1) * 128],
                            identb[:])
                    return inst
                dep_tr = [st_onl[m]]
                if mb >= 1:
                    dep_tr.append(st_trev7[(mb - 1) * 4 + 3])
                if m >= 1:
                    dep_tr.append(st_trev7[m - 1])
                st_tr[m] = sc.step("tensor", tr7, deps=dep_tr, tag=f"tr7_{m}")
                st_trev7[m] = sc.step("scalar", lambda s, mi=mi: s.activation(
                    onm[:, mi * 4096:(mi + 1) * 4096],
                    psA[:, 0:2048].bitcast(BF16), AF.Copy),
                    deps=[st_tr[m]], tag=f"trev7_{m}")
            for nq in range(4):
                parn = nq % 2
                if nq >= 1:
                    pmb, pnq = mb, nq - 1
                elif mb >= 1:
                    pmb, pnq = mb - 1, 3
                else:
                    pmb = None
                wdep = ([st_g[(pmb, pnq, pmb * 4 + 3)],
                         st_s7[(pmb, pnq, pmb * 4 + 3)]]
                        if pmb is not None else [st_yc])
                wl_a = sc.dma("gpsimd", lambda g, nq=nq, parn=parn: g.dma_start(
                    fc2sl[:, parn * 4096:(parn + 1) * 4096],
                    fc2T[:, nq * 512:(nq + 1) * 512].rearrange(
                        "(kt p) q -> p kt q", p=128)),
                    deps=wdep, stream=2, tag=f"wl7a_{mb}_{nq}")
                wl_b = sc.dma("gpsimd", lambda g, nq=nq: g.dma_start(
                    sdnb, sdnT[:, nq * 512:(nq + 1) * 512].rearrange(
                        "(kt p) q -> p kt q", p=128)),
                    deps=[wl_a], stream=2, tag=f"wl7b_{mb}_{nq}")
                st_wl7[(mb, nq)] = sc.dma("gpsimd", lambda g, nq=nq: g.dma_start(
                    woutb, woutT[:, nq * 512:(nq + 1) * 512].rearrange(
                        "(kt p) q -> p kt q", p=128)),
                    deps=[wl_b], stream=2, tag=f"wl7c_{mb}_{nq}")
                for mi in range(4):
                    m = mb * 4 + mi
                    parm = m % 2

                    def gmm(t, m=m, mi=mi, parm=parm, parn=parn):
                        for k in range(8):
                            t.matmul(psB[:, parm * 512:(parm + 1) * 512],
                                     ylb[:, k * T + m * 128:k * T + (m + 1) * 128],
                                     fc2sl[:, parn * 4096 + k * 512:
                                           parn * 4096 + (k + 1) * 512],
                                     start=(k == 0), stop=False)
                        for k in range(32):
                            inst = t.matmul(
                                psB[:, parm * 512:(parm + 1) * 512],
                                um[:, mi * 4096 + k * 128:mi * 4096 + (k + 1) * 128],
                                sdnb[:, k * 512:(k + 1) * 512],
                                start=False, stop=(k == 31))
                        return inst

                    def smm(t, mi=mi, parm=parm):
                        for k in range(32):
                            inst = t.matmul(
                                psB[:, 1024 + parm * 512:1024 + (parm + 1) * 512],
                                onm[:, mi * 4096 + k * 128:mi * 4096 + (k + 1) * 128],
                                woutb[:, k * 512:(k + 1) * 512],
                                start=(k == 0), stop=(k == 31))
                        return inst
                    depg = [st_wl7[(mb, nq)], st_yc, st_uml[m], st_trev7[m]]
                    if (mb, nq, m) != (0, 0, 0):
                        prev = _prev_mnq(mb, nq, mi)
                        if prev is not None and prev in st_o2:
                            depg.append(st_o2[prev])
                    st_g[(mb, nq, m)] = sc.step("tensor", gmm, deps=depg,
                                                tag=f"g{mb}_{nq}_{m}")
                    st_s7[(mb, nq, m)] = sc.step("tensor", smm,
                                                 deps=[st_g[(mb, nq, m)]],
                                                 tag=f"s{mb}_{nq}_{m}")
                    st_xl = sc.dma("sync", lambda e, m=m, nq=nq, parm=parm:
                                   e.dma_start(
                                       xre[:, parm * 512:(parm + 1) * 512],
                                       x_in[m * 128:(m + 1) * 128,
                                            nq * 512:(nq + 1) * 512]),
                                   deps=[st_o2[_prev_mnq(mb, nq, mi)]]
                                   if _prev_mnq(mb, nq, mi) in st_o2 else [],
                                   stream=4 + parm, tag=f"xl{m}_{nq}")
                    st_sil = sc.step("scalar", lambda s, parm=parm: s.activation(
                        tsil[:, parm * 512:(parm + 1) * 512],
                        psB[:, parm * 512:(parm + 1) * 512], AF.Silu),
                        deps=[st_g[(mb, nq, m)]], tag=f"sil{mb}_{nq}_{m}")

                    def o2f(v, parm=parm):
                        v.scalar_tensor_tensor(
                            outst[:, parm * 512:(parm + 1) * 512],
                            tsil[:, parm * 512:(parm + 1) * 512], 1.0,
                            psB[:, 1024 + parm * 512:1024 + (parm + 1) * 512],
                            ALU.mult, ALU.mult)
                        return v.scalar_tensor_tensor(
                            outst[:, parm * 512:(parm + 1) * 512],
                            outst[:, parm * 512:(parm + 1) * 512], 1.0,
                            xre[:, parm * 512:(parm + 1) * 512],
                            ALU.mult, ALU.add)
                    st_o2[(mb, nq, m)] = sc.step(
                        "vector", o2f,
                        deps=[st_sil, st_s7[(mb, nq, m)], st_xl],
                        tag=f"o2_{mb}_{nq}_{m}")
                    st_od[(mb, nq, m)] = sc.dma(
                        "sync", lambda e, m=m, nq=nq, parm=parm: e.dma_start(
                            out_d[m * 128:(m + 1) * 128, nq * 512:(nq + 1) * 512],
                            outst[:, parm * 512:(parm + 1) * 512]),
                        deps=[st_o2[(mb, nq, m)]], stream=6 + parm,
                        tag=f"od{mb}_{nq}_{m}")
        with nc.Block() as blk:
            sc.emit(blk)

    _drain_all(nc, dsems)
    ctx.close()
    return nc


def _prev_mnq(mb, nq, mi):
    m = mb * 4 + mi
    order = []
    for mb2 in range(4):
        for nq2 in range(4):
            for mi2 in range(4):
                order.append((mb2, nq2, mb2 * 4 + mi2))
    idx = order.index((mb, nq, m))
    parm = m % 2
    for j in range(idx - 1, -1, -1):
        if order[j][2] % 2 == parm:
            return order[j]
    return None


def _finish_stub(nc, sched, out_d, arn, dsems):
    sc = sched()
    sc.dma("sync", lambda e: e.dma_start(out_d[0:128, :], arn(0, 2048, F32)),
           stream=0, tag="stub_out")
    with nc.Block() as blk:
        sc.emit(blk)
    _drain_all(nc, dsems)


def _drain_all(nc, dsems):
    with nc.Block() as blk:
        @blk.sync
        def _(e):
            for ds in dsems:
                if ds.n:
                    e.wait_ge(ds.h, ds.n)


def _host_pack(inputs):
    f32 = np.float32
    g = lambda k: np.asarray(inputs[k], f32)
    conv_w = g("L0_attn_conv_w")
    return {
        "winT": np.ascontiguousarray(g("L0_attn_W_in").T),
        "woutT": np.ascontiguousarray(g("L0_attn_W_out").T),
        "gw_pk": np.ascontiguousarray(
            g("L0_ffn_gate_w").T.reshape(16, 128, 32).transpose(1, 0, 2).reshape(128, 512)),
        "fc1T": np.ascontiguousarray(g("L0_ffn_fc1_w").T),
        "fc2T": np.ascontiguousarray(g("L0_ffn_fc2_w").T),
        "w1T": np.ascontiguousarray(g("L0_ffn_w1").transpose(0, 2, 1)),
        "w2T": np.ascontiguousarray(g("L0_ffn_w2").transpose(0, 2, 1)),
        "supT": np.ascontiguousarray(g("L0_ffn_shared_up_w").T),
        "sdnT": np.ascontiguousarray(g("L0_ffn_shared_down_w").T),
        "convw_p": np.ascontiguousarray(
            conv_w.reshape(48, 128, 4).transpose(1, 0, 2).reshape(128, 192)),
        "convb_p": np.ascontiguousarray(g("L0_attn_conv_b").reshape(48, 128).T),
        "dtb": np.ascontiguousarray(g("L0_attn_dt_bias")[:, None]),
        "alog": np.ascontiguousarray(g("L0_attn_A_log")[:, None]),
        "d_row": np.ascontiguousarray(g("L0_attn_D")[None, :]),
        "nw_row": np.ascontiguousarray(g("L0_attn_norm_weight")[None, :]),
        "gb_row": np.ascontiguousarray(g("L0_ffn_gate_bias")[None, :]),
        "identb_h": np.eye(128, dtype=bfdt),
        "identf_h": np.eye(128, dtype=f32),
        "maskU_h": np.triu(np.ones((128, 128), f32)),
    }


def kernel(**inputs):
    shared = _host_pack(inputs)
    x = np.asarray(inputs["L0_attn_hidden_states"], np.float32)
    nc = _build()
    in_maps = []
    for c in range(8):
        m = dict(shared)
        m["x"] = np.ascontiguousarray(x[c])
        in_maps.append(m)
    res = run_bass_kernel_spmd(nc, in_maps, core_ids=list(range(8)))
    return np.stack([np.asarray(res.results[c]["out"]) for c in range(8)], axis=0)


# version 06 (final)
# speedup vs baseline: 1.2112x; 1.0050x over previous
import contextlib

import numpy as np
import ml_dtypes

import concourse.bass as bass
import concourse.mybir as mybir
from concourse.bass_utils import run_bass_kernel_spmd

F32 = mybir.dt.float32
BF16 = mybir.dt.bfloat16
U8 = mybir.dt.uint8
AF = mybir.ActivationFunctionType
ALU = mybir.AluOpType
AX = mybir.AxisListType
bfdt = ml_dtypes.bfloat16

T = 2048
D = 2048
NH = 64
HP = 64
NS = 128
NG = 8
INTER = 4096
PROJ = 10304
CS = 128
NCH = T // CS
NE = 32
LAT = 1024
SHI = 4096
EPS = 1e-5

GATE0 = 0
HBC0 = 4096
DT0 = 10240

_PHASES = 7
_NCH_LIM = None
_P4_LVL = 99
_P4_TREV = 3


class SemC:
    __slots__ = ("h", "n", "last_step")

    def __init__(self, h):
        self.h = h
        self.n = 0
        self.last_step = None


class Step:
    __slots__ = ("engine", "emit", "deps", "needs_event", "sem", "val",
                 "is_dma", "tag")

    def __init__(self, engine, emit, deps, is_dma, tag):
        self.engine = engine
        self.emit = emit
        assert all(d is not None for d in deps), (tag, 'None dep')
        self.deps = list(deps)
        self.needs_event = False
        self.sem = None
        self.val = None
        self.is_dma = is_dma
        self.tag = tag


class Sched:
    ENGINES = ("sync", "gpsimd", "vector", "scalar", "tensor")

    def __init__(self, nc, esems, dsems, done):
        self.nc = nc
        self.esems = esems
        self.dsems = dsems
        self.done = done
        self.steps = {e: [] for e in self.ENGINES}

    def step(self, engine, emit, deps=(), tag=""):
        st = Step(engine, emit, deps, False, tag)
        self.steps[engine].append(st)
        return st

    def dma(self, engine, emit, deps=(), stream=0, tag=""):
        st = Step(engine, emit, deps, True, tag)
        st.sem = self.dsems[stream]
        if st.sem.last_step is not None:
            st.deps.append(st.sem.last_step)
        st.sem.last_step = st
        self.steps[engine].append(st)
        return st

    def emit(self, blk):
        for e in self.ENGINES:
            for st in self.steps[e]:
                for d in st.deps:
                    d.needs_event = True
        for e in self.ENGINES:
            for st in self.steps[e]:
                if st.is_dma:
                    st.val = st.sem.n + 16
                    st.sem.n = st.val
                elif st.needs_event:
                    st.sem = self.esems[e]
                    st.val = st.sem.n + 1
                    st.sem.n = st.val
        self._simulate()

        def body_for(engine):
            steps = self.steps[engine]

            def body(eng):
                waited = {}
                for st in steps:
                    for d in st.deps:
                        assert d.sem is not None, (st.tag, d.tag)
                        key = id(d.sem)
                        if waited.get(key, -1) < d.val:
                            eng.wait_ge(d.sem.h, d.val)
                            waited[key] = d.val
                    inst = st.emit(eng)
                    if st.is_dma:
                        inst.then_inc(st.sem.h, 16)
                    elif st.needs_event:
                        inst.then_inc(st.sem.h, 1)
            return body

        for engine in self.ENGINES:
            if self.steps[engine]:
                getattr(blk, engine)(body_for(engine))
        for e in self.ENGINES:
            for st in self.steps[e]:
                self.done.add(id(st))

    def _simulate(self):
        local_done = set(self.done)
        ptr = {e: 0 for e in self.ENGINES}
        progressed = True
        while progressed:
            progressed = False
            for e in self.ENGINES:
                while ptr[e] < len(self.steps[e]):
                    st = self.steps[e][ptr[e]]
                    if any(id(d) not in local_done for d in st.deps):
                        break
                    local_done.add(id(st))
                    ptr[e] += 1
                    progressed = True
        stuck = [(e, self.steps[e][ptr[e]].tag)
                 for e in self.ENGINES if ptr[e] < len(self.steps[e])]
        if stuck:
            raise RuntimeError(f"schedule deadlock at: {stuck}")


def _build(dbg=frozenset(), phases=None):
    if phases is None:
        phases = _PHASES
    nc = bass.Bass("TRN2", target_bir_lowering=False,
                   detect_race_conditions=False)
    ctx = contextlib.ExitStack()

    def dram_in(name, shape, dt=F32):
        return nc.dram_tensor(name, shape, dt, kind="ExternalInput").ap()

    def scratch(name, shape, dt):
        kind = "ExternalOutput" if name in dbg else "Internal"
        return nc.dram_tensor(name, shape, dt, kind=kind).ap()

    x_in = dram_in("x", [T, D])
    winT = dram_in("winT", [D, PROJ])
    woutT = dram_in("woutT", [INTER, D])
    gw_pk = dram_in("gw_pk", [128, 512])
    fc1T = dram_in("fc1T", [D, LAT])
    fc2T = dram_in("fc2T", [LAT, D])
    w1T = dram_in("w1T", [NE, LAT, LAT])
    w2T = dram_in("w2T", [NE, LAT, LAT])
    supT = dram_in("supT", [D, SHI])
    sdnT = dram_in("sdnT", [SHI, D])
    convw_p = dram_in("convw_p", [128, 192])
    convb_p = dram_in("convb_p", [128, 48])
    dtb_in = dram_in("dtb", [NH, 1])
    alog_in = dram_in("alog", [NH, 1])
    d_row_in = dram_in("d_row", [1, NH])
    nw_row_in = dram_in("nw_row", [1, INTER])
    gb_row_in = dram_in("gb_row", [1, NE])
    identb_in = dram_in("identb_h", [128, 128], BF16)
    identf_in = dram_in("identf_h", [128, 128])
    maskU_in = dram_in("maskU_h", [128, 128])

    out_d = nc.dram_tensor("out", [T, D], F32, kind="ExternalOutput").ap()

    xT_s = scratch("xT_s", [T, INTER], BF16)
    BT_s = scratch("BT_s", [T, NG * NS], BF16)
    Bf_s = scratch("Bf_s", [NG * NS, T], BF16)
    C_s = scratch("C_s", [NG * NS, T], BF16)
    gate_s = scratch("gate_s", [T, INTER], F32)
    u_s = scratch("u_s", [SHI, T], BF16)
    onorm_s = scratch("onorm_s", [T, INTER], BF16)
    cwhi_s = scratch("cwhi_s", [NE, T], BF16)
    cwlo_s = scratch("cwlo_s", [NE, T], BF16)
    acs_d = scratch("acs_d", [2, NH * CS], BF16)
    xlat_s = scratch("xlat_s", [LAT, T], BF16)

    dbg_t = {}
    for nm, shape, dt in [("xn_d", [128, 16 * T], BF16),
                          ("dt_d", [NH, T], F32),
                          ("xlat_d", [128, 8 * T], BF16),
                          ("cw_d", [NE, T], F32),
                          ("ylat_d", [128, 8 * T], F32),
                          ("ych_d", [NCH * 128, INTER], F32)]:
        if nm in dbg:
            dbg_t[nm] = nc.dram_tensor(nm, shape, dt, kind="ExternalOutput").ap()

    identb = nc.alloc_sbuf_tensor("identb", [128, 128], BF16)
    identf = nc.alloc_sbuf_tensor("identf", [128, 128], F32)
    maskU = nc.alloc_sbuf_tensor("maskU", [128, 128], F32)
    ones2 = nc.alloc_sbuf_tensor("ones2", [2, 128], BF16)
    ones1f = nc.alloc_sbuf_tensor("ones1f", [1, 128], F32)
    ones64 = nc.alloc_sbuf_tensor("ones64", [NH, CS], F32)
    dtbc = nc.alloc_sbuf_tensor("dtbc", [NH, 1], F32)
    expA = nc.alloc_sbuf_tensor("expA", [NH, 1], F32)
    alog_t = nc.alloc_sbuf_tensor("alog_t", [NH, 1], F32)
    d_rep = nc.alloc_sbuf_tensor("d_rep", [128, NH], F32)
    nw_bc = nc.alloc_sbuf_tensor("nw_bc", [128, INTER], BF16)
    gb_bc = nc.alloc_sbuf_tensor("gb_bc", [128, NE], F32)
    epsc = nc.alloc_sbuf_tensor("epsc", [128, 1], F32)
    x_lat = nc.alloc_sbuf_tensor("x_lat", [128, 8 * T], BF16)
    arena = nc.alloc_sbuf_tensor("arena", [128, 65536], U8)

    def arn(off_bytes, n, dt):
        step = 2 if dt == BF16 else 4
        return arena[:, off_bytes:off_bytes + n * step].bitcast(dt)

    def xn_t(k):
        return arn(k * T * 2, T, BF16)

    def xn_all():
        return arn(0, 16 * T, BF16)

    psA = nc.alloc_psum_tensor("psA", [128, 2048], F32)
    psB = nc.alloc_psum_tensor("psB", [128, 2048], F32)

    esems = {e: SemC(ctx.enter_context(nc.semaphore("se_" + e)))
             for e in Sched.ENGINES}
    dsems = [SemC(ctx.enter_context(nc.semaphore(f"sd{i}"))) for i in range(10)]
    fsems = {e: SemC(ctx.enter_context(nc.semaphore("sf_" + e)))
             for e in ("vector", "scalar", "gpsimd")}
    DONE = set()

    def mkfence(name, eng):
        fs = fsems[name]

        def F(inst):
            fs.n += 1
            inst.then_inc(fs.h, 1)
            eng.wait_ge(fs.h, fs.n)
            return inst
        return F

    def sched():
        return Sched(nc, esems, dsems, DONE)

    p0_stack = contextlib.ExitStack()
    nwrow = p0_stack.enter_context(nc.sbuf_tensor("nwrow", [1, INTER], F32))

    sc0 = sched()
    ld = []
    for dst, src in [(identb[:], identb_in), (identf[:], identf_in),
                     (maskU[:], maskU_in), (dtbc[:], dtb_in),
                     (alog_t[:], alog_in), (d_rep[0:1, :], d_row_in),
                     (gb_bc[0:1, :], gb_row_in), (nwrow[:], nw_row_in)]:
        ld.append(sc0.dma("sync", lambda e, d=dst, s=src: e.dma_start(d, s),
                          stream=0, tag="c0"))
    sc0.step("vector", lambda v: v.memset(ones2[:], 1.0))
    st_o1 = sc0.step("vector", lambda v: v.memset(ones1f[:], 1.0), tag="ones1f")
    sc0.step("vector", lambda v: v.memset(ones64[:], 1.0))
    sc0.step("vector", lambda v: v.memset(epsc[:], EPS))
    st_ea = sc0.step("scalar", lambda s: s.activation(expA[:], alog_t[:], AF.Exp),
                     deps=[ld[6]], tag="expA")
    def _bc1(t):
        t.matmul(psA[:, 0:64], ones1f[:], d_rep[0:1, :], start=True, stop=True)
        t.matmul(psA[:, 64:96], ones1f[:], gb_bc[0:1, :], start=True, stop=True)
        for n in range(4):
            i = t.matmul(psB[:, n * 512:(n + 1) * 512], ones1f[:],
                         nwrow[:, n * 512:(n + 1) * 512], start=True, stop=True)
        return i

    st_bc = sc0.step("tensor", _bc1, deps=ld[7:10] + [ld[0], st_o1], tag="bcasts")
    st_ev1 = sc0.step("vector", lambda v: (
        v.tensor_copy(d_rep[:], psA[:, 0:64]),
        v.tensor_copy(gb_bc[:], psA[:, 64:96]),
        v.tensor_copy(nw_bc[:, 0:2048], psB[:]))[-1],
                      deps=[st_bc], tag="bcast_evac1")
    st_bc2 = sc0.step("tensor", lambda t: [
        t.matmul(psB[:, n * 512:(n + 1) * 512], ones1f[:],
                 nwrow[:, 2048 + n * 512:2048 + (n + 1) * 512],
                 start=True, stop=True) for n in range(4)][-1],
                      deps=[st_ev1], tag="bcasts2")
    sc0.step("vector", lambda v: v.tensor_copy(nw_bc[:, 2048:4096], psB[:]),
             deps=[st_bc2], tag="bcast_evac2")
    with nc.Block() as blk:
        sc0.emit(blk)
    p0_stack.close()

    with contextlib.ExitStack() as p1:
        xld = p1.enter_context(nc.sbuf_tensor("xld", [128, 2 * D], F32))
        xtm = p1.enter_context(nc.sbuf_tensor("xtm", [128, 2 * D], BF16))
        stat = p1.enter_context(nc.sbuf_tensor("stat", [128, 16], F32))
        junk = p1.enter_context(nc.sbuf_tensor("junk", [128, D], F32))

        def stc(pp, i):
            return stat[:, 8 * pp + i:8 * pp + i + 1]

        def trreg(pp):
            return (psA if pp == 0 else psB)[:, 0:1024].bitcast(BF16)

        sc = sched()
        st_load = [None] * 16
        st_a3 = [None] * 16
        st_tr = [None] * 16
        st_ev = [None] * 16
        for m in range(16):
            pp = m % 2
            st_load[m] = sc.dma(
                "sync",
                lambda e, m=m, pp=pp: e.dma_start(
                    xld[:, pp * D:(pp + 1) * D], x_in[m * 128:(m + 1) * 128, :]),
                deps=[st_a3[m - 2]] if m >= 2 else [],
                stream=pp, tag=f"xld{m}")
            xv = xld[:, pp * D:(pp + 1) * D]
            dep_free = [st_a3[m - 2]] if m >= 2 else []
            def v_stat(v, xv=xv, pp=pp):
                F = mkfence("vector", v)
                v.tensor_scalar(junk[:], xv, 1.0, 0.0, ALU.mult, ALU.add,
                                accum_out=stc(pp, 1))
                F(v.scalar_tensor_tensor(junk[:], xv, 1.0, xv, ALU.mult, ALU.mult,
                                         accum_out=stc(pp, 0)))
                F(v.tensor_scalar(stc(pp, 2), stc(pp, 1), 1.0 / D, None, ALU.mult))
                return F(v.tensor_scalar(stc(pp, 3), stc(pp, 0), 1.0 / D, None,
                                         ALU.mult))

            sc.step("vector", v_stat, deps=[st_load[m]] + dep_free,
                    tag=f"v_stat{m}")
            st_v5 = sc.step("vector", lambda v, pp=pp: v.scalar_tensor_tensor(
                stc(pp, 4), stc(pp, 2), stc(pp, 2), stc(pp, 3), ALU.mult,
                ALU.subtract), tag=f"v5_{m}")
            def a2f(s, pp=pp):
                F = mkfence("scalar", s)
                F(s.activation(stc(pp, 5), stc(pp, 4), AF.Ln, bias=epsc[:, 0:1],
                               scale=-1.0))
                return s.activation(stc(pp, 6), stc(pp, 5), AF.Exp, scale=-0.5)

            st_a2 = sc.step("scalar", a2f, deps=[st_v5], tag=f"a2_{m}")
            st_v6 = sc.step("vector", lambda v, pp=pp: v.tensor_scalar(
                stc(pp, 7), stc(pp, 2), stc(pp, 6), -1.0, ALU.mult, ALU.mult),
                            deps=[st_a2], tag=f"v6_{m}")
            st_a3[m] = sc.step("scalar", lambda s, pp=pp: s.activation(
                xtm[:, pp * D:(pp + 1) * D], xld[:, pp * D:(pp + 1) * D],
                AF.Identity, bias=stc(pp, 7), scale=stc(pp, 6)),
                               deps=[st_v6], tag=f"a3_{m}")
            st_tr[m] = sc.step("tensor", lambda t, m=m, pp=pp: [
                t.transpose(trreg(pp)[:, j * 128:(j + 1) * 128],
                            xtm[:, pp * D + j * 128:pp * D + (j + 1) * 128],
                            identb[:]) for j in range(16)][-1],
                deps=[st_a3[m]] + ([st_ev[m - 2]] if m >= 2 else []),
                tag=f"tr{m}")
            st_ev[m] = sc.step("vector", lambda v, m=m, pp=pp: v.tensor_copy(
                xn_all().rearrange("p (j t) -> p j t", t=T)[
                    :, :, m * 128:(m + 1) * 128],
                trreg(pp).rearrange("p (j q) -> p j q", q=128)),
                deps=[st_tr[m]], tag=f"ev{m}")
        with nc.Block() as blk:
            sc.emit(blk)

    if "xn_d" in dbg_t:
        sc = sched()
        sc.dma("sync", lambda e: e.dma_start(dbg_t["xn_d"], xn_all()),
               stream=0, tag="dump_xn")
        with nc.Block() as blk:
            sc.emit(blk)

    p24 = ctx.enter_context(contextlib.ExitStack())
    dt_raw = p24.enter_context(nc.sbuf_tensor("dt_raw", [NH, T], F32))

    with contextlib.ExitStack() as p2:
        wdt = p2.enter_context(nc.sbuf_tensor("wdt", [128, 16 * 64], BF16))
        convw = p2.enter_context(nc.sbuf_tensor("convw", [128, 192], F32))
        convb = p2.enter_context(nc.sbuf_tensor("convb", [128, 48], F32))
        wsl = p2.enter_context(nc.sbuf_tensor("wsl", [128, 2 * 2048], BF16))
        cvin = p2.enter_context(nc.sbuf_tensor("cvin", [128, 2 * 2051], F32))
        tA = p2.enter_context(nc.sbuf_tensor("tA", [128, 2048], F32))
        tB = p2.enter_context(nc.sbuf_tensor("tB", [128, 2048], F32))
        tC = p2.enter_context(nc.sbuf_tensor("tC", [128, 2048], F32))
        tD = p2.enter_context(nc.sbuf_tensor("tD", [128, 2048], F32))

        NSLAB = 48

        def kindof(s):
            return "x" if s < 32 else ("b" if s < 40 else "c")

        sc = sched()
        st_wl = [None] * NSLAB
        st_mm = [None] * NSLAB
        st_cvev = [None] * NSLAB
        st_tap = [None] * NSLAB
        st_w = [None] * NSLAB
        st_silu = [None] * NSLAB
        st_tp = [None] * NSLAB
        st_tev = [None] * NSLAB
        st_fl = [None] * NSLAB
        st_fc = [None] * NSLAB

        st_pad = sc.step("vector", lambda v: (
            v.memset(cvin[:, 0:3], 0.0),
            v.memset(cvin[:, 2051:2054], 0.0))[-1], tag="cvpad")
        st_wdt = sc.dma("gpsimd", lambda g: g.dma_start(
            wdt[:], winT[:, DT0:DT0 + 64].rearrange("(kt p) m -> p kt m", p=128)),
            stream=2, tag="wdt")
        st_cw1 = sc.dma("sync", lambda e: e.dma_start(convw[:], convw_p),
                        stream=8, tag="convw")
        st_cw2 = sc.dma("sync", lambda e: e.dma_start(convb[:], convb_p),
                        stream=8, tag="convb")

        def wload(g, s):
            pp = s % 2
            return g.dma_start(
                wsl[:, pp * 2048:(pp + 1) * 2048],
                winT[:, HBC0 + s * 128:HBC0 + (s + 1) * 128].rearrange(
                    "(kt p) m -> p kt m", p=128))

        st_dtmm = sc.step("tensor", lambda t: [
            t.matmul(psB[0:64, n * 512:(n + 1) * 512], wdt[:, k * 64:(k + 1) * 64],
                     xn_t(k)[:, n * 512:(n + 1) * 512],
                     start=(k == 0), stop=(k == 15))
            for k in range(16) for n in range(4)][-1],
                          deps=[st_wdt], tag="dtmm")
        st_dtev = sc.step("vector", lambda v: v.tensor_copy(dt_raw[:], psB[0:64, :]),
                          deps=[st_dtmm], tag="dtev")

        for s in range(NSLAB):
            pp = s % 2
            kd = kindof(s)
            st_wl[s] = sc.dma("gpsimd", lambda g, s=s: wload(g, s),
                              deps=[st_mm[s - 2]] if s >= 2 else [],
                              stream=2 + pp, tag=f"wl{s}")
            free = []
            if s == 1:
                free.append(st_dtev)
            if s >= 2:
                free.append(st_cvev[s - 2])
                if kindof(s - 2) != "c":
                    free.append(st_tev[s - 2])
            ps = psA if pp == 0 else psB
            st_mm[s] = sc.step("tensor", lambda t, s=s, pp=pp, ps=ps: [
                t.matmul(ps[:, n * 512:(n + 1) * 512],
                         wsl[:, pp * 2048 + k * 128:pp * 2048 + (k + 1) * 128],
                         xn_t(k)[:, n * 512:(n + 1) * 512],
                         start=(k == 0), stop=(k == 15))
                for k in range(16) for n in range(4)][-1],
                deps=[st_wl[s]] + free, tag=f"mm{s}")
            cv = cvin[:, pp * 2051:(pp + 1) * 2051]
            st_cvev[s] = sc.step("vector", lambda v, cv=cv, ps=ps: v.tensor_copy(
                cv[:, 3:2051], ps[:]),
                deps=[st_mm[s]] + ([st_silu[s - 1]] if s >= 1 else []),
                tag=f"cvev{s}")
            st_tap[s] = sc.step("gpsimd", lambda g, cv=cv, s=s: (
                g.tensor_tensor(tC[:], cv[:, 3:2051],
                                convw[:, 4 * s + 3:4 * s + 4].broadcast_to(
                                    [128, 2048]), ALU.mult),
                g.tensor_tensor(tD[:], cv[:, 2:2050],
                                convw[:, 4 * s + 2:4 * s + 3].broadcast_to(
                                    [128, 2048]), ALU.mult),
                g.tensor_add(tC[:], tC[:], tD[:]))[-1],
                deps=[st_cvev[s], st_cw1] + ([st_w[s - 1]] if s >= 1 else []),
                tag=f"tap{s}")
            st_w[s] = sc.step("vector", lambda v, cv=cv, s=s: (
                v.tensor_scalar(tA[:], cv[:, 1:2049], convw[:, 4 * s + 1:4 * s + 2],
                                convb[:, s:s + 1], ALU.mult, ALU.add),
                v.scalar_tensor_tensor(tB[:], cv[:, 0:2048],
                                       convw[:, 4 * s:4 * s + 1], tA[:],
                                       ALU.mult, ALU.add),
                v.tensor_add(tA[:], tB[:], tC[:]))[-1],
                deps=[st_tap[s], st_cw1, st_cw2] +
                ([st_silu[s - 1]] if s >= 1 else []),
                tag=f"w{s}")
            sxfree = []
            if s >= 2:
                k2 = kindof(s - 2)
                if k2 in ("b", "c"):
                    sxfree.append(st_fc[s - 2])
                if k2 != "c":
                    sxfree.append(st_tp[s - 2])
            st_silu[s] = sc.step("scalar", lambda sce, s=s, pp=pp: sce.activation(
                x_lat[:, pp * 2048:(pp + 1) * 2048], tA[:], AF.Silu),
                deps=[st_w[s]] + sxfree, tag=f"silu{s}")
            if kd in ("b", "c"):
                dst = (Bf_s[(s - 32) * 128:(s - 31) * 128, :] if kd == "b"
                       else C_s[(s - 40) * 128:(s - 39) * 128, :])
                st_fc[s] = sc.dma("sync", lambda e, dst=dst, pp=pp: e.dma_start(
                    dst, x_lat[:, pp * 2048:(pp + 1) * 2048]),
                    deps=[st_silu[s]], stream=4 + pp, tag=f"fc{s}")
            if kd != "c":
                ps = psA if pp == 0 else psB
                st_tp[s] = sc.step("tensor", lambda t, pp=pp, ps=ps: [
                    t.transpose(ps[:, 0:1024].bitcast(BF16)[:, j * 128:(j + 1) * 128],
                                x_lat[:, pp * 2048 + j * 128:pp * 2048 + (j + 1) * 128],
                                identb[:]) for j in range(16)][-1],
                    deps=[st_silu[s], st_cvev[s]], tag=f"tp{s}")
                hp = (s // 2) % 2
                dst = x_lat[:, 4096 + hp * 4096:4096 + (hp + 1) * 4096].rearrange(
                    "p (j q) -> p j q", q=256)[:, :, pp * 128:pp * 128 + 128]
                stage_free = [st_fl[2 * (s // 2 - 2) + 1]] if s // 2 >= 2 else []
                st_tev[s] = sc.step("scalar", lambda sce, dst=dst, pp=pp, ps=ps:
                                    sce.activation(dst, ps[:, 0:1024].bitcast(
                                        BF16).rearrange("p (j q) -> p j q", q=128),
                                        AF.Copy),
                                    deps=[st_tp[s]] + stage_free, tag=f"tev{s}")
                if s % 2 == 1:
                    hp = (s // 2) % 2
                    src = x_lat[:, 4096 + hp * 4096:4096 + (hp + 1) * 4096].rearrange(
                        "p (j q) -> p j q", q=256)
                    if kd == "x":
                        dstd = xT_s[:, (s - 1) * 128:(s + 1) * 128].rearrange(
                            "(j p) q -> p j q", p=128)
                    else:
                        dstd = BT_s[:, (s - 33) * 128:(s - 31) * 128].rearrange(
                            "(j p) q -> p j q", p=128)
                    st_fl[s] = sc.dma("sync", lambda e, dstd=dstd, src=src:
                                      e.dma_start(dstd, src),
                                      deps=[st_tev[s], st_tev[s - 1]],
                                      stream=6 + hp, tag=f"fl{s}")
        with nc.Block() as blk:
            sc.emit(blk)

    if "dt_d" in dbg_t:
        sc = sched()
        sc.dma("sync", lambda e: e.dma_start(dbg_t["dt_d"], dt_raw[:]), stream=0,
               tag="dump_dt")
        with nc.Block() as blk:
            sc.emit(blk)

    with contextlib.ExitStack() as p2b:
        wg = p2b.enter_context(nc.sbuf_tensor("wg", [128, 2 * 16 * 512], BF16))
        gst = p2b.enter_context(nc.sbuf_tensor("gst", [128, 2 * 512], F32))

        sc = sched()
        st_wgl = [None] * 8
        st_gmm = {}
        st_gev = {}
        st_gdm = {}
        for ng in range(8):
            pp = ng % 2
            deps = [st_gmm[ng * 16 - 17]] if ng >= 2 else []
            st_wgl[ng] = sc.dma("gpsimd", lambda g, ng=ng, pp=pp: g.dma_start(
                wg[:, pp * 8192:(pp + 1) * 8192],
                winT[:, ng * 512:(ng + 1) * 512].rearrange(
                    "(kt p) m -> p kt m", p=128)),
                deps=deps, stream=2 + pp, tag=f"wgl{ng}")
            for m in range(16):
                gi = ng * 16 + m
                sl = gi % 2
                psv = psA[:, sl * 512:(sl + 1) * 512]
                prev_ev = st_gev.get(gi - 2)
                st_gmm[gi] = sc.step("tensor", lambda t, m=m, pp=pp, psv=psv: [
                    t.matmul(psv, xn_t(k)[:, m * 128:(m + 1) * 128],
                             wg[:, pp * 8192 + k * 512:pp * 8192 + (k + 1) * 512],
                             start=(k == 0), stop=(k == 15)) for k in range(16)][-1],
                    deps=[st_wgl[ng]] + ([prev_ev] if prev_ev else []),
                    tag=f"gmm{gi}")
                prev_dm = st_gdm.get(gi - 2)
                st_gev[gi] = sc.step("scalar", lambda s, sl=sl, psv=psv:
                                     s.activation(gst[:, sl * 512:(sl + 1) * 512],
                                                  psv, AF.Copy),
                                     deps=[st_gmm[gi]] +
                                     ([prev_dm] if prev_dm else []),
                                     tag=f"gev{gi}")
                st_gdm[gi] = sc.dma("sync", lambda e, ng=ng, m=m, sl=sl:
                                    e.dma_start(
                                        gate_s[m * 128:(m + 1) * 128,
                                               ng * 512:(ng + 1) * 512],
                                        gst[:, sl * 512:(sl + 1) * 512]),
                                    deps=[st_gev[gi]], stream=4 + sl,
                                    tag=f"gdm{gi}")
        with nc.Block() as blk:
            sc.emit(blk)

    if phases <= 2:
        _finish_stub(nc, sched, out_d, arn, dsems)
        ctx.close()
        return nc

    with contextlib.ExitStack() as p3:
        gwb = p3.enter_context(nc.sbuf_tensor("gwb", [128, 512], BF16))
        rw = p3.enter_context(nc.sbuf_tensor("rw", [128, 2 * 144], F32))
        cwT = p3.enter_context(nc.sbuf_tensor("cwT", [NE, T], F32))
        cwbf = p3.enter_context(nc.sbuf_tensor("cwbf", [NE, 2 * T], BF16))
        cwlf = p3.enter_context(nc.sbuf_tensor("cwlf", [NE, T], F32))
        wstr = p3.enter_context(nc.sbuf_tensor("wstr", [128, 2 * 2048], BF16))
        tr32 = p3.enter_context(nc.sbuf_tensor("tr32", [128, 2 * 2048], F32))
        usq = p3.enter_context(nc.sbuf_tensor("usq", [128, 2 * 2048], BF16))
        spacer = p3.enter_context(nc.sbuf_tensor("spacer", [128, 32], F32))

        sc = sched()
        st_gw = sc.dma("gpsimd", lambda g: g.dma_start(gwb[:], gw_pk), stream=2,
                       tag="gwb")
        st_rt = [None] * 16
        st_cwev = [None] * 16

        def psl_v(sl):
            return psB[:, 1024 + sl * 512:1024 + sl * 512 + 32]

        def pst_v(sl):
            return psB[0:32, 1024 + sl * 512 + 32:1024 + sl * 512 + 160]

        for m in range(16):
            pp = m % 2
            b = pp * 144
            sl = m % 2
            dep_ps = [st_cwev[m - 2]] if m >= 2 else []
            st_lg = sc.step("tensor", lambda t, m=m, sl=sl: [
                t.matmul(psl_v(sl), xn_t(k)[:, m * 128:(m + 1) * 128],
                         gwb[:, k * 32:(k + 1) * 32],
                         start=(k == 0), stop=(k == 15)) for k in range(16)][-1],
                deps=[st_gw] + dep_ps, tag=f"lg{m}")
            st_sig = sc.step("scalar", lambda s, b=b, sl=sl: s.activation(
                rw[:, b:b + 32], psl_v(sl), AF.Sigmoid),
                deps=[st_lg], tag=f"sig{m}")

            def routing(v, b=b):
                w = rw
                F = mkfence("vector", v)
                F(v.tensor_add(w[:, b + 32:b + 64], w[:, b:b + 32], gb_bc[:]))
                for g in range(4):
                    F(v.max(out=w[:, b + 104 + 8 * g:b + 112 + 8 * g],
                            in_=w[:, b + 32 + 8 * g:b + 40 + 8 * g]))
                F(v.memset(w[:, b + 72:b + 76], -1e30))
                for g in range(4):
                    F(v.tensor_add(w[:, b + 64 + g:b + 65 + g],
                                   w[:, b + 104 + 8 * g:b + 105 + 8 * g],
                                   w[:, b + 105 + 8 * g:b + 106 + 8 * g]))
                F(v.tensor_copy(w[:, b + 68:b + 72], w[:, b + 64:b + 68]))
                F(v.max(out=w[:, b + 84:b + 92], in_=w[:, b + 68:b + 76]))
                F(v.tensor_scalar(w[:, b + 92:b + 96], w[:, b + 64:b + 68],
                                  w[:, b + 85:b + 86], None, ALU.is_ge))
                F(v.tensor_scalar(w[:, b + 96:b + 100], w[:, b + 92:b + 96],
                                  1e30, -1e30, ALU.mult, ALU.add))
                for g in range(4):
                    F(v.scalar_tensor_tensor(
                        w[:, b + 32 + 8 * g:b + 40 + 8 * g],
                        w[:, b + 96 + g:b + 97 + g].broadcast_to([128, 8]),
                        1.0, w[:, b + 32 + 8 * g:b + 40 + 8 * g],
                        ALU.mult, ALU.add))
                F(v.max(out=w[:, b + 76:b + 84], in_=w[:, b + 32:b + 64]))
                F(v.tensor_scalar(w[:, b + 32:b + 64], w[:, b + 32:b + 64],
                                  w[:, b + 83:b + 84], None, ALU.is_ge))
                F(v.scalar_tensor_tensor(w[:, b + 104:b + 136], w[:, b + 32:b + 64],
                                         1.0, w[:, b:b + 32], ALU.mult, ALU.mult,
                                         accum_out=w[:, b + 100:b + 101]))
                F(v.reciprocal(w[:, b + 101:b + 102], w[:, b + 100:b + 101]))
                return v.tensor_scalar(w[:, b + 32:b + 64], w[:, b + 104:b + 136],
                                       w[:, b + 101:b + 102], 2.5, ALU.mult,
                                       ALU.mult)

            st_rt[m] = sc.step("vector", routing, deps=[st_sig], tag=f"rt{m}")
            st_ct = sc.step("tensor", lambda t, b=b, sl=sl: t.transpose(
                pst_v(sl), rw[:, b + 32:b + 64], identf[:]),
                deps=[st_rt[m]], tag=f"ct{m}")
            st_cwev[m] = sc.step("vector", lambda v, m=m, sl=sl: v.tensor_copy(
                cwT[:, m * 128:(m + 1) * 128], pst_v(sl)),
                deps=[st_ct], tag=f"cwev{m}")
        st_hl = sc.step("vector", lambda v: (
            v.tensor_copy(cwbf[:, 0:T], cwT[:]),
            v.tensor_copy(cwlf[:], cwbf[:, 0:T]),
            v.tensor_sub(cwlf[:], cwT[:], cwlf[:]),
            v.tensor_copy(cwbf[:, T:2 * T], cwlf[:]))[-1],
            deps=[st_cwev[15]], tag="cwhl")
        sc.dma("sync", lambda e: e.dma_start(cwhi_s, cwbf[:, 0:T]),
               deps=[st_hl], stream=6, tag="cwd1")
        sc.dma("sync", lambda e: e.dma_start(cwlo_s, cwbf[:, T:2 * T]),
               deps=[st_hl], stream=7, tag="cwd2")
        if "cw_d" in dbg_t:
            sc.dma("sync", lambda e: e.dma_start(dbg_t["cw_d"], cwT[:]),
                   deps=[st_hl], stream=6, tag="dump_cw")

        st_f1l = [None] * 8
        st_f1m = [None] * 8
        st_f1e = [None] * 8
        for mj in range(8):
            pp = mj % 2
            st_f1l[mj] = sc.dma("gpsimd", lambda g, mj=mj, pp=pp: g.dma_start(
                wstr[:, pp * 2048:(pp + 1) * 2048],
                fc1T[:, mj * 128:(mj + 1) * 128].rearrange(
                    "(kt p) m -> p kt m", p=128)),
                deps=[st_f1m[mj - 2]] if mj >= 2 else [], stream=2 + pp,
                tag=f"f1l{mj}")
            st_f1m[mj] = sc.step("tensor", lambda t, mj=mj, pp=pp: [
                t.matmul(psA[:, n * 512:(n + 1) * 512],
                         wstr[:, pp * 2048 + k * 128:pp * 2048 + (k + 1) * 128],
                         xn_t(k)[:, n * 512:(n + 1) * 512],
                         start=(k == 0), stop=(k == 15))
                for k in range(16) for n in range(4)][-1],
                deps=[st_f1l[mj]] + ([st_f1e[mj - 1]] if mj >= 1 else []),
                tag=f"f1m{mj}")
            st_f1e[mj] = sc.step("scalar", lambda s, mj=mj: s.activation(
                x_lat[:, mj * T:(mj + 1) * T], psA[:], AF.Copy),
                deps=[st_f1m[mj]], tag=f"f1e{mj}")
            sc.dma("sync", lambda e, mj=mj: e.dma_start(
                xlat_s[mj * 128:(mj + 1) * 128, :].rearrange(
                    "(a p) q -> p a q", p=128).squeeze(1),
                x_lat[:, mj * T:(mj + 1) * T]),
                deps=[st_f1e[mj]], stream=8, tag=f"f1d{mj}")
        with nc.Block() as blk:
            sc.emit(blk)

        if "xlat_d" in dbg_t:
            sc = sched()
            sc.dma("sync", lambda e: e.dma_start(dbg_t["xlat_d"], x_lat[:]),
                   stream=0, tag="dump_xlat")
            with nc.Block() as blk:
                sc.emit(blk)

        sc = sched()
        st_sul = [None] * 32
        st_sum = [None] * 32
        st_sue = [None] * 32
        st_sud = [None] * 32
        for mj in range(32):
            pp = mj % 2
            ps = psA if pp == 0 else psB
            st_sul[mj] = sc.dma("gpsimd", lambda g, mj=mj, pp=pp: g.dma_start(
                wstr[:, pp * 2048:(pp + 1) * 2048],
                supT[:, mj * 128:(mj + 1) * 128].rearrange(
                    "(kt p) m -> p kt m", p=128)),
                deps=[st_sum[mj - 2]] if mj >= 2 else [], stream=2 + pp,
                tag=f"sul{mj}")
            st_sum[mj] = sc.step("tensor", lambda t, mj=mj, pp=pp, ps=ps: [
                t.matmul(ps[:, n * 512:(n + 1) * 512],
                         wstr[:, pp * 2048 + k * 128:pp * 2048 + (k + 1) * 128],
                         xn_t(k)[:, n * 512:(n + 1) * 512],
                         start=(k == 0), stop=(k == 15))
                for k in range(16) for n in range(4)][-1],
                deps=[st_sul[mj]] + ([st_sue[mj - 2]] if mj >= 2 else []),
                tag=f"sum{mj}")
            st_sue[mj] = sc.step("scalar", lambda s, mj=mj, pp=pp, ps=ps: (
                s.activation(tr32[:, pp * 2048:(pp + 1) * 2048], ps[:], AF.Relu),
                s.activation(usq[:, pp * 2048:(pp + 1) * 2048],
                             tr32[:, pp * 2048:(pp + 1) * 2048], AF.Square))[-1],
                deps=[st_sum[mj]] + ([st_sud[mj - 2]] if mj >= 2 else []),
                tag=f"sue{mj}")
            st_sud[mj] = sc.dma("sync", lambda e, mj=mj, pp=pp: e.dma_start(
                u_s[mj * 128:(mj + 1) * 128, :],
                usq[:, pp * 2048:(pp + 1) * 2048]),
                deps=[st_sue[mj]], stream=4 + pp, tag=f"sud{mj}")
        with nc.Block() as blk:
            sc.emit(blk)

    if phases <= 3:
        _finish_stub(nc, sched, out_d, arn, dsems)
        ctx.close()
        return nc

    with contextlib.ExitStack() as p4:
        xTc2 = p4.enter_context(nc.sbuf_tensor("xTc2", [128, 2 * 4096], BF16))
        BTc2 = p4.enter_context(nc.sbuf_tensor("BTc2", [128, 2 * 1024], BF16))
        Bfc2 = p4.enter_context(nc.sbuf_tensor("Bfc2", [128, 2 * 1024], BF16))
        Cc2 = p4.enter_context(nc.sbuf_tensor("Cc2", [128, 2 * 1024], BF16))
        gatec2 = p4.enter_context(nc.sbuf_tensor("gatec2", [128, 2 * 4096], F32))
        dtw = p4.enter_context(nc.sbuf_tensor("dtw", [128, 10 * CS], F32))
        hilo = p4.enter_context(nc.sbuf_tensor("hilo", [NH, 2 * CS], BF16))
        tmw = p4.enter_context(nc.sbuf_tensor("tmw", [128, 5 * 64], F32))
        cdrep = p4.enter_context(nc.sbuf_tensor("cdrep", [128, 64], F32))
        sct = x_lat[:, 12288:13312].bitcast(F32)
        EB = p4.enter_context(nc.sbuf_tensor("EB", [128, 2 * 1024], BF16))
        SB2 = p4.enter_context(nc.sbuf_tensor("SB2", [128, 2 * 1024], BF16))
        SMb = p4.enter_context(nc.sbuf_tensor("SMb", [128, 2 * 1024], BF16))
        CSD = p4.enter_context(nc.sbuf_tensor("CSD", [128, 2 * 1024], BF16))
        carry_f = x_lat[:, 0:8192].bitcast(F32)
        carry_bf = x_lat[:, 8192:12288]
        vrw = p4.enter_context(nc.sbuf_tensor("vrw", [128, 24], F32))
        tn512 = x_lat[:, 13312:14336].bitcast(F32)
        sqj = x_lat[:, 14336:15360].bitcast(F32)

        xdtT = arn(0, 4096, BF16)
        xddT = arn(8192, 4096, BF16)
        t0full = arn(16384, 4096, F32)
        Ych = arn(32768, 4096, F32)
        onstage = arn(16384, 4096, BF16)
        acsfl = arena[0:2, 49152:49152 + NH * CS * 2].bitcast(BF16)

        def dw(i):
            return dtw[0:NH, i * CS:(i + 1) * CS]

        def dwf(i):
            return dtw[:, i * CS:(i + 1) * CS]

        def tw(i):
            return tmw[:, i * 64:(i + 1) * 64]

        def ldsl(buf, par, w):
            return buf[:, par * w:(par + 1) * w]

        sc = sched()

        def mb(lvl, fn):
            return fn if _P4_LVL >= lvl else (lambda e: e.nop())

        st_init_cf = sc.step("vector", lambda v: v.memset(carry_f[:], 0.0),
                             tag="cf0")
        st_dtw0 = sc.step("vector", lambda v: v.memset(dtw[64:128, :], 0.0),
                          tag="dtw0")
        st_init_cb = sc.step("gpsimd", lambda g: g.memset(carry_bf[:], 0.0),
                             tag="cb0")

        st_ld = {}
        st_px = [None] * NCH
        st_ych = [None] * NCH
        st_gm = [None] * NCH
        st_onrm = [None] * NCH
        st_dmon = [None] * NCH
        st_trev = [None] * NCH
        st_cdexp = [None] * NCH
        st_fl3 = [None] * NCH
        st_sc_l = {}
        st_sct_l = {}
        st_esgb = {}
        st_sm = {}
        st_csd = {}
        st_y = {}
        st_stt = {}
        st_t0 = {}
        st_carry = {}
        st_silu = [None] * NCH

        nch_lim = _NCH_LIM if _NCH_LIM is not None else NCH
        for c in range(nch_lim):
            par = c % 2
            cs0, cs1 = c * CS, (c + 1) * CS
            dep_x = [st_ych[c - 2]] if c >= 2 else []
            dep_bt = [st_stt[(c - 2, 7)]] if c >= 2 else []
            dep_bf = [st_sc_l[(c - 2, 7)]] if c >= 2 else []
            dep_c = [st_y[(c - 2, 7)], st_csd[(c - 2, 7)]] if c >= 2 else []
            dep_g = [st_gm[c - 2]] if c >= 2 else []
            l_x = sc.dma("sync", lambda e, c=c, par=par: e.dma_start(
                ldsl(xTc2, par, 4096), xT_s[c * CS:(c + 1) * CS, :]),
                deps=dep_x, stream=par, tag=f"lx{c}")
            l_bt = sc.dma("sync", lambda e, c=c, par=par: e.dma_start(
                ldsl(BTc2, par, 1024), BT_s[c * CS:(c + 1) * CS, :]),
                deps=dep_bt, stream=par, tag=f"lbt{c}")
            l_bf = sc.dma("sync", lambda e, c=c, par=par: e.dma_start(
                ldsl(Bfc2, par, 1024),
                Bf_s[:, c * CS:(c + 1) * CS].rearrange("(gt p) s -> p gt s", p=128)),
                deps=dep_bf, stream=par, tag=f"lbf{c}")
            l_c = sc.dma("sync", lambda e, c=c, par=par: e.dma_start(
                ldsl(Cc2, par, 1024),
                C_s[:, c * CS:(c + 1) * CS].rearrange("(gt p) s -> p gt s", p=128)),
                deps=dep_c, stream=par, tag=f"lc{c}")
            l_g = sc.dma("sync", lambda e, c=c, par=par: e.dma_start(
                ldsl(gatec2, par, 4096), gate_s[c * CS:(c + 1) * CS, :]),
                deps=dep_g, stream=par, tag=f"lg{c}")

            def dt1(v, cs0=cs0, cs1=cs1):
                return v.tensor_scalar(dw(0), dt_raw[:, cs0:cs1], dtbc[:, 0:1],
                                       None, ALU.add)
            st_dt1 = sc.step("vector", mb(1, dt1), tag=f"dt1_{c}")

            def act1(s):
                F = mkfence("scalar", s)
                F(s.activation(dw(1), dw(0), AF.Abs))
                F(s.activation(dw(2), dw(1), AF.Exp, scale=-1.0))
                return s.activation(dw(3), dw(2), AF.Ln, bias=1.0)
            st_act1 = sc.step("scalar", mb(1, act1), deps=[st_dt1], tag=f"act1_{c}")

            def dt2(v):
                F = mkfence("vector", v)
                F(v.tensor_scalar(dw(4), dw(0), 0.0, None, ALU.max))
                F(v.tensor_add(dw(5), dw(4), dw(3)))
                F(v.tensor_scalar(dw(6), dw(5), expA[:, 0:1], -1.0, ALU.mult,
                                  ALU.mult))
                F(v.tensor_tensor_scan(dw(7), ones64[:], dw(6), 0.0, ALU.mult,
                                       ALU.add))
                F(v.tensor_copy(hilo[:, 0:CS], dw(7)))
                F(v.tensor_copy(dw(8), hilo[:, 0:CS]))
                F(v.tensor_sub(dw(8), dw(7), dw(8)))
                F(v.tensor_copy(hilo[:, CS:2 * CS], dw(8)))
                return v.tensor_scalar(dw(8), dw(7), dw(7)[:, CS - 1:CS], None,
                                       ALU.subtract)
            st_dt2 = sc.step("vector", mb(1, dt2), deps=[st_act1], tag=f"dt2_{c}")

            def act2(s):
                return s.activation(dw(9), dw(8), AF.Exp, scale=-1.0)
            st_act2 = sc.step("scalar", mb(1, act2), deps=[st_dt2], tag=f"act2_{c}")

            g1 = sc.dma("gpsimd", lambda g: g.dma_start(
                acs_d[0:1, :].rearrange("a (h l) -> a h l", h=NH).squeeze(0),
                hilo[:, 0:CS]), deps=[st_dt2], stream=2, tag=f"fh{c}")
            g2 = sc.dma("gpsimd", lambda g: g.dma_start(
                acs_d[1:2, :].rearrange("a (h l) -> a h l", h=NH).squeeze(0),
                hilo[:, CS:2 * CS]), deps=[st_dt2], stream=2, tag=f"fl{c}")
            dep_fl = [st_sc_l[(c - 1, 7)]] if c >= 1 else []
            st_fl3[c] = sc.dma("gpsimd", lambda g: g.dma_start(acsfl[:], acs_d),
                               deps=[g1, g2] + ([st_esgb[(c - 1, 7)],
                                                 st_cdexp[c - 1]] if c >= 1 else []),
                               stream=2, tag=f"fa{c}")

            dep_b4 = [st_sct_l[(c - 1, 7)], st_cdexp[c - 1]] if c >= 1 else []
            def tr1(t, c=c):
                t.transpose(psB[:, 0:128], dwf(7), identf[:])
                t.transpose(psB[:, 128:256], dwf(5), identf[:])
                t.transpose(psB[:, 256:384], dwf(9), identf[:])
                return t.matmul(psB[:, 384:448], ones2[:],
                                acsfl[0:2, CS - 1::CS], start=True, stop=True)
            st_tr1 = sc.step("tensor", mb(3, tr1),
                             deps=[st_dt2, st_act2, st_fl3[c]] + dep_b4,
                             tag=f"tr1_{c}")

            def trev(v):
                if _P4_TREV == 1:
                    v.tensor_copy(tw(0), psB[:, 0:64])
                    v.tensor_copy(tw(2), psB[:, 128:192])
                    v.tensor_copy(tw(3), psB[:, 256:320])
                    v.tensor_scalar(tw(1), tw(0), -1.0, None, ALU.mult)
                    return v.tensor_tensor(tw(4), tw(2), tw(3), ALU.mult)
                F = mkfence("vector", v)
                F(v.tensor_copy(tw(0), psB[:, 0:64]))
                F(v.tensor_scalar(tw(1), tw(0), -1.0, None, ALU.mult))
                F(v.tensor_copy(tw(2), psB[:, 128:192]))
                F(v.tensor_copy(tw(3), psB[:, 256:320]))
                return v.tensor_tensor(tw(4), tw(2), tw(3), ALU.mult)
            st_trev[c] = sc.step("vector", trev, deps=[st_tr1], tag=f"trev{c}")

            st_cdexp[c] = sc.step("scalar", lambda s: s.activation(
                cdrep[:], psB[:, 384:448], AF.Exp),
                deps=[st_tr1, st_trev[c]], tag=f"cde{c}")

            def px(g, par=par):
                xr = ldsl(xTc2, par, 4096).rearrange("p (h q) -> p h q", q=64)
                g.tensor_tensor(xdtT.rearrange("p (h q) -> p h q", q=64), xr,
                                tw(2).unsqueeze(2).broadcast_to([128, 64, 64]),
                                ALU.mult)
                return g.tensor_tensor(
                    xddT.rearrange("p (h q) -> p h q", q=64), xr,
                    tw(4).unsqueeze(2).broadcast_to([128, 64, 64]), ALU.mult)
            st_px[c] = sc.step("gpsimd", mb(4, px),
                               deps=[st_trev[c], l_x] +
                               ([st_y[(c - 1, 7)], st_stt[(c - 1, 7)]]
                                if c >= 1 else []),
                               tag=f"px{c}")

            for g in range(8):
                G = c * 8 + g
                parG = G % 2
                h0 = 8 * g

                def pe_seg(t, parG=parG, h0=h0):
                    for hp in range(8):
                        i = t.matmul(
                            psA[:, parG * 1024 + hp * 128:parG * 1024 + (hp + 1) * 128],
                            ones2[:],
                            acsfl[0:2, (h0 + hp) * CS:(h0 + hp + 1) * CS],
                            start=True, stop=True)
                    return i
                dep_seg = [st_fl3[c]] + ([st_esgb[divmod(G - 2, 8)]]
                                         if G >= 2 else [])
                st_seg = sc.step("tensor", mb(5, pe_seg), deps=dep_seg, tag=f"seg{G}")

                dep_sc = [l_bf, l_c] + ([st_sct_l[divmod(G - 1, 8)]]
                                        if G >= 1 else [])
                if g == 0:
                    dep_sc += [st_trev[c], st_cdexp[c]]
                st_sc_l[(c, g)] = sc.step("tensor", lambda t, par=par, g=g: t.matmul(
                    psB[:, 0:128], ldsl(Bfc2, par, 1024)[:, g * 128:(g + 1) * 128],
                    ldsl(Cc2, par, 1024)[:, g * 128:(g + 1) * 128],
                    start=True, stop=True),
                    deps=dep_sc, tag=f"sc{G}")

                def v_sct(v, parG=parG):
                    F = mkfence("vector", v)
                    F(v.tensor_tensor(
                        sct[:, parG * 128:(parG + 1) * 128], psB[:, 0:128],
                        maskU[:], ALU.mult))
                    return v.engine_nop()
                st_sct_l[(c, g)] = sc.step("vector", mb(5, v_sct),
                                           deps=[st_sc_l[(c, g)]], tag=f"sctm{G}")

                def a_esgb(s, parG=parG, h0=h0):
                    for hp in range(8):
                        pseg = psA[:, parG * 1024 + hp * 128:parG * 1024 + (hp + 1) * 128]
                        s.activation(EB[:, parG * 1024 + hp * 128:parG * 1024 + (hp + 1) * 128],
                                     pseg, AF.Exp, bias=tw(1)[:, h0 + hp - h0 + 0:0] if False
                                     else tw(1)[:, hp + h0 - h0:hp + 1] if False
                                     else tmw[:, 64 + h0 + hp:64 + h0 + hp + 1])
                        i = s.activation(SB2[:, parG * 1024 + hp * 128:parG * 1024 + (hp + 1) * 128],
                                         pseg, AF.Exp)
                    return i
                st_esgb[(c, g)] = sc.step(
                    "scalar", mb(5, a_esgb),
                    deps=[st_seg, st_trev[c]] +
                    ([st_sm[divmod(G - 2, 8)], st_csd[divmod(G - 2, 8)]]
                     if G >= 2 else []),
                    tag=f"esgb{G}")

                def v_sm(v, parG=parG):
                    for hp in range(8):
                        i = v.scalar_tensor_tensor(
                            SMb[:, parG * 1024 + hp * 128:parG * 1024 + (hp + 1) * 128],
                            EB[:, parG * 1024 + hp * 128:parG * 1024 + (hp + 1) * 128],
                            1.0, sct[:, parG * 128:(parG + 1) * 128].unsqueeze(1)
                            .broadcast_to([128, 1, 128]).squeeze(1) if False
                            else sct[:, parG * 128:(parG + 1) * 128],
                            ALU.min, ALU.mult)
                    return i
                st_sm[(c, g)] = sc.step("vector", mb(6, v_sm),
                                        deps=[st_esgb[(c, g)], st_sct_l[(c, g)]],
                                        tag=f"sm{G}")

                def g_csd(g_, par=par, parG=parG, g=g):
                    for hp in range(8):
                        i = g_.tensor_tensor(
                            CSD[:, parG * 1024 + hp * 128:parG * 1024 + (hp + 1) * 128],
                            ldsl(Cc2, par, 1024)[:, g * 128:(g + 1) * 128],
                            SB2[:, parG * 1024 + hp * 128:parG * 1024 + (hp + 1) * 128],
                            ALU.mult)
                    return i
                st_csd[(c, g)] = sc.step("gpsimd", mb(6, g_csd),
                                         deps=[st_esgb[(c, g)], l_c],
                                         tag=f"csd{G}")

                def pe_y(t, parG=parG, h0=h0):
                    for hp in range(8):
                        ps = psB[:, 512 + parG * 1024 + hp * 64:
                                 512 + parG * 1024 + (hp + 1) * 64]
                        t.matmul(ps,
                                 SMb[:, parG * 1024 + hp * 128:parG * 1024 + (hp + 1) * 128],
                                 xdtT[:, (h0 + hp) * 64:(h0 + hp + 1) * 64],
                                 start=True, stop=False)
                        i = t.matmul(ps,
                                     CSD[:, parG * 1024 + hp * 128:parG * 1024 + (hp + 1) * 128],
                                     carry_bf[:, (h0 + hp) * 64:(h0 + hp + 1) * 64],
                                     start=False, stop=True)
                    return i
                dep_y = [st_sm[(c, g)], st_csd[(c, g)], st_px[c]]
                dep_y += [st_init_cb] if c == 0 else [st_cb]
                dep_y += [st_t0[divmod(G - 2, 8)]] if G >= 2 else []
                st_y[(c, g)] = sc.step("tensor", mb(7, pe_y), deps=dep_y, tag=f"y{G}")

                def pe_st(t, par=par, g=g, h0=h0):
                    for hp in range(8):
                        i = t.matmul(
                            psB[:, 1024 + hp * 64:1024 + (hp + 1) * 64],
                            ldsl(BTc2, par, 1024)[:, g * 128:(g + 1) * 128],
                            xddT[:, (h0 + hp) * 64:(h0 + hp + 1) * 64],
                            start=True, stop=True)
                    return i
                dep_st = [st_px[c], l_bt] + ([st_carry[divmod(G - 1, 8)]]
                                             if G >= 1 else [])
                st_stt[(c, g)] = sc.step("tensor", mb(7, pe_st), deps=dep_st,
                                         tag=f"st{G}")

                st_t0[(c, g)] = sc.step("scalar", mb(8, lambda s, parG=parG, g=g:
                                        s.activation(
                    t0full[:, g * 512:(g + 1) * 512],
                    psB[:, 512 + parG * 1024:512 + parG * 1024 + 512], AF.Copy)),
                    deps=[st_y[(c, g)]] +
                    ([st_ych[c - 1]] if c >= 1 and g == 0 else []) +
                    ([st_dmon[c - 1]] if c >= 1 and g <= 3 else []),
                    tag=f"t0_{G}")

                def v_carry(v, h0=h0):
                    for hp in range(8):
                        i = v.scalar_tensor_tensor(
                            carry_f[:, (h0 + hp) * 64:(h0 + hp + 1) * 64],
                            carry_f[:, (h0 + hp) * 64:(h0 + hp + 1) * 64],
                            cdrep[:, h0 + hp:h0 + hp + 1],
                            psB[:, 1024 + hp * 64:1024 + (hp + 1) * 64],
                            ALU.mult, ALU.add)
                    return i
                st_carry[(c, g)] = sc.step(
                    "vector", mb(8, v_carry),
                    deps=[st_stt[(c, g)], st_cdexp[c]] +
                    ([st_init_cf] if c == 0 and g == 0 else []),
                    tag=f"carry{G}")

            st_cb = sc.step("gpsimd", mb(8, lambda g: g.tensor_copy(
                carry_bf[:], carry_f[:])),
                deps=[st_carry[(c, 7)]], tag=f"cb{c}")
            st_silu[c] = sc.step("scalar", mb(9, lambda s, par=par: s.activation(
                ldsl(gatec2, par, 4096), ldsl(gatec2, par, 4096), AF.Silu)),
                deps=[l_g], tag=f"silu{c}")

            def g_ych(g, par=par):
                xr = ldsl(xTc2, par, 4096).rearrange("p (h q) -> p h q", q=64)
                g.tensor_tensor(Ych.rearrange("p (h q) -> p h q", q=64), xr,
                                d_rep[:].unsqueeze(2).broadcast_to([128, 64, 64]),
                                ALU.mult)
                return g.tensor_add(Ych[:], Ych[:], t0full[:])
            st_ych[c] = sc.step("gpsimd", mb(9, g_ych),
                                deps=[st_t0[(c, 7)], l_x] +
                                ([st_onrm[c - 1]] if c >= 1 else []),
                                tag=f"ych{c}")
            st_gm[c] = sc.step("gpsimd", mb(9, lambda g, par=par: g.tensor_tensor(
                Ych[:], Ych[:], ldsl(gatec2, par, 4096), ALU.mult)),
                deps=[st_ych[c], st_silu[c]], tag=f"gm{c}")

            def v_var(v):
                for ng in range(8):
                    i = v.scalar_tensor_tensor(
                        sqj[:], Ych[:, ng * 512:(ng + 1) * 512], 1.0,
                        Ych[:, ng * 512:(ng + 1) * 512], ALU.mult, ALU.mult,
                        accum_out=vrw[:, ng:ng + 1])
                return i
            st_var = sc.step("vector", mb(9, v_var), deps=[st_gm[c]], tag=f"var{c}")

            def a_rstd(s):
                F = mkfence("scalar", s)
                F(s.activation(vrw[:, 8:16], vrw[:, 0:8], AF.Ln,
                               bias=epsc[:, 0:1], scale=1.0 / 512))
                return s.activation(vrw[:, 8:16], vrw[:, 8:16], AF.Exp,
                                    scale=-0.5)
            st_rstd = sc.step("scalar", mb(9, a_rstd), deps=[st_var], tag=f"rstd{c}")

            def g_onrm(g):
                for ng in range(8):
                    g.tensor_tensor(tn512[:], Ych[:, ng * 512:(ng + 1) * 512],
                                    vrw[:, 8 + ng:9 + ng].broadcast_to([128, 512]),
                                    ALU.mult)
                    i = g.tensor_tensor(onstage[:, ng * 512:(ng + 1) * 512],
                                        tn512[:], nw_bc[:, ng * 512:(ng + 1) * 512],
                                        ALU.mult)
                return i
            st_onrm[c] = sc.step("gpsimd", mb(9, g_onrm),
                                 deps=[st_rstd, st_gm[c]] +
                                 ([st_dmon[c - 1]] if c >= 1 else []),
                                 tag=f"onrm{c}")
            st_dmon[c] = sc.dma("sync", lambda e, c=c: e.dma_start(
                onorm_s[c * CS:(c + 1) * CS, :], onstage[:]),
                deps=[st_onrm[c]], stream=4 + par, tag=f"dmon{c}")
            if "ych_d" in dbg_t:
                sc.dma("sync", lambda e, c=c: e.dma_start(
                    dbg_t["ych_d"][c * CS:(c + 1) * CS, :], Ych[:]),
                    deps=[st_gm[c], st_var], stream=6, tag=f"dych{c}")

        with nc.Block() as blk:
            sc.emit(blk)

    if phases <= 4:
        _finish_stub(nc, sched, out_d, arn, dsems)
        ctx.close()
        return nc

    with contextlib.ExitStack() as p6:
        w1b = p6.enter_context(nc.sbuf_tensor("w1b", [128, 2 * 8192], BF16))
        w2b = p6.enter_context(nc.sbuf_tensor("w2b", [128, 8192], BF16))
        hbuf = p6.enter_context(nc.sbuf_tensor("hbuf", [128, 8 * 2048], BF16))
        cwrow = p6.enter_context(nc.sbuf_tensor("cwrow", [2, 2048], BF16))
        cwb_bf = p6.enter_context(nc.sbuf_tensor("cwb_bf", [128, 2048], BF16))
        trl = p6.enter_context(nc.sbuf_tensor("trl", [128, 1024], F32))

        def ylat(l):
            return arn(l * T * 4, T, F32)

        sc = sched()
        st_xlr = sc.dma("sync", lambda e: e.dma_start(
            x_lat[:].rearrange("p (t q) -> p t q", q=T),
            xlat_s.rearrange("(t p) q -> p t q", p=128)), stream=0, tag="xlr")
        st_y0 = sc.step("vector", lambda v: v.memset(arn(0, 8 * T, F32), 0.0),
                        tag="y0")

        st_w1 = [None] * NE
        st_w2 = [None] * NE
        st_cwr = [None] * NE
        st_cwb = [None] * NE
        st_cwbe = [None] * NE
        st_hm = {}
        st_sq = {}
        st_hx = {}
        st_ym = {}
        st_ya = {}
        for e_ in range(NE):
            par = e_ % 2
            st_w1[e_] = sc.dma("gpsimd", lambda g, e_=e_, par=par: g.dma_start(
                w1b[:, par * 8192:(par + 1) * 8192],
                w1T[e_].rearrange("(kt p) i -> p kt i", p=128)),
                deps=[st_hm[(e_ - 2, 15)]] if e_ >= 2 else [],
                stream=2 + par, tag=f"w1_{e_}")
            st_w2[e_] = sc.dma("gpsimd", lambda g, e_=e_: g.dma_start(
                w2b[:], w2T[e_].rearrange("(kt p) i -> p kt i", p=128)),
                deps=[st_ym[(e_ - 1, 15)]] if e_ >= 1 else [],
                stream=4, tag=f"w2_{e_}")
            st_cwra = sc.dma("sync", lambda e, e_=e_: e.dma_start(
                cwrow[0:1, :], cwhi_s[e_:e_ + 1, :]),
                deps=[st_cwb[e_ - 1]] if e_ >= 1 else [], stream=6,
                tag=f"cwra{e_}")
            st_cwr[e_] = sc.dma("sync", lambda e, e_=e_: e.dma_start(
                cwrow[1:2, :], cwlo_s[e_:e_ + 1, :]),
                deps=[st_cwra], stream=6, tag=f"cwr{e_}")
            dep_psA = [st_sq[(e_ - 1, 15)]] if e_ >= 1 else []
            st_cwb[e_] = sc.step("tensor", lambda t, e_=e_: [
                t.matmul(psA[:, n * 512:(n + 1) * 512], ones2[:],
                         cwrow[0:2, n * 512:(n + 1) * 512], start=True, stop=True)
                for n in range(4)][-1],
                deps=[st_cwr[e_]] + dep_psA, tag=f"cwb{e_}")
            st_cwbe[e_] = sc.step("scalar", lambda s: s.activation(
                cwb_bf[:], psA[:], AF.Copy),
                deps=[st_cwb[e_]] + ([st_hx[(e_ - 1, 15)]] if e_ >= 1 else []),
                tag=f"cwbe{e_}")
            for i in range(8):
                for nh in range(2):
                    gi2 = 2 * i + nh
                    sl = gi2 % 2

                    def hmm(t, e_=e_, par=par, i=i, nh=nh, sl=sl):
                        for k in range(8):
                            for nn in range(2):
                                col = nh * 1024 + nn * 512
                                inst = t.matmul(
                                    psA[:, sl * 1024 + nn * 512:sl * 1024 + nn * 512 + 512],
                                    w1b[:, par * 8192 + k * 1024 + i * 128:
                                        par * 8192 + k * 1024 + (i + 1) * 128],
                                    x_lat[:, k * T + col:k * T + col + 512],
                                    start=(k == 0), stop=(k == 7))
                        return inst
                    dep_h = [st_w1[e_], st_xlr, st_cwbe[e_]]
                    if gi2 >= 2:
                        dep_h.append(st_sq[(e_, gi2 - 2)])
                    elif e_ >= 1:
                        dep_h.append(st_sq[(e_ - 1, 15 - (1 - gi2))])
                    st_hm[(e_, gi2)] = sc.step("tensor", hmm, deps=dep_h,
                                               tag=f"hm{e_}_{gi2}")

                    def sqf(s, i=i, nh=nh, sl=sl):
                        s.activation(trl[:], psA[:, sl * 1024:(sl + 1) * 1024],
                                     AF.Relu)
                        return s.activation(
                            hbuf[:, i * 2048 + nh * 1024:i * 2048 + (nh + 1) * 1024],
                            trl[:], AF.Square)
                    st_sq[(e_, gi2)] = sc.step("scalar", sqf,
                                               deps=[st_hm[(e_, gi2)]],
                                               tag=f"sq{e_}_{gi2}")
                    st_hx[(e_, gi2)] = sc.step(
                        "vector", lambda v, i=i, nh=nh: v.tensor_tensor(
                            hbuf[:, i * 2048 + nh * 1024:i * 2048 + (nh + 1) * 1024],
                            hbuf[:, i * 2048 + nh * 1024:i * 2048 + (nh + 1) * 1024],
                            cwb_bf[:, nh * 1024:(nh + 1) * 1024], ALU.mult),
                        deps=[st_sq[(e_, gi2)], st_cwbe[e_]], tag=f"hx{e_}_{gi2}")
            for l in range(8):
                for nh in range(2):
                    gi2 = 2 * l + nh
                    sl = gi2 % 2

                    def ymm(t, l=l, nh=nh, sl=sl):
                        for k in range(8):
                            for nn in range(2):
                                col = nh * 1024 + nn * 512
                                inst = t.matmul(
                                    psB[:, sl * 1024 + nn * 512:sl * 1024 + nn * 512 + 512],
                                    w2b[:, k * 1024 + l * 128:k * 1024 + (l + 1) * 128],
                                    hbuf[:, k * 2048 + col:k * 2048 + col + 512],
                                    start=(k == 0), stop=(k == 7))
                        return inst
                    dep_y = [st_w2[e_], st_hx[(e_, 15)]]
                    if gi2 >= 2:
                        dep_y.append(st_ya[(e_, gi2 - 2)])
                    elif e_ >= 1:
                        dep_y.append(st_ya[(e_ - 1, 15 - (1 - gi2))])
                    st_ym[(e_, gi2)] = sc.step("tensor", ymm, deps=dep_y,
                                               tag=f"ym{e_}_{gi2}")
                    st_ya[(e_, gi2)] = sc.step(
                        "vector", lambda v, l=l, nh=nh, sl=sl:
                        v.scalar_tensor_tensor(
                            ylat(l)[:, nh * 1024:(nh + 1) * 1024],
                            psA[:, 0:0] if False else
                            psB[:, sl * 1024:(sl + 1) * 1024], 1.0,
                            ylat(l)[:, nh * 1024:(nh + 1) * 1024],
                            ALU.mult, ALU.add),
                        deps=[st_ym[(e_, gi2)], st_y0], tag=f"ya{e_}_{gi2}")
        with nc.Block() as blk:
            sc.emit(blk)

    if "ylat_d" in dbg_t:
        sc = sched()
        sc.dma("sync", lambda e: e.dma_start(dbg_t["ylat_d"], arn(0, 8 * T, F32)),
               stream=0, tag="dump_ylat")
        with nc.Block() as blk:
            sc.emit(blk)

    if phases <= 6:
        _finish_stub(nc, sched, out_d, arn, dsems)
        ctx.close()
        return nc

    with contextlib.ExitStack() as p7:
        um = p7.enter_context(nc.sbuf_tensor("um", [128, 4 * 4096], BF16))
        onm = p7.enter_context(nc.sbuf_tensor("onm", [128, 4 * 4096], BF16))
        fc2sl = p7.enter_context(nc.sbuf_tensor("fc2sl", [128, 2 * 4096], BF16))
        xre = p7.enter_context(nc.sbuf_tensor("xre", [128, 2 * 512], F32))
        outst = p7.enter_context(nc.sbuf_tensor("outst", [128, 2 * 512], F32))
        tsil = p7.enter_context(nc.sbuf_tensor("tsil", [128, 2 * 512], F32))

        sc = sched()
        def ycast(v):
            for l in range(8):
                inst = v.tensor_copy(x_lat[:, l * T:(l + 1) * T], arn(l * T * 4, T, F32))
            return inst
        st_yc = sc.step("vector", ycast, tag="ycast")
        ylb = x_lat

        sdnA = arn(0, 8192, BF16)
        sdnB = arn(16384, 8192, BF16)
        woutA = arn(32768, 8192, BF16)
        woutB = arn(49152, 8192, BF16)

        st_tr = {}
        st_trev7 = {}
        st_g = {}
        st_gA = {}
        st_sA = {}
        st_s7 = {}
        st_o2 = {}
        st_od = {}
        st_wl7 = {}
        st_uml = {}
        st_onl = {}
        last_wl = {}
        for mb in range(4):
            for mi in range(4):
                m = mb * 4 + mi
                parm = m % 2
                dep_u = [st_g[(mb - 1, 3, mb * 4 - 2 + mi - 2)]] if False else []
                st_uml[m] = sc.dma("sync", lambda e, m=m, mi=mi: e.dma_start(
                    um[:, mi * 4096:(mi + 1) * 4096],
                    u_s[:, m * 128:(m + 1) * 128].rearrange(
                        "(kt p) q -> p kt q", p=128)),
                    deps=[st_g[(mb - 2, 3, (mb - 2) * 4 + mi)]] if mb >= 2 else [],
                    stream=0, tag=f"um{m}")
                st_onl[m] = sc.dma("sync", lambda e, m=m, mi=mi: e.dma_start(
                    onm[:, mi * 4096:(mi + 1) * 4096],
                    onorm_s[m * 128:(m + 1) * 128, :]),
                    deps=[st_s7[(mb - 2, 3, (mb - 2) * 4 + mi)]] if mb >= 2 else [],
                    stream=1, tag=f"on{m}")
                def tr7(t, mi=mi):
                    for j in range(32):
                        inst = t.transpose(
                            psA[:, j * 64:(j + 1) * 64].bitcast(BF16),
                            onm[:, mi * 4096 + j * 128:mi * 4096 + (j + 1) * 128],
                            identb[:])
                    return inst
                dep_tr = [st_onl[m]]
                if mb >= 1:
                    dep_tr.append(st_trev7[(mb - 1) * 4 + 3])
                if m >= 1:
                    dep_tr.append(st_trev7[m - 1])
                st_tr[m] = sc.step("tensor", tr7, deps=dep_tr, tag=f"tr7_{m}")
                st_trev7[m] = sc.step("scalar", lambda s, mi=mi: s.activation(
                    onm[:, mi * 4096:(mi + 1) * 4096],
                    psA[:, 0:2048].bitcast(BF16), AF.Copy),
                    deps=[st_tr[m]], tag=f"trev7_{m}")
            for nq in range(4):
                parn = nq % 2
                if nq >= 1:
                    pmb, pnq = mb, nq - 1
                elif mb >= 1:
                    pmb, pnq = mb - 1, 3
                else:
                    pmb = None
                if pmb is not None:
                    pm3 = pmb * 4 + 3
                    wdep_f = [st_gA[(pmb, pnq, pm3)]]
                    wdep_gA = [st_gA[(pmb, pnq, pm3)]]
                    wdep_gB = [st_g[(pmb, pnq, pm3)]]
                    wdep_sA = [st_sA[(pmb, pnq, pm3)]]
                    wdep_sB = [st_s7[(pmb, pnq, pm3)]]
                else:
                    wdep_f = [st_yc]
                    wdep_gA = wdep_gB = wdep_sA = wdep_sB = []
                wl_f = sc.dma("gpsimd", lambda g, nq=nq, parn=parn: g.dma_start(
                    fc2sl[:, parn * 4096:(parn + 1) * 4096],
                    fc2T[:, nq * 512:(nq + 1) * 512].rearrange(
                        "(kt p) q -> p kt q", p=128)),
                    deps=wdep_f, stream=2, tag=f"wl7f_{mb}_{nq}")
                wl_sA = sc.dma("gpsimd", lambda g, nq=nq: g.dma_start(
                    sdnA, sdnT[0:2048, nq * 512:(nq + 1) * 512].rearrange(
                        "(kt p) q -> p kt q", p=128)),
                    deps=[wl_f] + wdep_gA, stream=2, tag=f"wl7sA_{mb}_{nq}")
                wl_wA = sc.dma("gpsimd", lambda g, nq=nq: g.dma_start(
                    woutA, woutT[0:2048, nq * 512:(nq + 1) * 512].rearrange(
                        "(kt p) q -> p kt q", p=128)),
                    deps=[wl_sA] + wdep_sA, stream=2, tag=f"wl7wA_{mb}_{nq}")
                wl_sB = sc.dma("gpsimd", lambda g, nq=nq: g.dma_start(
                    sdnB, sdnT[2048:4096, nq * 512:(nq + 1) * 512].rearrange(
                        "(kt p) q -> p kt q", p=128)),
                    deps=[wl_wA] + wdep_gB, stream=2, tag=f"wl7sB_{mb}_{nq}")
                wl_wB = sc.dma("gpsimd", lambda g, nq=nq: g.dma_start(
                    woutB, woutT[2048:4096, nq * 512:(nq + 1) * 512].rearrange(
                        "(kt p) q -> p kt q", p=128)),
                    deps=[wl_sB] + wdep_sB, stream=2, tag=f"wl7wB_{mb}_{nq}")
                for mi in range(4):
                    m = mb * 4 + mi
                    parm = m % 2

                    def gmmA(t, m=m, mi=mi, parm=parm, parn=parn):
                        for k in range(8):
                            t.matmul(psB[:, parm * 512:(parm + 1) * 512],
                                     ylb[:, k * T + m * 128:k * T + (m + 1) * 128],
                                     fc2sl[:, parn * 4096 + k * 512:
                                           parn * 4096 + (k + 1) * 512],
                                     start=(k == 0), stop=False)
                        for k in range(16):
                            inst = t.matmul(
                                psB[:, parm * 512:(parm + 1) * 512],
                                um[:, mi * 4096 + k * 128:mi * 4096 + (k + 1) * 128],
                                sdnA[:, k * 512:(k + 1) * 512],
                                start=False, stop=False)
                        return inst

                    def gmmB(t, mi=mi, parm=parm):
                        for k in range(16):
                            inst = t.matmul(
                                psB[:, parm * 512:(parm + 1) * 512],
                                um[:, mi * 4096 + (16 + k) * 128:
                                    mi * 4096 + (17 + k) * 128],
                                sdnB[:, k * 512:(k + 1) * 512],
                                start=False, stop=(k == 15))
                        return inst

                    def smmA(t, mi=mi, parm=parm):
                        for k in range(16):
                            inst = t.matmul(
                                psB[:, 1024 + parm * 512:1024 + (parm + 1) * 512],
                                onm[:, mi * 4096 + k * 128:mi * 4096 + (k + 1) * 128],
                                woutA[:, k * 512:(k + 1) * 512],
                                start=(k == 0), stop=False)
                        return inst

                    def smmB(t, mi=mi, parm=parm):
                        for k in range(16):
                            inst = t.matmul(
                                psB[:, 1024 + parm * 512:1024 + (parm + 1) * 512],
                                onm[:, mi * 4096 + (16 + k) * 128:
                                    mi * 4096 + (17 + k) * 128],
                                woutB[:, k * 512:(k + 1) * 512],
                                start=False, stop=(k == 15))
                        return inst
                    depg = [wl_f, wl_sA, st_yc, st_uml[m], st_trev7[m]]
                    if (mb, nq, m) != (0, 0, 0):
                        prev = _prev_mnq(mb, nq, mi)
                        if prev is not None and prev in st_o2:
                            depg.append(st_o2[prev])
                    st_gA[(mb, nq, m)] = sc.step("tensor", gmmA, deps=depg,
                                                 tag=f"gA{mb}_{nq}_{m}")
                    st_g[(mb, nq, m)] = sc.step("tensor", gmmB,
                                                deps=[st_gA[(mb, nq, m)], wl_sB],
                                                tag=f"g{mb}_{nq}_{m}")
                    st_sA[(mb, nq, m)] = sc.step("tensor", smmA,
                                                 deps=[st_g[(mb, nq, m)], wl_wA],
                                                 tag=f"sA{mb}_{nq}_{m}")
                    st_s7[(mb, nq, m)] = sc.step("tensor", smmB,
                                                 deps=[st_sA[(mb, nq, m)], wl_wB],
                                                 tag=f"s{mb}_{nq}_{m}")
                    st_xl = sc.dma("sync", lambda e, m=m, nq=nq, parm=parm:
                                   e.dma_start(
                                       xre[:, parm * 512:(parm + 1) * 512],
                                       x_in[m * 128:(m + 1) * 128,
                                            nq * 512:(nq + 1) * 512]),
                                   deps=[st_o2[_prev_mnq(mb, nq, mi)]]
                                   if _prev_mnq(mb, nq, mi) in st_o2 else [],
                                   stream=4 + parm, tag=f"xl{m}_{nq}")
                    st_sil = sc.step("scalar", lambda s, parm=parm: s.activation(
                        tsil[:, parm * 512:(parm + 1) * 512],
                        psB[:, parm * 512:(parm + 1) * 512], AF.Silu),
                        deps=[st_g[(mb, nq, m)]], tag=f"sil{mb}_{nq}_{m}")

                    def o2f(v, parm=parm):
                        v.scalar_tensor_tensor(
                            outst[:, parm * 512:(parm + 1) * 512],
                            tsil[:, parm * 512:(parm + 1) * 512], 1.0,
                            psB[:, 1024 + parm * 512:1024 + (parm + 1) * 512],
                            ALU.mult, ALU.mult)
                        return v.scalar_tensor_tensor(
                            outst[:, parm * 512:(parm + 1) * 512],
                            outst[:, parm * 512:(parm + 1) * 512], 1.0,
                            xre[:, parm * 512:(parm + 1) * 512],
                            ALU.mult, ALU.add)
                    st_o2[(mb, nq, m)] = sc.step(
                        "vector", o2f,
                        deps=[st_sil, st_s7[(mb, nq, m)], st_xl],
                        tag=f"o2_{mb}_{nq}_{m}")
                    st_od[(mb, nq, m)] = sc.dma(
                        "sync", lambda e, m=m, nq=nq, parm=parm: e.dma_start(
                            out_d[m * 128:(m + 1) * 128, nq * 512:(nq + 1) * 512],
                            outst[:, parm * 512:(parm + 1) * 512]),
                        deps=[st_o2[(mb, nq, m)]], stream=6 + parm,
                        tag=f"od{mb}_{nq}_{m}")
        with nc.Block() as blk:
            sc.emit(blk)

    _drain_all(nc, dsems)
    ctx.close()
    return nc


def _prev_mnq(mb, nq, mi):
    m = mb * 4 + mi
    order = []
    for mb2 in range(4):
        for nq2 in range(4):
            for mi2 in range(4):
                order.append((mb2, nq2, mb2 * 4 + mi2))
    idx = order.index((mb, nq, m))
    parm = m % 2
    for j in range(idx - 1, -1, -1):
        if order[j][2] % 2 == parm:
            return order[j]
    return None


def _finish_stub(nc, sched, out_d, arn, dsems):
    sc = sched()
    sc.dma("sync", lambda e: e.dma_start(out_d[0:128, :], arn(0, 2048, F32)),
           stream=0, tag="stub_out")
    with nc.Block() as blk:
        sc.emit(blk)
    _drain_all(nc, dsems)


def _drain_all(nc, dsems):
    with nc.Block() as blk:
        @blk.sync
        def _(e):
            for ds in dsems:
                if ds.n:
                    e.wait_ge(ds.h, ds.n)


def _host_pack(inputs):
    f32 = np.float32
    g = lambda k: np.asarray(inputs[k], f32)
    conv_w = g("L0_attn_conv_w")
    return {
        "winT": np.ascontiguousarray(g("L0_attn_W_in").T),
        "woutT": np.ascontiguousarray(g("L0_attn_W_out").T),
        "gw_pk": np.ascontiguousarray(
            g("L0_ffn_gate_w").T.reshape(16, 128, 32).transpose(1, 0, 2).reshape(128, 512)),
        "fc1T": np.ascontiguousarray(g("L0_ffn_fc1_w").T),
        "fc2T": np.ascontiguousarray(g("L0_ffn_fc2_w").T),
        "w1T": np.ascontiguousarray(g("L0_ffn_w1").transpose(0, 2, 1)),
        "w2T": np.ascontiguousarray(g("L0_ffn_w2").transpose(0, 2, 1)),
        "supT": np.ascontiguousarray(g("L0_ffn_shared_up_w").T),
        "sdnT": np.ascontiguousarray(g("L0_ffn_shared_down_w").T),
        "convw_p": np.ascontiguousarray(
            conv_w.reshape(48, 128, 4).transpose(1, 0, 2).reshape(128, 192)),
        "convb_p": np.ascontiguousarray(g("L0_attn_conv_b").reshape(48, 128).T),
        "dtb": np.ascontiguousarray(g("L0_attn_dt_bias")[:, None]),
        "alog": np.ascontiguousarray(g("L0_attn_A_log")[:, None]),
        "d_row": np.ascontiguousarray(g("L0_attn_D")[None, :]),
        "nw_row": np.ascontiguousarray(g("L0_attn_norm_weight")[None, :]),
        "gb_row": np.ascontiguousarray(g("L0_ffn_gate_bias")[None, :]),
        "identb_h": np.eye(128, dtype=bfdt),
        "identf_h": np.eye(128, dtype=f32),
        "maskU_h": np.triu(np.ones((128, 128), f32)),
    }


def kernel(**inputs):
    shared = _host_pack(inputs)
    x = np.asarray(inputs["L0_attn_hidden_states"], np.float32)
    nc = _build()
    in_maps = []
    for c in range(8):
        m = dict(shared)
        m["x"] = np.ascontiguousarray(x[c])
        in_maps.append(m)
    res = run_bass_kernel_spmd(nc, in_maps, core_ids=list(range(8)))
    return np.stack([np.asarray(res.results[c]["out"]) for c in range(8)], axis=0)
